# version 05
# baseline (speedup 1.0000x reference)
import numpy as np
import ml_dtypes
import concourse.bass as bass
import concourse.mybir as mybir
from concourse.bass_utils import run_bass_kernel_spmd

F32 = mybir.dt.float32
BF16 = mybir.dt.bfloat16
I32 = mybir.dt.int32
AF = mybir.ActivationFunctionType
ALU = mybir.AluOpType
AX = mybir.AxisListType
ET = mybir.EngineType

B, N, D = 4, 4096, 1024
HALF = 2048
NCH = 8
MT = 512
NMT = 4
DH = 64
BLOCK_M = 64
SEG = 256
DIL = 4
EPS_LN = 1e-5
WINDOW = 512
HID = 2730
QK_SCALE = 8.0
NEG = -1e30

ENGINES = ("tensor", "vector", "scalar", "gpsimd", "sync")
BF = ml_dtypes.bfloat16
SB_BASE = 16512



def host_gather_indices(mapping, own_start):
    m = np.asarray(mapping).astype(np.int64)
    qidx = m[np.arange(own_start, own_start + HALF)]
    kidx = np.empty(HALF, dtype=np.int64)
    for j in range(HALF // BLOCK_M):
        gn = (own_start + j * BLOCK_M) // BLOCK_M
        seg = m[gn * BLOCK_M] // SEG
        kidx[j * BLOCK_M:(j + 1) * BLOCK_M] = m[seg * SEG + DIL * np.arange(BLOCK_M)]
    return qidx, kidx


def host_consts():
    c = {}
    c["cn_ident"] = np.eye(128, dtype=BF)
    c["cn_ones_1x128"] = np.ones((1, 128), dtype=BF)
    c["cn_ones_1x512"] = np.ones((1, 512), dtype=BF)
    blk = np.zeros((128, 2), dtype=BF)
    blk[0:64, 0] = 1
    blk[64:128, 1] = 1
    c["cn_blk128x2"] = blk
    blk2 = np.zeros((2, 128), dtype=BF)
    blk2[0, 0:64] = 1
    blk2[1, 64:128] = 1
    c["cn_blk2x128"] = blk2
    for e in range(4):
        s = np.zeros((4, 128), dtype=BF)
        s[e, :] = 1
        c[f"cn_sel{e}"] = s
    return c



class PB:

    def __init__(self, nc):
        self.nc = nc
        self.q = {e: [] for e in ENGINES}
        self.cnt = {e: 0 for e in ENGINES}
        self.psem = {e: nc.alloc_semaphore(f"p_{e}") for e in ENGINES}
        self.chsem = {}
        self.chcnt = {}
        self.wm = {e: {} for e in ENGINES}
        self.want_pid = False
        self.pid = None
        self.ppid = None
        self.phase = "init"
        self.etags = {e: [] for e in ENGINES}

    def _waits(self, eng, deps):
        out = []
        wm = self.wm[eng]
        for d in deps:
            if d is None:
                continue
            sem, val = d
            k = sem.name
            if wm.get(k, 0) >= val:
                continue
            wm[k] = val
            out.append((sem, val))
        return out

    def comp(self, eng, fn, deps=()):
        self.cnt[eng] += 1
        self.etags[eng].append(self.phase)
        tok = (self.psem[eng], self.cnt[eng])
        self.q[eng].append((self._waits(eng, deps), fn, tok, 1))
        return tok

    def comp_n(self, eng, fn, n, deps=()):
        self.cnt[eng] += n
        self.etags[eng].extend([self.phase] * n)
        tok = (self.psem[eng], self.cnt[eng])
        self.q[eng].append((self._waits(eng, deps), fn, tok, None))
        return tok

    CH_ENG = {}

    def dma(self, eng, out, in_, deps=(), channel="misc", **kw):
        eng = self.CH_ENG.get(channel, eng)
        if channel not in self.chsem:
            self.chsem[channel] = self.nc.alloc_semaphore(f"c_{channel}")
            self.chcnt[channel] = 0
        self.chcnt[channel] += 1
        tok = (self.chsem[channel], 16 * self.chcnt[channel])

        def fn(e, out=out, in_=in_, kw=kw):
            if callable(out):
                out = out()
            if callable(in_):
                in_ = in_()
            return e.dma_start(out=out, in_=in_, **kw)
        self.q[eng].append((self._waits(eng, deps), fn, tok, 16))
        return tok

    def raw(self, eng, fn, deps=()):
        self.q[eng].append((self._waits(eng, deps), fn, None, 0))

    def waits_only(self, eng, deps):
        ws = self._waits(eng, deps)
        if ws:
            self.q[eng].append((ws, None, None, 0))

    def snapshot(self):
        toks = []
        for e in ENGINES:
            if self.cnt[e]:
                toks.append((self.psem[e], self.cnt[e]))
        for ch, sem in self.chsem.items():
            toks.append((sem, 16 * self.chcnt[ch]))
        return toks

    def barrier(self):
        toks = self.snapshot()
        for e in ENGINES:
            self.waits_only(e, list(toks))

    def realize(self):
        with self.nc.Block() as block:
            for ename in ENGINES:
                ops = self.q[ename]

                def body(eng, ops=ops, ename=ename):
                    if ename == "sync" and self.want_pid:
                        pid = self.nc.partition_id(engines=[ET.SP])
                        self.pid = pid
                        self.ppid = self.nc.s_assert_within(
                            pid + 1 - (pid % 2) * 2, 0, 7,
                            skip_runtime_assert=True)
                    for oi, (waits, fn, tok, inc) in enumerate(ops):
                        for (s, v) in waits:
                            eng.wait_ge(s, v)
                        if fn is None:
                            continue
                        try:
                            if inc is None:
                                fn(eng, tok[0])
                                continue
                            inst = fn(eng)
                        except Exception:
                            import sys
                            print(f"REALIZE FAIL {ename} op {oi}/{len(ops)}",
                                  file=sys.stderr)
                            raise
                        if tok is not None and inst is not None:
                            inst.then_inc(tok[0], inc)
                getattr(block, ename)(body)


class Ctx:
    def __init__(self, nc):
        self.nc = nc
        self.pb = PB(nc)
        self.PS = nc.alloc_psum_tensor("PS", [128, 4096], F32).ap()
        self.PSb = self.PS.bitcast(BF16)
        self.bank_tok = [None] * 8
        self.slab_last = {"sla": None, "slb": None, "u1": None}
        self.sb = {}

    def bank(self, i):
        return self.PS[:, 512 * i: 512 * (i + 1)]

    def bank_bf(self, i):
        return self.PSb[:, 1024 * i: 1024 * (i + 1)]

    def sb_at(self, name, offset, shape, dtype):
        if name not in self.sb:
            self.sb[name] = self.nc.alloc_sbuf_tensor_at(
                name, shape, dtype, offset=SB_BASE + offset).ap()
        return self.sb[name]


def mmg(ctx, bank_i, terms, deps=(), cont=False, independent=False):
    pb = ctx.pb
    alldeps = list(deps)
    if not cont and ctx.bank_tok[bank_i] is not None:
        alldeps.append(ctx.bank_tok[bank_i])

    def fn(t, sem, terms=terms, cont=cont, independent=independent):
        for i, (out_ap, lhsT, rhs, tp) in enumerate(terms):
            kw = {}
            if tp is not None:
                kw["tile_position"] = tp
            st = True if independent else (i == 0 and not cont)
            sp = True if independent else (i == len(terms) - 1)
            t.matmul(out_ap, lhsT, rhs, start=st, stop=sp,
                     skip_group_check=True, **kw).then_inc(sem, 1)
    return pb.comp_n("tensor", fn, len(terms), deps=alldeps)


SBMAP = dict(
    BIG_A=0,
    BIG_B=32768,
    BIG_C=65536,
    BIG_D=98304,
    ATt=131072,
    E1=139264,
    E2=147456,
    RAWF=155648,
    U1=171840,
    SQ=180032,
    STM=184128,
    STB=194368,
    LNT=196416,
    SLA=198464,
    SLB=200512,
    LS=202560,
    LSR=204608,
    NMX=205632,
    BQK=205664,
    BQKQ=205760,
    BV=205792,
    IDENT=207840,
    ONES1x128=208096,
    ONES1x512=208352,
    BLK128x2=209376,
    BLK2x128=209440,
    ONES128x1=209696,
    GW=209728,
    GSC=209792,
    GROWS4=209920,
    GBF=211968,
    GB=211976,
    SB1S=211968,
    SEL=212112,
    BCOMB=212112,
)
SBMAP["GROWS4"] = 209920
SBMAP["BCOMB"] = 210944
SBMAP["GBF"] = 209792 + 64
SBMAP["GB"] = 209792 + 96
SBMAP["GROWE"] = 209792 + 96
SBMAP["GROWE"] = 202560
SBMAP["ONES128x1F"] = 212096
del SBMAP["SEL"]


def emit_ln(ctx, raw, raw_dep, out_bf, mt=MT,
            bank_s=6, bank_q=7, bank_a=4, bank_c=5):
    pb = ctx.pb
    onescol = ctx.sb_at("ONES128x1F", SBMAP["ONES128x1F"], [128, 1], F32)
    onesrow = ctx.sb_at("ONES1x128", SBMAP["ONES1x128"], [1, 128], BF16)
    sq = ctx.sb_at("SQ", SBMAP["SQ"], [128, 1024], F32)
    stm = ctx.sb_at("STM", SBMAP["STM"], [1, 2560], F32)
    stb = ctx.sb_at("STB", SBMAP["STB"], [1, 1024], BF16)
    lnt = ctx.sb_at("LNT", SBMAP["LNT"], [128, 512], F32)
    mean = stm[:, 0:mt]
    var = stm[:, 512:512 + mt]
    rstd = stm[:, 1024:1024 + mt]
    crow = stm[:, 1536:1536 + mt]
    m2 = stm[:, 2048:2048 + mt]
    rstd_b = stb[:, 0:mt]
    crow_b = stb[:, 512:512 + mt]

    t_s = mmg(ctx, bank_s,
              [(ctx.bank(bank_s)[0:1, 0:mt], onescol, raw(c), None)
               for c in range(NCH)], deps=[raw_dep])
    sq_tok = [None, None]
    mm_prev = None
    for c in range(NCH):
        half = (c % 2) * 512
        tk = pb.comp("scalar", lambda e, c=c, half=half: e.activation(
            sq[:, half:half + mt], raw(c), AF.Square),
            deps=[raw_dep, sq_tok[c % 2]])
        dep = [tk]
        if c == 0 and ctx.bank_tok[bank_q] is not None:
            dep.append(ctx.bank_tok[bank_q])
        mm_prev = mmg(ctx, bank_q, [(ctx.bank(bank_q)[0:1, 0:mt], onescol,
                                     sq[:, half:half + mt], None)],
                      deps=dep, cont=(c > 0))
        sq_tok[c % 2] = mm_prev
    t_q = mm_prev

    t1 = pb.comp("scalar", lambda e: e.activation(
        mean, ctx.bank(bank_s)[0:1, 0:mt], AF.Identity, scale=1.0 / D), deps=[t_s])
    ctx.bank_tok[bank_s] = t1
    tm2 = pb.comp("scalar", lambda e: e.activation(m2, mean, AF.Square), deps=[t1])
    t2 = pb.comp("vector", lambda e: e.scalar_tensor_tensor(
        var, ctx.bank(bank_q)[0:1, 0:mt], 1.0 / D, m2,
        op0=ALU.mult, op1=ALU.subtract), deps=[t_q, tm2])
    ctx.bank_tok[bank_q] = t2
    t2b = pb.comp("vector", lambda e: e.tensor_scalar_add(var, var, EPS_LN),
                  deps=[t2])
    t3 = pb.comp("scalar", lambda e: e.activation(var, var, AF.Ln), deps=[t2b])
    t4 = pb.comp("scalar", lambda e: e.activation(rstd, var, AF.Exp, scale=-0.5),
                 deps=[t3])
    t5 = pb.comp("vector", lambda e: e.scalar_tensor_tensor(
        crow, mean, -1.0, rstd, op0=ALU.mult, op1=ALU.mult), deps=[t4])
    tb1 = pb.comp("vector", lambda e: e.tensor_copy(rstd_b, rstd), deps=[t4])
    tb2 = pb.comp("vector", lambda e: e.tensor_copy(crow_b, crow), deps=[t5])
    tA = mmg(ctx, bank_a, [(ctx.bank(bank_a)[:, 0:mt], onesrow, rstd_b, None)],
             deps=[tb1])
    tC = mmg(ctx, bank_c, [(ctx.bank(bank_c)[:, 0:mt], onesrow, crow_b, None)],
             deps=[tb2])
    prev = None
    for c in range(NCH):
        p1 = pb.comp("vector", lambda e, c=c: e.tensor_tensor(
            lnt[:, 0:mt], raw(c), ctx.bank(bank_a)[:, 0:mt], op=ALU.mult),
            deps=[tA, raw_dep, prev])
        p2 = pb.comp("vector", lambda e, c=c: e.tensor_tensor(
            out_bf(c), lnt[:, 0:mt], ctx.bank(bank_c)[:, 0:mt], op=ALU.add),
            deps=[p1, tC])
        prev = p2
    ctx.bank_tok[bank_a] = prev
    ctx.bank_tok[bank_c] = prev
    return prev, [prev, t_s, t_q]


def rearr_kpm(dram_2d):
    return dram_2d.rearrange("(kc kp) m -> kp kc m", kp=128)


def apv(x):
    return x() if callable(x) else x


def emit_block0(ctx, io, ph1_only=False, stage=9):
    nc, pb = ctx.nc, ctx.pb
    sa = ctx.sb_at
    BIG_A = sa("BIG_A", SBMAP["BIG_A"], [128, 16384], BF16)
    BIG_B = sa("BIG_B", SBMAP["BIG_B"], [128, 16384], BF16)
    BIG_C = sa("BIG_C", SBMAP["BIG_C"], [128, 16384], BF16)
    BIG_D = sa("BIG_D", SBMAP["BIG_D"], [128, 16384], BF16)
    ATt = sa("ATt", SBMAP["ATt"], [128, 4096], BF16)
    Ebuf = sa("E1", SBMAP["E1"], [128, 4096], BF16)
    E2buf = sa("E2", SBMAP["E2"], [128, 4096], BF16)
    RAWF = sa("RAWF", SBMAP["RAWF"], [128, 4096], F32)
    U1 = sa("U1", SBMAP["U1"], [128, 4096], BF16)
    SLA = sa("SLA", SBMAP["SLA"], [128, 1024], BF16)
    SLB = sa("SLB", SBMAP["SLB"], [128, 1024], BF16)
    LS = sa("LS", SBMAP["LS"], [2, 512], F32)
    LSR = sa("LSR", SBMAP["LSR"], [2, 512], BF16)
    NMX = sa("NMX", SBMAP["NMX"], [128, 8], F32)
    BQK = sa("BQK", SBMAP["BQK"], [128, 24], F32)
    BQKQ = sa("BQKQ", SBMAP["BQKQ"], [128, 8], F32)
    BV = sa("BV", SBMAP["BV"], [1, 1024], BF16)
    IDENT = sa("IDENT", SBMAP["IDENT"], [128, 128], BF16)
    ONES1x128 = sa("ONES1x128", SBMAP["ONES1x128"], [1, 128], BF16)
    ONES1x512 = sa("ONES1x512", SBMAP["ONES1x512"], [1, 512], BF16)
    BLK128x2 = sa("BLK128x2", SBMAP["BLK128x2"], [128, 2], BF16)
    BLK2x128 = sa("BLK2x128", SBMAP["BLK2x128"], [2, 128], BF16)
    sa("ONES128x1", SBMAP["ONES128x1"], [128, 1], BF16)
    GW = sa("GW", SBMAP["GW"], [128, 32], BF16)
    GSC = sa("GSC", SBMAP["GSC"], [128, 16], F32)
    GBF = sa("GBF", SBMAP["GBF"], [128, 4], BF16)
    GB = sa("GB", SBMAP["GB"], [1, 4], BF16)
    GROWS4 = sa("GROWS4", SBMAP["GROWS4"], [4, 1024], BF16)
    GROWE = sa("GROWE", SBMAP["GROWE"], [1, 1024], BF16)
    SB1S = sa("SB1S", SBMAP["SB1S"], [128, 32], F32)

    pb.phase = "b0.init"
    tk_ones = pb.comp("gpsimd", lambda e: e.memset(
        ctx.sb["ONES128x1"], 1.0))
    sa("ONES128x1F", SBMAP["ONES128x1F"], [128, 1], F32)
    tk_onesf = pb.comp("gpsimd", lambda e: e.memset(ctx.sb["ONES128x1F"], 1.0))
    t_id = pb.dma("gpsimd", IDENT, io["cn_ident"], channel="const")
    t_o1 = pb.dma("gpsimd", ONES1x128, io["cn_ones_1x128"], channel="const")
    t_o5 = pb.dma("gpsimd", ONES1x512, io["cn_ones_1x512"], channel="const")
    t_bk = pb.dma("gpsimd", BLK128x2, io["cn_blk128x2"], channel="const")
    t_bk2 = pb.dma("gpsimd", BLK2x128, io["cn_blk2x128"], channel="const")
    t_bq = pb.dma("gpsimd", BQK, io["bqkv_pc"], channel="const")

    t_bv = pb.dma("gpsimd", BV, io["b_qkv"][2048:3072].unsqueeze(0), channel="const")
    t_gw = pb.dma("gpsimd", GW.rearrange("p (kc e) -> p kc e", kc=8),
                  io["gate_w"].rearrange("(kc kp) e -> kp kc e", kp=128),
                  channel="const")
    t_gb = pb.dma("gpsimd", GB, io["gate_b"].unsqueeze(0), channel="const")
    pb.barrier()
    t_bqs = pb.comp("scalar", lambda e: e.mul(BQKQ, BQK[:, 0:8], 0.125))

    pb.phase = "b0.ph1"
    def ln_into(src_dram, dst_big, free0):
        toks = []
        free = list(free0)
        for mi in range(NMT):
            dep = pb.dma("sync", RAWF.rearrange("p (c m) -> p c m", c=8),
                         rearr_kpm(src_dram)[:, :, mi * MT:(mi + 1) * MT],
                         deps=free, channel="rawf")
            last, free = emit_ln(
                ctx, lambda c: RAWF[:, c * MT:(c + 1) * MT], dep,
                lambda c, mi=mi: dst_big[:, c * 2048 + mi * MT:c * 2048 + (mi + 1) * MT])
            toks.append(last)
        return toks[-1], free

    xnq_tok, _lnfree_q = ln_into(io["xqT"], BIG_A, [])
    if stage <= 1:
        return []

    def fm_proj(w_cols_n0, dst_big, bias_col, scale, src_big, src_tok,
                banks=(0, 1, 2, 3)):
        out_toks = []
        for ni in range(NCH):
            sl = (SLA, SLB)[ni % 2]
            ch = ("sla", "slb")[ni % 2]
            tld = pb.dma("gpsimd", sl,
                         rearr_kpm(io["w_qkv"])[:, :, w_cols_n0 + ni * 128:
                                                w_cols_n0 + (ni + 1) * 128],
                         deps=[ctx.slab_last[ch]], channel=ch)
            last_mm = None
            for mi in range(NMT):
                bk = banks[mi % len(banks)]
                terms = [(ctx.bank(bk), sl[:, kc * 128:(kc + 1) * 128],
                          src_big[:, kc * 2048 + mi * MT:kc * 2048 + (mi + 1) * MT],
                          None) for kc in range(NCH)]
                tm = mmg(ctx, bk, terms, deps=[tld, src_tok])
                te = pb.comp("scalar", lambda e, bk=bk, ni=ni, mi=mi: e.activation(
                    dst_big[:, ni * 2048 + mi * MT:ni * 2048 + (mi + 1) * MT],
                    ctx.bank(bk), AF.Identity, bias=bias_col(ni), scale=scale),
                    deps=[tm])
                ctx.bank_tok[bk] = te
                out_toks.append(te)
                last_mm = tm
            ctx.slab_last[ch] = last_mm
        return out_toks[-1]

    q_tok = fm_proj(0, BIG_B, lambda ni: BQKQ[:, ni:ni + 1], 0.125,
                    BIG_A, xnq_tok)
    if stage <= 2:
        return []

    xnkv_tok, _lnfree_kv = ln_into(io["xkvT"], BIG_C, _lnfree_q)
    k_tok = fm_proj(1024, BIG_D, lambda ni: BQK[:, 8 + ni:9 + ni], 1.0,
                    BIG_C, xnkv_tok)
    if stage <= 3:
        return []

    v_toks = []
    vw_tok = None
    for nh in range(2):
        vw_tok = pb.dma("gpsimd", U1,
                        rearr_kpm(io["w_qkv"])[:, :, 2048 + nh * 512:2048 + (nh + 1) * 512],
                        deps=[ctx.slab_last["u1"], q_tok], channel="u1")
        for tc in range(16):
            bk = tc % 4
            terms = [(ctx.bank(bk), BIG_C[:, kc * 2048 + tc * 128:kc * 2048 + (tc + 1) * 128],
                      U1[:, kc * 512:(kc + 1) * 512], None) for kc in range(NCH)]
            terms.append((ctx.bank(bk), ONES1x128, BV[:, nh * 512:(nh + 1) * 512], None))
            tm = mmg(ctx, bk, terms, deps=[vw_tok, xnkv_tok, t_bv, q_tok])
            ctx.slab_last["u1"] = tm
            te = pb.comp("vector", lambda e, bk=bk, tc=tc, nh=nh: e.tensor_copy(
                BIG_A[:, tc * 1024 + nh * 512:tc * 1024 + (nh + 1) * 512],
                ctx.bank(bk)), deps=[tm])
            ctx.bank_tok[bk] = te
            v_toks.append(te)
    v_tok = v_toks[-1]
    if stage <= 4:
        return []

    QT, KT, VT = BIG_B, BIG_D, BIG_A
    at_dma = None
    last_exp = None
    last_r = None
    last_br = None
    prev_ta = None
    VT2t = E2buf
    for mi in range(NMT):
        vsw1 = pb.dma("sync", VT2t[0:64, :], VT[64:128, mi * 4096:(mi + 1) * 4096],
                      deps=[v_tok, prev_ta], channel="e2s")
        vsw_tok = pb.dma("sync", VT2t[64:128, :], VT[0:64, mi * 4096:(mi + 1) * 4096],
                         deps=[v_tok, prev_ta], channel="e2s")
        hp_toks = []
        for hp in range(NCH):
            bk_s = hp % 2
            bk_l = 2
            bk_r = 3
            bk_o = 4 + hp % 2
            col = lambda j: hp * 2048 + mi * MT + j * 64
            terms = []
            for j in range(8):
                for h2 in range(2):
                    b0 = h2 * 64
                    terms.append((ctx.bank(bk_s)[b0:b0 + 64, j * 64:j * 64 + 64],
                                  KT[b0:b0 + 64, col(j):col(j) + 64],
                                  QT[b0:b0 + 64, col(j):col(j) + 64],
                                  (b0, b0)))
            t_s = mmg(ctx, bk_s, terms, deps=[q_tok, k_tok, at_dma],
                      independent=True)
            t_mx = pb.comp("vector", lambda e, bk_s=bk_s: e.tensor_reduce(
                NMX, ctx.bank(bk_s).rearrange("p (j x) -> p j x", j=8),
                axis=AX.X, op=ALU.max, negate=True), deps=[t_s, last_exp])
            e_toks = []
            for j in range(8):
                tk = pb.comp("scalar", lambda e, j=j, hp=hp, bk_s=bk_s: e.activation(
                    Ebuf[:, hp * 512 + j * 64:hp * 512 + (j + 1) * 64],
                    ctx.bank(bk_s)[:, j * 64:j * 64 + 64], AF.Exp,
                    bias=NMX[:, j:j + 1]), deps=[t_mx])
                e_toks.append(tk)
            ctx.bank_tok[bk_s] = e_toks[-1]
            if stage <= 5:
                last_exp = e_toks[-1]
                prev_ta = e_toks[-1]
                hp_toks.append(e_toks[-1])
                continue
            lterms = [(ctx.bank(bk_l)[0:2, j * 64:j * 64 + 64], BLK128x2,
                       Ebuf[:, hp * 512 + j * 64:hp * 512 + (j + 1) * 64], None)
                      for j in range(8)]
            t_l = mmg(ctx, bk_l, lterms, deps=e_toks, independent=True)
            t_eps = pb.comp("vector", lambda e, bk_l=bk_l: e.tensor_scalar_add(
                LS, ctx.bank(bk_l)[0:2, :], 1e-6), deps=[t_l, last_r])
            ctx.bank_tok[bk_l] = t_eps
            t_r0 = pb.comp("vector", lambda e: e.reciprocal(LS, LS),
                           deps=[t_eps, last_br])
            t_r = pb.comp("vector", lambda e: e.tensor_copy(LSR, LS), deps=[t_r0])
            t_br = mmg(ctx, bk_r, [(ctx.bank(bk_r), BLK2x128, LSR, None)],
                       deps=[t_r])
            t_en = pb.comp("vector", lambda e, hp=hp, bk_r=bk_r: e.tensor_tensor(
                Ebuf[:, hp * 512:(hp + 1) * 512], Ebuf[:, hp * 512:(hp + 1) * 512],
                ctx.bank(bk_r), op=ALU.mult), deps=[t_br] + e_toks)
            ctx.bank_tok[bk_r] = t_en
            if stage <= 6:
                last_exp = e_toks[-1]
                last_r = t_r
                last_br = t_br
                prev_ta = t_en
                hp_toks.append(t_en)
                continue
            aterms = []
            for j in range(8):
                tcl = j // 2
                for h2 in range(2):
                    h = 2 * hp + h2
                    b0 = h2 * 64
                    src = VT if (j % 2) == h2 else VT2t
                    off = mi * 4096 if src is VT else 0
                    aterms.append((
                        ctx.bank(bk_o)[b0:b0 + 64, j * 64:j * 64 + 64],
                        src[b0:b0 + 64, off + tcl * 1024 + h * 64:
                            off + tcl * 1024 + h * 64 + 64],
                        Ebuf[b0:b0 + 64, hp * 512 + j * 64:hp * 512 + (j + 1) * 64],
                        (b0, b0)))
            t_a = mmg(ctx, bk_o, aterms, deps=[t_en, vsw_tok, v_tok],
                      independent=True)
            t_ev = pb.comp("scalar", lambda e, hp=hp, bk_o=bk_o: e.copy(
                ATt[:, hp * 512:(hp + 1) * 512], ctx.bank(bk_o)), deps=[t_a])
            ctx.bank_tok[bk_o] = t_ev
            hp_toks.append(t_ev)
            last_exp = e_toks[-1]
            last_r = t_r
            last_br = t_br
            prev_ta = t_a
        if stage >= 7:
            at_dma = pb.dma("sync", rearr_kpm(io["sc_A"])[:, :, mi * MT:(mi + 1) * MT],
                            ATt.rearrange("p (c m) -> p c m", c=8),
                            deps=hp_toks + [at_dma], channel="ascA")
        else:
            at_dma = None

    pb.barrier()
    if ph1_only or stage <= 8:
        return []

    AT2 = sa("AT2", SBMAP["E1"], [128, 8192], BF16)
    X1ACC = sa("X1ACC", SBMAP["BIG_D"], [128, 8192], F32)
    BCOMB = sa("BCOMB", SBMAP["ATt"], [1, 1024], BF16)
    GROWE = sa("GROWE", SBMAP["LS"], [1, 1024], BF16)
    RB1S = sa("RB1S", SBMAP["ATt"] + 2048, [1, 128], BF16)
    SELs = [sa(f"SEL{e}", SBMAP["LSR"] + 256 * e, [4, 128], BF16)
            for e in range(4)]

    def xn_ap(c, q):
        return BIG_B[:, c * 1024 + q * 512:c * 1024 + q * 512 + 512]

    def xg_ap(c, q):
        return BIG_B[:, 8192 + c * 1024 + q * 512:8192 + c * 1024 + q * 512 + 512]

    def rh_ap(fc, q):
        big = BIG_A if fc < 16 else BIG_C
        f = fc % 16
        return big[:, f * 1024 + q * 512:f * 1024 + q * 512 + 512]

    def acc_ap(oc, q):
        return X1ACC[:, oc * 1024 + q * 512:oc * 1024 + q * 512 + 512]

    rowa = RAWF[0:1, 0:1024]
    rowb = RAWF[0:1, 1024:2048]
    rowc = RAWF[0:1, 2048:3072]
    ta = pb.dma("gpsimd", rowa, io["sb2"][0].unsqueeze(0))
    tb = pb.dma("gpsimd", rowb, io["sb2"][1].unsqueeze(0))
    tc = pb.dma("gpsimd", rowc, io["b_proj"].unsqueeze(0))
    t1 = pb.comp("vector", lambda e: e.tensor_add(rowa, rowa, rowb), deps=[tc])
    bcomb_tok = pb.comp("vector", lambda e: e.scalar_tensor_tensor(
        BCOMB, rowa, 0.5, rowc, op0=ALU.mult, op1=ALU.add), deps=[t1, tc])
    sel_toks = [pb.dma("gpsimd", SELs[e], io[f"cn_sel{e}"], channel="sel")
                for e in range(4)]
    sel_last = sel_toks[-1]

    x1_dmas = []
    mp_prev_done = None
    x1_prev = None
    for mp in range(2):
        pb.phase = "b0.ph2.gate"
        gate_toks = []
        xn_toks = []
        raw_free = [bcomb_tok]
        for q in range(2):
            mi = 2 * mp + q
            x0_dep = pb.dma("sync", RAWF.rearrange("p (c m) -> p c m", c=8),
                            rearr_kpm(io["xownT"])[:, :, mi * MT:(mi + 1) * MT],
                            deps=raw_free + [mp_prev_done, x1_prev],
                            channel="rawf")
            t_xn, raw_free = emit_ln(ctx, lambda c: RAWF[:, c * MT:(c + 1) * MT],
                                     x0_dep, lambda c, q=q: xn_ap(c, q))
            xn_toks.append(t_xn)
            for mc in range(4):
                bkg = 6
                gterms = [(ctx.bank(bkg)[:, 0:4],
                           xn_ap(kc, q)[:, mc * 128:(mc + 1) * 128],
                           GW[:, kc * 4:(kc + 1) * 4], None) for kc in range(NCH)]
                gterms.append((ctx.bank(bkg)[:, 0:4], ONES1x128, GB, None))
                tg = mmg(ctx, bkg, gterms, deps=[t_xn, t_gw, t_gb])
                g0 = pb.comp("vector", lambda e, bkg=bkg: e.tensor_reduce(
                    GSC[:, 0:1], ctx.bank(bkg)[:, 0:4], axis=AX.X, op=ALU.max,
                    negate=True), deps=[tg])
                g1 = pb.comp("scalar", lambda e, bkg=bkg: e.activation(
                    GSC[:, 4:8], ctx.bank(bkg)[:, 0:4], AF.Exp,
                    bias=GSC[:, 0:1]), deps=[g0])
                ctx.bank_tok[bkg] = g1
                g2 = pb.comp("vector", lambda e: e.tensor_reduce(
                    GSC[:, 1:2], GSC[:, 4:8], axis=AX.X, op=ALU.add), deps=[g1])
                g3 = pb.comp("vector", lambda e: e.reciprocal(
                    GSC[:, 2:3], GSC[:, 1:2]), deps=[g2])
                g4 = pb.comp("vector", lambda e: e.tensor_scalar_mul(
                    GSC[:, 4:8], GSC[:, 4:8], GSC[:, 2:3]), deps=[g3])
                g5 = pb.comp("vector", lambda e: e.tensor_reduce(
                    GSC[:, 8:9], GSC[:, 4:8], axis=AX.X, op=ALU.max), deps=[g4])
                g6 = pb.comp("vector", lambda e: e.tensor_scalar(
                    GSC[:, 9:13], GSC[:, 4:8], GSC[:, 8:9], None,
                    op0=ALU.is_ge), deps=[g5])
                g7 = pb.comp("vector", lambda e: e.scalar_tensor_tensor(
                    GSC[:, 9:13], GSC[:, 9:13], -1e9, GSC[:, 4:8],
                    op0=ALU.mult, op1=ALU.add), deps=[g6])
                g8 = pb.comp("vector", lambda e: e.tensor_reduce(
                    GSC[:, 13:14], GSC[:, 9:13], axis=AX.X, op=ALU.max), deps=[g7])
                g9 = pb.comp("vector", lambda e: e.tensor_scalar(
                    GSC[:, 9:13], GSC[:, 4:8], GSC[:, 13:14], None,
                    op0=ALU.is_ge), deps=[g8])
                g10 = pb.comp("vector", lambda e: e.tensor_tensor(
                    GBF, GSC[:, 4:8], GSC[:, 9:13], op=ALU.mult), deps=[g9])
                bkt = 7
                tt = pb.comp("tensor", lambda e, bkt=bkt: e.transpose(
                    ctx.bank_bf(bkt)[0:4, 0:128], GBF, IDENT),
                    deps=[g10, t_id, ctx.bank_tok[bkt]])
                tcp = pb.comp("scalar", lambda e, mc=mc, q=q, bkt=bkt: e.copy(
                    GROWS4[:, q * 512 + mc * 128:q * 512 + (mc + 1) * 128],
                    ctx.bank_bf(bkt)[0:4, 0:128]), deps=[tt])
                ctx.bank_tok[bkt] = tcp
                gate_toks.append(tcp)

        pb.phase = "b0.ph2.at2"
        at2_tok = pb.dma("sync", AT2.rearrange("p (c q m) -> p c q m", c=8, q=2),
                         rearr_kpm(io["sc_A"])[:, :, mp * 1024:(mp + 1) * 1024]
                         .rearrange("p c (q m) -> p c q m", q=2),
                         deps=[at_dma, mp_prev_done], channel="at2")

        pb.phase = "b0.ph2.base"
        base_last = None
        for q in range(2):
            mi = 2 * mp + q
            x0_dep = pb.dma("sync", RAWF.rearrange("p (c m) -> p c m", c=8),
                            rearr_kpm(io["xownT"])[:, :, mi * MT:(mi + 1) * MT],
                            deps=raw_free + gate_toks[-1:], channel="rawf")
            raw_free = []
            for oc in range(NCH):
                sl = (SLA, SLB)[oc % 2]
                ch = ("sla", "slb")[oc % 2]
                tld = pb.dma("gpsimd", sl,
                             rearr_kpm(io["w_proj"])[:, :, oc * 128:(oc + 1) * 128],
                             deps=[ctx.slab_last[ch]], channel=ch)
                bk = oc % 4
                terms = [(ctx.bank(bk), sl[:, kc * 128:(kc + 1) * 128],
                          AT2[:, kc * 1024 + q * 512:kc * 1024 + q * 512 + 512],
                          None) for kc in range(NCH)]
                terms.append((ctx.bank(bk), BCOMB[:, oc * 128:(oc + 1) * 128],
                              ONES1x512, None))
                tm = mmg(ctx, bk, terms, deps=[tld, at2_tok, bcomb_tok, t_o5])
                ctx.slab_last[ch] = tm
                te = pb.comp("vector", lambda e, bk=bk, oc=oc, q=q: e.tensor_tensor(
                    acc_ap(oc, q), ctx.bank(bk), RAWF[:, oc * MT:(oc + 1) * MT],
                    op=ALU.add), deps=[tm, x0_dep, x1_prev])
                ctx.bank_tok[bk] = te
                base_last = te
            raw_free = [base_last]

        pb.phase = "b0.ph2.experts"
        prev_done = base_last
        for kind, ne in (("r", 4), ("s", 2)):
            for e in range(ne):
                if kind == "r":
                    w1d = io["rw1"][e]
                    w2d = io["rw2"][e]
                    xg_toks = []
                    ge_toks = []
                    for q in range(2):
                        bkg = 6
                        tbg = mmg(ctx, bkg, [(ctx.bank(bkg), SELs[e],
                                              GROWS4[:, q * 512:(q + 1) * 512],
                                              None)],
                                  deps=[sel_last, gate_toks[-1], prev_done])
                        tge = pb.comp("vector", lambda e2, q=q, bkg=bkg:
                                      e2.tensor_copy(
                                          GROWE[:, q * 512:(q + 1) * 512],
                                          ctx.bank(bkg)[0:1, :]),
                                      deps=[tbg, prev_done])
                        ge_toks.append(tge)
                        for c in range(NCH):
                            txg = pb.comp("vector", lambda e2, c=c, q=q, bkg=bkg:
                                          e2.tensor_tensor(
                                              xg_ap(c, q), xn_ap(c, q),
                                              ctx.bank(bkg), op=ALU.mult),
                                          deps=[tbg, prev_done, xn_toks[q]])
                            xg_toks.append(txg)
                        ctx.bank_tok[bkg] = xg_toks[-1]
                    rhs_src = xg_ap
                    ge_dep = ge_toks[-1]
                    rb1_dram = io["rb1"][e]
                    rb2_tok = pb.dma("gpsimd", BV, io["rb2"][e].unsqueeze(0),
                                     deps=[prev_done], channel="bv")
                    stage1_extra = xg_toks[-1]
                else:
                    w1d = io["sw1"][e]
                    w2d = io["sw2"][e]
                    rhs_src = xn_ap
                    ge_dep = None
                    rb1_dram = None
                    tsb = pb.dma("gpsimd", SB1S, io["sb1_pc"][e],
                                 deps=[prev_done], channel="sb1")
                    sb1_tok = pb.comp("scalar", lambda e2: e2.mul(SB1S, SB1S, 0.5),
                                      deps=[tsb])
                    stage1_extra = sb1_tok

                rh_last = None
                for fc in range(32):
                    sl = (SLA, SLB)[fc % 2]
                    ch = ("sla", "slb")[fc % 2]
                    tld = pb.dma("gpsimd", sl,
                                 rearr_kpm(w1d)[:, :, fc * 128:(fc + 1) * 128],
                                 deps=[ctx.slab_last[ch], prev_done], channel=ch)
                    if kind == "r":
                        trb = pb.dma("gpsimd", RB1S,
                                     rb1_dram[fc * 128:(fc + 1) * 128].unsqueeze(0),
                                     deps=[rh_last, prev_done], channel="rb1")
                    for q in range(2):
                        bk = fc % 2
                        terms = [(ctx.bank(bk), sl[:, kc * 128:(kc + 1) * 128],
                                  rhs_src(kc, q), None) for kc in range(NCH)]
                        if kind == "r":
                            terms.append((ctx.bank(bk), RB1S,
                                          GROWE[:, q * 512:(q + 1) * 512], None))
                            dep2 = [trb, ge_dep]
                        else:
                            dep2 = []
                        tm = mmg(ctx, bk, terms,
                                 deps=[tld, stage1_extra] + dep2)
                        if kind == "r":
                            if fc % 2 == 0:
                                te = pb.comp("vector", lambda e2, bk=bk, fc=fc, q=q:
                                             e2.tensor_scalar_max(
                                                 rh_ap(fc, q), ctx.bank(bk), 0.0),
                                             deps=[tm])
                            else:
                                te = pb.comp("scalar", lambda e2, bk=bk, fc=fc, q=q:
                                             e2.activation(rh_ap(fc, q),
                                                           ctx.bank(bk), AF.Relu),
                                             deps=[tm])
                        else:
                            te = pb.comp("scalar", lambda e2, bk=bk, fc=fc, q=q:
                                         e2.activation(
                                             rh_ap(fc, q), ctx.bank(bk), AF.Relu,
                                             bias=SB1S[:, fc:fc + 1], scale=0.5),
                                         deps=[tm, sb1_tok])
                        ctx.bank_tok[bk] = te
                        rh_last = te
                    ctx.slab_last[ch] = tm

                w2_tok = None
                st2_last = None
                for oc in range(NCH):
                    w2_tok = pb.dma("gpsimd", U1,
                                    w2d.rearrange("(fc kp) o -> kp fc o", kp=128)
                                    [:, :, oc * 128:(oc + 1) * 128],
                                    deps=[ctx.slab_last["u1"], rh_last],
                                    channel="u1")
                    for q in range(2):
                        bk = 2 + (oc * 2 + q) % 2
                        terms = [(ctx.bank(bk), U1[:, fc * 128:(fc + 1) * 128],
                                  rh_ap(fc, q), None) for fc in range(32)]
                        if kind == "r":
                            terms.append((ctx.bank(bk),
                                          BV[:, oc * 128:(oc + 1) * 128],
                                          GROWE[:, q * 512:(q + 1) * 512], None))
                            dep2 = [rb2_tok, ge_dep]
                        else:
                            dep2 = []
                        tm = mmg(ctx, bk, terms, deps=[w2_tok, rh_last] + dep2)
                        ctx.slab_last["u1"] = tm
                        te = pb.comp("vector", lambda e2, bk=bk, oc=oc, q=q:
                                     e2.tensor_tensor(
                                         acc_ap(oc, q), acc_ap(oc, q),
                                         ctx.bank(bk), op=ALU.add),
                                     deps=[tm, base_last, prev_done])
                        ctx.bank_tok[bk] = te
                        st2_last = te
                prev_done = st2_last

        for q in range(2):
            dd = pb.dma("sync",
                        (lambda x1o=io["x1out"], mp=mp, q=q:
                         rearr_kpm(apv(x1o))[:, :, (2 * mp + q) * MT:
                                             (2 * mp + q + 1) * MT]),
                        X1ACC.rearrange("p (c q m) -> p c q m", c=8, q=2)[:, :, q, :],
                        deps=[prev_done], channel="x1o")
            x1_dmas.append(dd)
        x1_prev = x1_dmas[-1]
        mp_prev_done = prev_done

    return x1_dmas



def build_block0(ph1_only=False, stage=9):
    nc = bass.Bass("TRN2", target_bir_lowering=False, num_devices=8)
    io = {}

    def d_in(name, shape):
        io[name] = nc.dram_tensor(name, list(shape), F32, kind="ExternalInput").ap()

    def d_in_bf(name, shape):
        io[name] = nc.dram_tensor(name, list(shape), BF16, kind="ExternalInput").ap()

    d_in("xownT", [D, HALF])
    d_in("xqT", [D, HALF])
    d_in("xkvT", [D, HALF])
    d_in_bf("w_qkv", [D, 3 * D])
    d_in("b_qkv", [3 * D])
    d_in_bf("w_proj", [D, D])
    d_in("b_proj", [D])
    d_in("gate_w", [D, 4])
    d_in("gate_b", [4])
    d_in_bf("sw1", [2, D, 4096])
    d_in("sb1", [2, 4096])
    d_in_bf("sw2", [2, 4096, D])
    d_in("sb2", [2, D])
    d_in_bf("rw1", [4, D, 4096])
    d_in("rb1", [4, 4096])
    d_in_bf("rw2", [4, 4096, D])
    d_in("rb2", [4, D])
    d_in("bqkv_pc", [128, 24])
    d_in("sb1_pc", [2, 128, 32])
    for k in host_consts():
        d_in_bf(k, host_consts()[k].shape)
    io["sc_A"] = nc.dram_tensor("sc_A", [D, HALF], BF16, kind="Internal").ap()
    io["x1out"] = nc.dram_tensor("x1out", [D, HALF], F32, kind="ExternalOutput").ap()

    ctx = Ctx(nc)
    x1_dmas = emit_block0(ctx, io, ph1_only=ph1_only, stage=stage)
    if not x1_dmas:
        ctx.pb.barrier()
    ctx.pb.waits_only("sync", x1_dmas[-1:])
    ctx.pb.realize()
    return nc


def stage_block0_inputs(inputs, core):
    b, h = core // 2, core % 2
    own = h * HALF
    x0 = np.asarray(inputs["L0_attn_x"])
    qidx, kidx = host_gather_indices(np.asarray(inputs["L0_attn_mapping"]), own)
    cm = {k: np.ascontiguousarray(v) for k, v in host_consts().items()}
    m = dict(
        xownT=np.ascontiguousarray(x0[b, own:own + HALF, :].T),
        xqT=np.ascontiguousarray(x0[b, qidx, :].T),
        xkvT=np.ascontiguousarray(x0[b, kidx, :].T),
        w_qkv=np.asarray(inputs["L0_attn_Wqkv"]).astype(BF),
        b_qkv=np.asarray(inputs["L0_attn_bqkv"]),
        w_proj=np.asarray(inputs["L0_attn_Wproj"]).astype(BF),
        b_proj=np.asarray(inputs["L0_attn_bproj"]),
        gate_w=np.asarray(inputs["L0_ffn_gate_w"]),
        gate_b=np.asarray(inputs["L0_ffn_gate_b"]),
        sw1=np.asarray(inputs["L0_ffn_sw1"]).astype(BF),
        sb1=np.asarray(inputs["L0_ffn_sb1"]),
        sw2=np.asarray(inputs["L0_ffn_sw2"]).astype(BF),
        sb2=np.asarray(inputs["L0_ffn_sb2"]),
        rw1=np.asarray(inputs["L0_ffn_rw1"]).astype(BF),
        rb1=np.asarray(inputs["L0_ffn_rb1"]),
        rw2=np.asarray(inputs["L0_ffn_rw2"]).astype(BF),
        rb2=np.asarray(inputs["L0_ffn_rb2"]),
        bqkv_pc=np.ascontiguousarray(
            np.asarray(inputs["L0_attn_bqkv"]).reshape(24, 128).T),
        sb1_pc=np.ascontiguousarray(
            np.asarray(inputs["L0_ffn_sb1"]).reshape(2, 32, 128).transpose(0, 2, 1)),
        **cm,
    )
    return m


def run_block0(inputs):
    nc = build_block0()
    in_maps = [stage_block0_inputs(inputs, c) for c in range(8)]
    res = run_bass_kernel_spmd(nc, in_maps, core_ids=list(range(8)))
    x1 = np.zeros((B, N, D), dtype=np.float32)
    for c in range(8):
        b, h = c // 2, c % 2
        x1[b, h * HALF:(h + 1) * HALF, :] = res.results[c]["x1out"].T
    return x1


def run_block1(inputs, x1):
    nc = build_block1()
    in_maps = [stage_block1_inputs(inputs, c, x1) for c in range(8)]
    res = run_bass_kernel_spmd(nc, in_maps, core_ids=list(range(8)))
    out = np.zeros((B, N, D), dtype=np.float32)
    for c in range(8):
        b, h = c // 2, c % 2
        out[b, h * HALF:(h + 1) * HALF, :] = res.results[c]["out"].T
    return out


def build_fused():
    nc = bass.Bass("TRN2", target_bir_lowering=False, num_devices=8)
    io = {}

    def d_in(name, shape, dtype=F32):
        io[name] = nc.dram_tensor(name, list(shape), dtype,
                                  kind="ExternalInput").ap()

    for name, shape in [
            ("xownT", [D, HALF]), ("xqT", [D, HALF]), ("xkvT", [D, HALF]),
            ("b_qkv", [3 * D]),
            ("b_proj", [D]), ("gate_w", [D, 4]), ("gate_b", [4]),
            ("sb1", [2, 4096]),
            ("sb2", [2, D]), ("rb1", [4, 4096]),
            ("rb2", [4, D]),
            ("bqkv_pc", [128, 24]), ("sb1_pc", [2, 128, 32]),
            ("bg", [D]),
            ("bp", [D]), ("bg_pc", [128, 8]),
            ("bp_pc", [128, 8]), ("szero", [128, 1]),
            ("lng_pc", [2, 128, 8]), ("lnb_pc", [2, 128, 8]),
            ("b1p", [2, 5504]),
            ("b2f", [2, D]),
            ("qs_row", [2, 1, 512]), ("ks_row", [2, 1, 512]),
            ("rot", [4, 4096, 64])]:
        d_in(name, shape)
    for name, shape in [
            ("w_qkv", [D, 3 * D]), ("w_proj", [D, D]),
            ("sw1", [2, D, 4096]), ("sw2", [2, 4096, D]),
            ("rw1", [4, D, 4096]), ("rw2", [4, 4096, D]),
            ("w_qkv1", [D, 3 * D]), ("wg", [D, D]), ("wp", [D, D]),
            ("wqkv_f", [2, D, 1536]), ("wo_f", [2, 512, D]),
            ("w1p", [2, D, 5504]), ("w2p", [2, 2752, D])]:
        d_in(name, shape, BF16)
    d_in("masks", [4, 1024, 512], BF16)
    d_in("masksT", [4, 512, 1024], BF16)
    d_in("nonce", [1, 1], I32)
    for k, v in host_consts().items():
        d_in(k, v.shape, BF16)
    io["sc_A"] = nc.dram_tensor("sc_A", [D, HALF], BF16, kind="Internal").ap()
    for nm, shp, dt in [("sc_xn1b", [D, 4104], BF16), ("sc_kp", [4096, D], BF16),
                        ("sc_v", [4096, 1040], BF16), ("sc_s1", [D, HALF], F32),
                        ("sc_xa", [D, 3588], F32), ("sc_xab", [D, 3588], BF16),
                        ("sc_y1", [D, 3080], F32), ("sc_y1b", [D, 3080], BF16),
                        ("sc_xb", [D, 2564], F32), ("sc_xbb", [D, 2564], BF16)]:
        io[nm] = nc.dram_tensor(nm, shp, dt, kind="Internal").ap()
    io["out"] = nc.dram_tensor("out", [D, HALF], F32, kind="ExternalOutput").ap()
    x1loc = nc.dram_tensor("x1loc", [D, HALF], F32, kind="Internal").ap()
    x1prt = nc.dram_tensor("x1prt", [D, HALF], F32, kind="Internal").ap()
    x1sh = nc.dram_tensor("x1sh", [8, D, HALF], F32, kind="Internal",
                          addr_space="Shared").ap()
    flags = nc.dram_tensor("flags", [8, 16], I32, kind="Internal",
                           addr_space="Shared").ap()

    ctx = Ctx(nc)
    build_fused.last_ctx = ctx
    pb = ctx.pb
    pb.want_pid = True
    io["x1out"] = x1loc
    io["x1self"] = x1loc
    io["x1part"] = x1prt
    x1_dmas = emit_block0(ctx, io)

    NB = ctx.sb_at("NB", 206656, [1, 1], I32)
    PFLG = ctx.sb_at("PFLG", 206688, [1, 1], I32)
    exs = nc.alloc_semaphore("exs")
    expoll = nc.alloc_semaphore("expoll")

    def spin(sy):
        sy.dma_start(out=NB, in_=io["nonce"]).then_inc(exs, 16)
        sy.dma_start(out=x1sh[pb.pid], in_=x1loc).then_inc(exs, 16)
        sy.wait_ge(exs, 32)
        sy.dma_start(out=flags[pb.pid, 0:1].unsqueeze(0),
                     in_=NB).then_inc(exs, 16)
        sy.wait_ge(exs, 48)
        with (sy.register("rn") as rn, sy.register("rf") as rf,
              sy.register("rit") as rit, sy.register("rcont") as rcont,
              sy.register("rexp") as rexp, sy.register("rtmp") as rtmp):
            sy.load(rn, NB)
            sy.reg_mov(rit, 0)
            sy.reg_mov(rcont, 1)
            sy.reg_mov(rexp, 0)
            with sy.While(rcont):
                sy.reg_add(rexp, rexp, 16)
                sy.dma_start(out=PFLG, in_=flags[pb.ppid, 0:1].unsqueeze(0)
                             ).then_inc(expoll, 16)
                sy.wait_ge(expoll, rexp)
                sy.load(rf, PFLG)
                sy.reg_add(rit, rit, 1)
                sy.reg_alu(rcont, rf, rn, ALU.not_equal)
                sy.reg_alu(rtmp, rit, 200000, ALU.is_lt)
                sy.reg_alu(rcont, rcont, rtmp, ALU.bitwise_and)
        sy.dma_start(out=x1prt, in_=x1sh[pb.ppid]).then_inc(exs, 16)
        sy.wait_ge(exs, 64)
    pb.raw("sync", spin, deps=x1_dmas)

    out_dmas = emit_block1(ctx, io, stage=19)
    pb.waits_only("sync", out_dmas[-1:])
    pb.realize()
    return nc


def stage_fused_inputs(inputs, core, nonce):
    m = stage_block0_inputs(inputs, core)
    m.update(stage_block1_inputs(inputs, core, None))
    m["nonce"] = nonce
    return m


def kernel(**inputs):
    nc = build_fused()
    nonce = np.array([[int(np.random.default_rng().integers(1 << 20, 1 << 30))]],
                     dtype=np.int32)
    in_maps = [stage_fused_inputs(inputs, c, nonce) for c in range(8)]
    res = run_bass_kernel_spmd(nc, in_maps, core_ids=list(range(8)))
    out = np.zeros((B, N, D), dtype=np.float32)
    for c in range(8):
        b, h = c // 2, c % 2
        out[b, h * HALF:(h + 1) * HALF, :] = res.results[c]["out"].T
    return out


TOFF = 2049
GSPAN = 2560
XA_OFF = 1536
Y1_OFF = 1028
XB_OFF = 512

SB2 = dict(
    QTH=0,
    KTH=40960,
    VTOK=81920,
    WQS=102400,
    WKS=110592,
    WVS=118784,
    YT=126976,
    EB=135296,
    MSK2=143488,
    OT=145536,
    OTMP=149632,
    M2=150656,
    SCR=167040,
)



def emit_block1(ctx, io, stage=19):
    nc, pb = ctx.nc, ctx.pb
    sa = ctx.sb_at
    QTH = sa("QTH", SB2["QTH"], [64, 8 * GSPAN], BF16)
    KTH = sa("KTH", SB2["KTH"], [64, 8 * GSPAN], BF16)
    VTOK = sa("VTOK", SB2["VTOK"], [128, 20 * 512], BF16)
    WQS = sa("WQS", SB2["WQS"], [128, 4096], BF16)
    WKS = sa("WKS", SB2["WKS"], [128, 4096], BF16)
    WVS = sa("WVS", SB2["WVS"], [128, 4096], BF16)
    YT = sa("YT", SB2["YT"], [128, 4160], BF16)
    EB = sa("EB", SB2["EB"], [128, 4096], BF16)
    MSK2 = sa("MSK2", SB2["MSK2"], [128, 1024], BF16)
    OT = sa("OT", SB2["OT"], [128, 2048], BF16)
    OTMP = sa("OTMP", SB2["OTMP"], [64, 512], BF16)
    MSKT = sa("MSKT", 202560, [128, 2048], BF16)
    RAW2 = sa("RAW2", SB2["VTOK"], [128, 4096], F32)
    RAW2F = sa("RAW2F", SB2["QTH"], [128, 4104], F32)
    HP = sa("HP", SB2["QTH"] + 16448, [128, 11264], BF16)
    S1T = sa("S1T", SB2["KTH"], [128, 4096], F32)
    XNown = sa("XNown", SB2["QTH"], [128, 16384], BF16)
    QP = sa("QP", SB2["KTH"], [128, 16384], BF16)
    m = SB2["M2"]

    def mget(name, shape, dtype):
        nonlocal m
        nbytes = int(np.prod(shape[1:])) * mybir.dt.size(dtype)
        ap = sa(name, m, shape, dtype)
        m += (nbytes + 31) // 32 * 32
        assert m <= 167040, (name, m)
        return ap

    KVS = mget("KVS", [128, 2048], BF16)
    scr = SB2["SCR"]
    GT2 = sa("GT2", scr, [128, 512], F32)
    QTMP = sa("QTMP", scr + 2048, [128, 512], F32)
    ETMP = sa("ETMP", scr + 4096, [128, 512], F32)
    RNR = sa("RNR", scr + 6144, [128, 512], F32)
    VPB = sa("VPB", scr + 8192, [128, 1040], BF16)
    KPB = sa("KPB", scr + 10272, [128, 512], BF16)
    RNB = sa("RNB", scr + 11296, [128, 512], BF16)
    QON = sa("QON", SB2["EB"], [128, 4096], BF16)
    ONESF = mget("ONESF", [128, 128], BF16)
    S2R = mget("S2R", [128, 16], F32)
    BQS = mget("BQS", [128, 1024], BF16)
    MXR = mget("MXR", [1, 512], BF16)
    NMC = mget("NMC", [128, 32], F32)
    NMCB = mget("NMCB", [128, 32], BF16)
    ROT = mget("ROT", [128, 256], F32)
    GPC = mget("GPC", [128, 8], F32)
    BPC = mget("BPC", [128, 8], F32)
    LNG = mget("LNG", [128, 16], F32)
    LNB = mget("LNB", [128, 16], F32)
    B2R = mget("B2R", [1, 1024], BF16)
    B1R = mget("B1R", [1, 128], BF16)
    bgr_off = m
    BGR = mget("BGR", [1, 1024], BF16)
    RESF = sa("RESF", bgr_off, [128, 512], F32)
    m = bgr_off + 2048
    QSR = mget("QSR", [1, 1024], BF16)
    SZC = mget("SZC", [128, 1], F32)
    ZC8 = mget("ZC8", [128, 8], BF16)
    TR1 = mget("TR1", [128, 128], BF16)

    ones_row = ctx.sb["ONES1x128"]
    onesf_col = ctx.sb["ONES128x1F"]
    ones_col = ctx.sb["ONES128x1"]
    ident = ctx.sb["IDENT"]
    ones512 = ctx.sb["ONES1x512"]

    pb.barrier()
    t_in = []
    t_in.append(pb.comp("gpsimd", lambda e: e.memset(ONESF, 1.0)))
    t_in.append(pb.comp("gpsimd", lambda e: e.memset(ZC8, 0.0)))
    t_in.append(pb.dma("gpsimd", SZC, io["szero"], channel="c2"))
    t_in.append(pb.dma("gpsimd", GPC, io["bg_pc"], channel="c2"))
    t_in.append(pb.dma("gpsimd", BPC, io["bp_pc"], channel="c2"))
    t_in.append(pb.dma("gpsimd", LNG.rearrange("p (l c) -> p l c", l=2),
                       io["lng_pc"].rearrange("l p c -> p l c"), channel="c2"))
    t_in.append(pb.dma("gpsimd", LNB.rearrange("p (l c) -> p l c", l=2),
                       io["lnb_pc"].rearrange("l p c -> p l c"), channel="c2"))
    t_in.append(pb.dma("gpsimd", BGR, io["bg"].unsqueeze(0), channel="c2"))
    pb.barrier()

    XN1B = io["sc_xn1b"]

    pb.phase = "b1.ph3a"
    lnfree = []
    ln_last = None
    for k in range(8):
        c0 = 1 + k * 512
        src = io["x1part"] if k < 4 else io["x1self"]
        s0 = (c0 - 1) % 2048
        dep = pb.dma("sync", RAW2.rearrange("p (c m) -> p c m", c=8),
                     (lambda src=src, s0=s0:
                      rearr_kpm(apv(src))[:, :, s0:s0 + 512]),
                     deps=lnfree + [ln_last],
                     channel="rawf")
        t_xn, lnfree = emit_ln(ctx, lambda c: RAW2[:, c * MT:(c + 1) * MT], dep,
                               lambda c: YT[:, c * MT:(c + 1) * MT])
        ln_last = pb.dma("sync", rearr_kpm(XN1B)[:, :, c0:c0 + 512],
                         YT[:, 0:4096].rearrange("p (c m) -> p c m", c=8),
                         deps=[t_xn, ln_last], channel="xn1o")
    z0 = pb.dma("sync", rearr_kpm(XN1B)[:, :, 0:1],
                ZC8.rearrange("p (c m) -> p c m", m=1), deps=[ln_last],
                channel="zc", allow_slow_non_contiguous=True)
    xn1_done = [z0, ln_last]
    pb.barrier()
    if stage <= 11:
        return []

    pb.phase = "b1.ph3b"
    vones = pb.comp("gpsimd", lambda e: e.memset(VPB, 1.0))
    kvz = pb.comp("gpsimd", lambda e: e.memset(KVS, 0.0))
    slabs_tok = None
    xt_dep = None
    kp_dma = None
    for nh in range(2):
        d0 = [slabs_tok]
        tw_g = pb.dma("gpsimd", WQS, rearr_kpm(io["wg"])[:, :, nh * 512:(nh + 1) * 512],
                      deps=d0, channel="sla")
        tw_k = pb.dma("gpsimd", WKS,
                      rearr_kpm(io["w_qkv1"])[:, :, 1024 + nh * 512:1536 + nh * 512],
                      deps=d0, channel="slb")
        tw_v = pb.dma("gpsimd", WVS,
                      rearr_kpm(io["w_qkv1"])[:, :, 2048 + nh * 512:2560 + nh * 512],
                      deps=d0, channel="u1")
        xt_tok = None
        for k8 in range(8):
            c0 = 1 + k8 * 512
            xt_tok = pb.dma("sync", YT[:, 0:4096].rearrange("p (c m) -> p c m", c=8),
                            rearr_kpm(XN1B)[:, :, c0:c0 + 512],
                            deps=[xt_dep] + xn1_done, channel="rawf")
            for mc in range(4):
                l0 = k8 * 512 + mc * 128
                lt = lambda kc, mc=mc: YT[:, kc * 512 + mc * 128:kc * 512 + (mc + 1) * 128]
                tg = mmg(ctx, 0, [(ctx.bank(0), lt(kc), WQS[:, kc * 512:(kc + 1) * 512],
                                   None) for kc in range(NCH)] +
                         [(ctx.bank(0), ones_row, BGR[:, nh * 512:(nh + 1) * 512], None)],
                         deps=[tw_g, xt_tok])
                tk = mmg(ctx, 1, [(ctx.bank(1), lt(kc), WKS[:, kc * 512:(kc + 1) * 512],
                                   None) for kc in range(NCH)],
                         deps=[tw_k, xt_tok])
                tv = mmg(ctx, 2, [(ctx.bank(2), lt(kc), WVS[:, kc * 512:(kc + 1) * 512],
                                   None) for kc in range(NCH)],
                         deps=[tw_v, xt_tok])
                xt_dep = tv
                eg = pb.comp("scalar", lambda e: e.activation(GT2, ctx.bank(0),
                                                              AF.Sigmoid),
                             deps=[tg, kp_dma])
                ctx.bank_tok[0] = eg
                e1 = pb.comp("vector", lambda e: e.tensor_tensor(
                    QTMP, ctx.bank(1), GT2, op=ALU.mult), deps=[tk, eg])
                ctx.bank_tok[1] = e1
                e2 = pb.comp("vector", lambda e: e.tensor_scalar_min(
                    ETMP, QTMP, 0.0), deps=[e1])
                e3 = pb.comp("scalar", lambda e: e.activation(ETMP, ETMP, AF.Exp),
                             deps=[e2])
                e4 = pb.comp("vector", lambda e: e.tensor_scalar_max(
                    QTMP, QTMP, 0.0), deps=[e3, e1])
                e5 = pb.comp("vector", lambda e: e.tensor_add(KPB, ETMP, QTMP),
                             deps=[e4, e3])
                ev = pb.comp("vector", lambda e, nh=nh: e.tensor_copy(
                    VPB[:, nh * 520:(nh + 1) * 520]
                    .rearrange("p (h c) -> p h c", c=65)[:, :, 0:64],
                    ctx.bank(2).rearrange("p (h c) -> p h c", c=64)),
                    deps=[tv, kp_dma])
                ctx.bank_tok[2] = ev
                kp_dma = pb.dma("sync", io["sc_kp"][l0:l0 + 128,
                                                    nh * 512:(nh + 1) * 512],
                                KPB, deps=[e5], channel="kpo")
                kp_dma = pb.dma("sync", io["sc_v"][l0:l0 + 128,
                                                   nh * 520:(nh + 1) * 520],
                                VPB[:, nh * 520:(nh + 1) * 520],
                                deps=[ev, vones], channel="kpo")
        slabs_tok = kp_dma
    kv_evs = None
    for hg in range(4):
        t_ld = None
        kv_toks = {}
        for chunk in range(32):
            l0 = chunk * 128
            tka = pb.dma("sync", YT[:, 0:1024], io["sc_kp"][l0:l0 + 128, :],
                         deps=[t_ld, kp_dma, kv_evs, xt_dep], channel="rawf")
            tvb = pb.dma("sync", YT[:, 1024:2064],
                         io["sc_v"][l0:l0 + 128, :],
                         deps=[t_ld, kp_dma, kv_evs, xt_dep], channel="rawf")
            for hl in range(4):
                h = hg * 4 + hl
                bk = 4 + hl
                kv_toks[hl] = mmg(
                    ctx, bk,
                    [(ctx.bank(bk)[0:64, 0:65],
                      YT[:, h * 64:(h + 1) * 64],
                      YT[:, 1024 + h * 65:1024 + (h + 1) * 65], None)],
                    deps=[tvb], cont=(chunk > 0))
            t_ld = kv_toks[3]
        evs = []
        for hl in range(4):
            h = hg * 4 + hl
            evs.append(pb.comp("scalar", lambda e, h=h, hl=hl: e.copy(
                KVS[0:64, h * 128:h * 128 + 65],
                ctx.bank(4 + hl)[0:64, 0:65]), deps=[kv_toks[hl]]))
            ctx.bank_tok[4 + hl] = evs[-1]
        kv_evs = evs[-1]
    kvsw = pb.dma("sync", KVS[64:128, :], KVS[0:64, :], deps=[kv_evs, kvz],
                  channel="zc")
    pb.barrier()
    if stage <= 12:
        return []

    pb.phase = "b1.ph3c"
    xo_tok = pb.dma("sync", XNown.rearrange("p (c m) -> p c m", c=8),
                    rearr_kpm(XN1B)[:, :, TOFF:TOFF + 2048], channel="rawf")
    s1_dmas = []
    prev_f5 = None
    a1_last = None
    for mi in range(NMT):
        for ni in range(NCH):
            sl = (ctx.sb["SLA"], ctx.sb["SLB"])[ni % 2]
            ch = ("sla", "slb")[ni % 2]
            tldq = pb.dma("gpsimd", sl,
                          rearr_kpm(io["w_qkv1"])[:, :, ni * 128:(ni + 1) * 128],
                          deps=[ctx.slab_last[ch]], channel=ch)
            bk = ni % 2
            tmq = mmg(ctx, bk, [(ctx.bank(bk),
                                 sl[:, kc * 128:(kc + 1) * 128],
                                 XNown[:, kc * 2048 + mi * MT:kc * 2048 + (mi + 1) * MT],
                                 None) for kc in range(NCH)],
                      deps=[tldq, xo_tok])
            eq = pb.comp("scalar", lambda e, bk=bk: e.copy(QTMP, ctx.bank(bk)),
                         deps=[tmq, prev_f5,
                               s1_dmas[-1] if s1_dmas else None])
            ctx.bank_tok[bk] = eq
            tldg = pb.dma("gpsimd", sl,
                          rearr_kpm(io["wg"])[:, :, ni * 128:(ni + 1) * 128],
                          deps=[tmq], channel=ch)
            tmg = mmg(ctx, bk, [(ctx.bank(bk),
                                 sl[:, kc * 128:(kc + 1) * 128],
                                 XNown[:, kc * 2048 + mi * MT:kc * 2048 + (mi + 1) * MT],
                                 None) for kc in range(NCH)],
                      deps=[tldg, eq])
            ctx.slab_last[ch] = tmg
            gg = pb.comp("scalar", lambda e, bk=bk, ni=ni: e.activation(
                GT2, ctx.bank(bk), AF.Sigmoid, bias=GPC[:, ni:ni + 1]),
                deps=[tmg, prev_f5])
            ctx.bank_tok[bk] = gg
            f1 = pb.comp("vector", lambda e: e.tensor_mul(QTMP, QTMP, GT2),
                         deps=[gg, eq])
            f2 = pb.comp("vector", lambda e: e.tensor_scalar_min(ETMP, QTMP, 0.0),
                         deps=[f1])
            f3 = pb.comp("scalar", lambda e: e.activation(ETMP, ETMP, AF.Exp),
                         deps=[f2])
            f4 = pb.comp("vector", lambda e: e.tensor_scalar_max(QTMP, QTMP, 0.0),
                         deps=[f3, f1])
            f5 = pb.comp("vector", lambda e, ni=ni, mi=mi: e.tensor_add(
                QP[:, ni * 2048 + mi * MT:ni * 2048 + (mi + 1) * MT], ETMP, QTMP),
                deps=[f4])
            prev_f5 = f5
        qp_tok = f5
        prev_qon = None
        for hp in range(NCH):
            b_qo = 2
            b_nm = 3
            terms = []
            for h2 in range(2):
                h = 2 * hp + h2
                b0 = h2 * 64
                terms.append((ctx.bank(b_qo)[b0:b0 + 64, 0:512],
                              KVS[b0:b0 + 64, h * 128:h * 128 + 64],
                              QP[b0:b0 + 64, hp * 2048 + mi * MT:hp * 2048 + (mi + 1) * MT],
                              (b0, b0)))
            t_qo = mmg(ctx, b_qo, terms, deps=[qp_tok, kvsw], independent=True)
            nterms = []
            for h2 in range(2):
                h = 2 * hp + h2
                b0 = h2 * 64
                nterms.append((ctx.bank(b_nm)[b0:b0 + 1, 0:512],
                               KVS[b0:b0 + 64, h * 128 + 64:h * 128 + 65],
                               QP[b0:b0 + 64, hp * 2048 + mi * MT:hp * 2048 + (mi + 1) * MT],
                               (b0, b0)))
            t_nm = mmg(ctx, b_nm, nterms, deps=[qp_tok, kvsw], independent=True)
            r1 = pb.comp("vector", lambda e, b_nm=b_nm: e.tensor_scalar_add(
                RNR[0:1, :], ctx.bank(b_nm)[0:1, 0:512], 1e-8), deps=[t_nm])
            r2 = pb.comp("vector", lambda e, b_nm=b_nm: e.tensor_scalar_add(
                RNR[64:65, :], ctx.bank(b_nm)[64:65, 0:512], 1e-8), deps=[t_nm])
            ctx.bank_tok[b_nm] = r2
            r3 = pb.comp("vector", lambda e: e.reciprocal(RNR[0:1, :], RNR[0:1, :]),
                         deps=[r1])
            r4 = pb.comp("vector", lambda e: e.reciprocal(RNR[64:65, :], RNR[64:65, :]),
                         deps=[r2])
            r5 = pb.comp("vector", lambda e: e.tensor_copy(RNB[0:1, :], RNR[0:1, :]),
                         deps=[r3])
            r6 = pb.comp("vector", lambda e: e.tensor_copy(RNB[64:65, :], RNR[64:65, :]),
                         deps=[r4])
            bterms = [(ctx.bank(b_nm)[0:64, 0:512], ONESF[0:1, 0:64], RNB[0:1, :],
                       (0, 0)),
                      (ctx.bank(b_nm)[64:128, 0:512], ONESF[64:65, 0:64],
                       RNB[64:65, :], (64, 64))]
            t_br = mmg(ctx, b_nm, bterms, deps=[r5, r6], independent=True)
            qoc = pb.comp("scalar", lambda e, b_qo=b_qo: e.copy(QTMP, ctx.bank(b_qo)),
                          deps=[t_qo, prev_qon])
            ctx.bank_tok[b_qo] = qoc
            qon = pb.comp("vector", lambda e, hp=hp, b_nm=b_nm: e.tensor_tensor(
                QON[:, hp * 512:(hp + 1) * 512], QTMP, ctx.bank(b_nm), op=ALU.mult),
                deps=[qoc, t_br, a1_last])
            ctx.bank_tok[b_nm] = qon
            prev_qon = qon
        x1_dep = pb.dma("sync", RAW2.rearrange("p (c m) -> p c m", c=8),
                        (lambda xs=io["x1self"], mi=mi:
                         rearr_kpm(apv(xs))[:, :, mi * MT:(mi + 1) * MT]),
                        deps=[xo_tok, s1_dmas[-1] if s1_dmas else None],
                        channel="raw2f")
        for oc in range(NCH):
            sl = (ctx.sb["SLA"], ctx.sb["SLB"])[oc % 2]
            ch = ("sla", "slb")[oc % 2]
            tld = pb.dma("gpsimd", sl,
                         rearr_kpm(io["wp"])[:, :, oc * 128:(oc + 1) * 128],
                         deps=[ctx.slab_last[ch]], channel=ch)
            bk = 6 + oc % 2
            terms = [(ctx.bank(bk), sl[:, kc * 128:(kc + 1) * 128],
                      QON[:, kc * 512:(kc + 1) * 512], None) for kc in range(NCH)]
            tm = mmg(ctx, bk, terms, deps=[tld, qon])
            ctx.slab_last[ch] = tm
            a1_last = tm
            te = pb.comp("vector", lambda e, bk=bk, oc=oc: e.scalar_tensor_tensor(
                ETMP, ctx.bank(bk), 1.0, RAW2[:, oc * MT:(oc + 1) * MT],
                op0=ALU.mult, op1=ALU.add),
                deps=[tm, x1_dep, s1_dmas[-1] if s1_dmas else None])
            tb = pb.comp("vector", lambda e, oc=oc: e.tensor_scalar_add(
                ETMP, ETMP, BPC[:, oc:oc + 1]), deps=[te])
            ctx.bank_tok[bk] = tb
            dd = pb.dma("sync",
                        rearr_kpm(io["sc_s1"])[:, oc, mi * MT:(mi + 1) * MT],
                        ETMP, deps=[tb], channel="s1o")
            s1_dmas.append(dd)
    pb.barrier()
    if stage <= 13:
        return []

    envs = [
        dict(ysrc_b=XN1B, ysrc_f=None, yoff=TOFF,
             groups=[(-2048, 512, [-3, -2, -1, 0]), (0, 2048, [1, 2, 3])],
             wo_f32=io["sc_xa"], wo_b16=io["sc_xab"], wo_off=XA_OFF,
             ffn_tiles=[(-1025, -513), (-513, -1), (-1, 511), (511, 1023),
                        (1023, 1535), (1535, 2047), (2047, 2048)],
             ffn_f32=io["sc_y1"], ffn_b16=io["sc_y1b"], fo_off=Y1_OFF,
             final=False, wq_free=None, ysrc_ready=None),
        dict(ysrc_b=io["sc_y1b"], ysrc_f=io["sc_y1"], yoff=Y1_OFF,
             groups=[(-1024, 512, [-1, 0]), (0, 2048, [1, 2, 3])],
             wo_f32=io["sc_xb"], wo_b16=io["sc_xbb"], wo_off=XB_OFF,
             ffn_tiles=[(0, 512), (512, 1024), (1024, 1536), (1536, 2048)],
             final=True, wq_free=None, ysrc_ready=None),
    ]
    out_dmas = None
    if stage == 131:
        envs[0]["qkv_only"] = True
        emit_l1ffn_layer(ctx, io, 0, envs[0])
        pb.barrier()
        return []
    for l in (0, 1):
        if stage <= 13 + l:
            return []
        out_dmas = emit_l1ffn_layer(ctx, io, l, envs[l])
        envs[1]["wq_free"] = (ctx.pb.psem["tensor"], ctx.pb.cnt["tensor"])
        pb.barrier()
    return out_dmas


def build_block1(stage=19):
    nc = bass.Bass("TRN2", target_bir_lowering=False, num_devices=8)
    io = {}

    def d_in(name, shape, dtype=F32):
        io[name] = nc.dram_tensor(name, list(shape), dtype, kind="ExternalInput").ap()

    d_in("x1self", [D, HALF])
    d_in("x1part", [D, HALF])
    d_in("w_qkv1", [D, 3 * D], BF16)
    d_in("wg", [D, D], BF16)
    d_in("bg", [D])
    d_in("wp", [D, D], BF16)
    d_in("bp", [D])
    d_in("bg_pc", [128, 8])
    d_in("bp_pc", [128, 8])
    d_in("szero", [128, 1])
    d_in("lng_pc", [2, 128, 8])
    d_in("lnb_pc", [2, 128, 8])
    d_in("wqkv_f", [2, D, 1536], BF16)
    d_in("wo_f", [2, 512, D], BF16)
    d_in("w1p", [2, D, 5504], BF16)
    d_in("b1p", [2, 5504])
    d_in("w2p", [2, 2752, D], BF16)
    d_in("b2f", [2, D])
    d_in("qs_row", [2, 1, 512])
    d_in("ks_row", [2, 1, 512])
    d_in("rot", [4, 4096, 64])
    d_in("masks", [4, 1024, 512], BF16)
    d_in("masksT", [4, 512, 1024], BF16)
    for k, v in host_consts().items():
        d_in(k, v.shape, BF16)
    io["sc_xn1b"] = nc.dram_tensor("sc_xn1b", [D, 4104], BF16, kind="Internal").ap()
    io["sc_kp"] = nc.dram_tensor("sc_kp", [4096, D], BF16, kind="Internal").ap()
    io["sc_v"] = nc.dram_tensor("sc_v", [4096, 1040], BF16, kind="Internal").ap()
    io["sc_s1"] = nc.dram_tensor("sc_s1", [D, HALF], F32, kind="Internal").ap()
    io["sc_xa"] = nc.dram_tensor("sc_xa", [D, 3588], F32, kind="Internal").ap()
    io["sc_xab"] = nc.dram_tensor("sc_xab", [D, 3588], BF16, kind="Internal").ap()
    io["sc_y1"] = nc.dram_tensor("sc_y1", [D, 3080], F32, kind="Internal").ap()
    io["sc_y1b"] = nc.dram_tensor("sc_y1b", [D, 3080], BF16, kind="Internal").ap()
    io["sc_xb"] = nc.dram_tensor("sc_xb", [D, 2564], F32, kind="Internal").ap()
    io["sc_xbb"] = nc.dram_tensor("sc_xbb", [D, 2564], BF16, kind="Internal").ap()
    io["out"] = nc.dram_tensor("out", [D, HALF], F32, kind="ExternalOutput").ap()
    if stage <= 13:
        io["dbg_s1"] = nc.dram_tensor("dbg_s1", [D, HALF], F32,
                                      kind="ExternalOutput").ap()
        io["sc_s1"] = io["dbg_s1"]

    ctx = Ctx(nc)
    sa = ctx.sb_at
    IDENT = sa("IDENT", SBMAP["IDENT"], [128, 128], BF16)
    ONES1x128 = sa("ONES1x128", SBMAP["ONES1x128"], [1, 128], BF16)
    ONES1x512 = sa("ONES1x512", SBMAP["ONES1x512"], [1, 512], BF16)
    sa("ONES128x1", SBMAP["ONES128x1"], [128, 1], BF16)
    sa("ONES128x1F", SBMAP["ONES128x1F"], [128, 1], F32)
    sa("SLA", SBMAP["SLA"], [128, 1024], BF16)
    sa("SLB", SBMAP["SLB"], [128, 1024], BF16)
    pb = ctx.pb
    pb.comp("gpsimd", lambda e: e.memset(ctx.sb["ONES128x1"], 1.0))
    pb.comp("gpsimd", lambda e: e.memset(ctx.sb["ONES128x1F"], 1.0))
    pb.dma("gpsimd", IDENT, io["cn_ident"], channel="const")
    pb.dma("gpsimd", ONES1x128, io["cn_ones_1x128"], channel="const")
    pb.dma("gpsimd", ONES1x512, io["cn_ones_1x512"], channel="const")
    out_dmas = emit_block1(ctx, io, stage=stage)
    if out_dmas:
        ctx.pb.waits_only("sync", out_dmas[-1:])
    else:
        ctx.pb.barrier()
    ctx.pb.realize()
    return nc


def host_rotary(own_start):
    t = (np.arange(-2048, 2048) + own_start) % N
    tt = t.astype(np.float64)
    half_idx = np.arange(0, 64, 2, dtype=np.float64)
    inv_freq = 1.0 / 10000.0 ** (half_idx / 64)
    base = (half_idx + 0.4 * 64) / (1.4 * 64)
    power = (tt - N // 2) / float(WINDOW // 2)
    fr = tt[:, None] * inv_freq[None, :]
    sc = base[None, :] ** power[:, None]
    cos = np.cos(np.concatenate([fr, fr], 1))
    sin = np.sin(np.concatenate([fr, fr], 1))
    scale = np.concatenate([sc, sc], 1)
    sgn = np.concatenate([-np.ones((1, 32)), np.ones((1, 32))], 1)
    return np.stack([cos * scale, sin * scale * sgn,
                     cos / scale, sin / scale * sgn]).astype(np.float32)


def host_l1_masks(is_second_half):
    sarr = np.arange(WINDOW)[:, None]
    marr = np.arange(WINDOW)[None, :]
    prev = np.where(sarr >= marr, 0.0, NEG).astype(np.float32)
    curr = np.where(sarr <= marr, 0.0, NEG).astype(np.float32)
    normal = np.concatenate([prev, curr], axis=0)
    masked0 = np.concatenate([np.full((WINDOW, WINDOW), NEG, np.float32), curr],
                             axis=0)
    out = np.stack([normal] * 4, axis=0)
    if not is_second_half:
        out[0] = masked0
    return out.astype(BF)


def host_l1_masksT(is_second_half):
    m = np.asarray(host_l1_masks(is_second_half), np.float32)
    return np.ascontiguousarray(m.transpose(0, 2, 1)).astype(BF)


def host_geglu_permute(w1, b1):
    L = w1.shape[0]
    w1p = np.zeros((L, 1024, 5504), w1.dtype)
    b1p = np.zeros((L, 5504), b1.dtype)
    c, fc = 0, 0
    while c < HID:
        wdt = min(64, HID - c)
        w1p[:, :, fc * 128:fc * 128 + wdt] = w1[:, :, c:c + wdt]
        w1p[:, :, fc * 128 + 64:fc * 128 + 64 + wdt] = w1[:, :, HID + c:HID + c + wdt]
        b1p[:, fc * 128:fc * 128 + wdt] = b1[:, c:c + wdt]
        b1p[:, fc * 128 + 64:fc * 128 + 64 + wdt] = b1[:, HID + c:HID + c + wdt]
        c += 64
        fc += 1
    return w1p, b1p


def host_w2_pad(w2):
    L = w2.shape[0]
    w2p = np.zeros((L, 2752, 1024), w2.dtype)
    c, fc = 0, 0
    while c < HID:
        wdt = min(64, HID - c)
        w2p[:, fc * 64:fc * 64 + wdt] = w2[:, c:c + wdt]
        c += 64
        fc += 1
    return w2p


def stage_block1_inputs(inputs, core, x1):
    b, h = core // 2, core % 2
    own = h * HALF
    cm = {k: np.ascontiguousarray(v) for k, v in host_consts().items()}
    w1p, b1p = host_geglu_permute(np.asarray(inputs["L1_ffn_w1"]),
                                  np.asarray(inputs["L1_ffn_b1"]))
    qs = np.asarray(inputs["L1_ffn_q_scale"])
    ks = np.asarray(inputs["L1_ffn_k_scale"])
    m = dict(
        w_qkv1=np.asarray(inputs["L1_attn_Wqkv"]).astype(BF),
        wg=np.asarray(inputs["L1_attn_Wg"]).astype(BF),
        bg=np.asarray(inputs["L1_attn_bg"]),
        wp=np.asarray(inputs["L1_attn_Wp"]).astype(BF),
        bp=np.asarray(inputs["L1_attn_bp"]),
        bg_pc=np.ascontiguousarray(np.asarray(inputs["L1_attn_bg"]).reshape(8, 128).T),
        bp_pc=np.ascontiguousarray(np.asarray(inputs["L1_attn_bp"]).reshape(8, 128).T),
        szero=np.full((128, 1), 0.0 if h == 0 else 1.0, np.float32),
        lng_pc=np.ascontiguousarray(np.asarray(inputs["L1_ffn_ln_g"])
                                    .reshape(2, 8, 128).transpose(0, 2, 1)),
        lnb_pc=np.ascontiguousarray(np.asarray(inputs["L1_ffn_ln_b"])
                                    .reshape(2, 8, 128).transpose(0, 2, 1)),
        wqkv_f=np.asarray(inputs["L1_ffn_wqkv"]).astype(BF),
        wo_f=np.asarray(inputs["L1_ffn_wo"]).astype(BF),
        w1p=w1p.astype(BF),
        b1p=b1p,
        w2p=host_w2_pad(np.asarray(inputs["L1_ffn_w2"])).astype(BF),
        b2f=np.asarray(inputs["L1_ffn_b2"]),
        qs_row=np.ascontiguousarray(
            np.tile(qs, (1, 8)).reshape(2, 1, 512) * QK_SCALE),
        ks_row=np.ascontiguousarray(np.tile(ks, (1, 8)).reshape(2, 1, 512)),
        rot=host_rotary(own),
        masks=host_l1_masks(h == 1),
        masksT=host_l1_masksT(h == 1),
        **cm,
    )
    if x1 is not None:
        m["x1self"] = np.ascontiguousarray(x1[b, own:own + HALF, :].T)
        m["x1part"] = np.ascontiguousarray(
            x1[b, (own + HALF) % N:(own + HALF) % N + HALF, :].T)
    return m


def emit_l1ffn_layer(ctx, io, l, env):
    nc, pb = ctx.nc, ctx.pb
    sb = ctx.sb
    QTH, KTH, VTOK = sb["QTH"], sb["KTH"], sb["VTOK"]
    WQS, WKS, WVS = sb["WQS"], sb["WKS"], sb["WVS"]
    YT, EB, MSK2, OT, OTMP = sb["YT"], sb["EB"], sb["MSK2"], sb["OT"], sb["OTMP"]
    MSKT = sb["MSKT"]
    HP, RAW2F, S1T = sb["HP"], sb["RAW2F"], sb["S1T"]
    GT2, QTMP, ETMP, RNR, RNB = sb["GT2"], sb["QTMP"], sb["ETMP"], sb["RNR"], sb["RNB"]
    RESF = sb["RESF"]
    KPB, VPB = sb["KPB"], sb["VPB"]
    S2R, BQS, ROT = sb["S2R"], sb["BQS"], sb["ROT"]
    LNG, LNB, B2R, B1R, SZC = sb["LNG"], sb["LNB"], sb["B2R"], sb["B1R"], sb["SZC"]
    MXRrow = sb["RNB"]
    ones_row = sb["ONES1x128"]
    ones_col = sb["ONES128x1"]
    ones512 = sb["ONES1x512"]
    ident = sb["IDENT"]
    TMP128 = sb["TR1"]
    NMC = sb["NMC"]
    NMCB = sb["NMCB"]
    MXR = sb["MXR"]

    ysrc = env["ysrc_b"]
    yoff = env["yoff"]
    ysrc_f = env["ysrc_f"]

    t_qsr = pb.dma("gpsimd", sb["QSR"][0:1, 0:512], io["qs_row"][l], channel="c2")
    t_ksr = pb.dma("gpsimd", sb["QSR"][0:1, 512:1024], io["ks_row"][l],
                   channel="c2")
    t_b2 = pb.dma("gpsimd", B2R, io["b2f"][l].unsqueeze(0), channel="bv")
    bq1 = mmg(ctx, 6, [(ctx.bank(6), ones_row, sb["QSR"][0:1, 0:512], None)],
              deps=[t_ksr])
    cb1 = pb.comp("vector", lambda e: e.tensor_copy(BQS[:, 0:512], ctx.bank(6)),
                  deps=[bq1])
    ctx.bank_tok[6] = cb1
    bq2 = mmg(ctx, 6, [(ctx.bank(6), ones_row, sb["QSR"][0:1, 512:1024], None)],
              deps=[t_ksr])
    cb2 = pb.comp("vector", lambda e: e.tensor_copy(BQS[:, 512:1024], ctx.bank(6)),
                  deps=[bq2])
    ctx.bank_tok[6] = cb2
    pb.barrier()

    wo_last = None
    for (t0, t1, wlist) in env["groups"]:
        for eng in ("vector", "scalar", "tensor"):
            pb.waits_only(eng, [env.get("xao_last"), env.get("res_last")])
        ntile = (t1 - t0) // 512
        wdeps = [env.get("wq_free"), env.get("msk_free")]
        env["wq_free"] = None
        t_wq = pb.dma("gpsimd", WQS, rearr_kpm(io["wqkv_f"][l])[:, :, 0:512],
                      deps=wdeps, channel="sla")
        t_wk = pb.dma("gpsimd", WKS, rearr_kpm(io["wqkv_f"][l])[:, :, 512:1024],
                      deps=wdeps, channel="slb")
        t_wv = pb.dma("gpsimd", WVS, rearr_kpm(io["wqkv_f"][l])[:, :, 1024:1536],
                      deps=wdeps, channel="u1")
        pb.phase = f"L{l}.qkv"
        yt_tok = env.get("msk_free")
        last_split = None
        prev_tt = None
        for ti in range(ntile):
            mt0 = t0 + ti * 512
            c0 = mt0 + yoff
            yt_tok = pb.dma("sync",
                            YT[:, 0:4160].rearrange("p (c m) -> p c m", m=520)
                            [:, :, 0:513],
                            rearr_kpm(ysrc)[:, :, c0 - 1:c0 + 512],
                            deps=[yt_tok, env["ysrc_ready"]], channel="rawf")
            sz_tok = None
            if mt0 <= 0 < mt0 + 512:
                p0 = -mt0

                def szwrap(e, sem, p0=p0):
                    for kc in range(4, 8):
                        e.tensor_scalar_mul(YT[:, kc * 520 + p0:kc * 520 + p0 + 1],
                                            YT[:, kc * 520 + p0:kc * 520 + p0 + 1],
                                            SZC).then_inc(sem, 1)
                sz_tok = pb.comp_n("vector", szwrap, 4, deps=[yt_tok])
            for mc in range(4):
                def lhs(kc, mc=mc):
                    off = 1 if kc < 4 else 0
                    return YT[:, kc * 520 + off + mc * 128:kc * 520 + off + mc * 128 + 128]
                tq = mmg(ctx, 0, [(ctx.bank(0), lhs(kc), WQS[:, kc * 512:(kc + 1) * 512],
                                   None) for kc in range(NCH)],
                         deps=[t_wq, yt_tok, sz_tok])
                tk = mmg(ctx, 1, [(ctx.bank(1), lhs(kc), WKS[:, kc * 512:(kc + 1) * 512],
                                   None) for kc in range(NCH)],
                         deps=[t_wk, yt_tok, sz_tok])
                tv = mmg(ctx, 2, [(ctx.bank(2), lhs(kc), WVS[:, kc * 512:(kc + 1) * 512],
                                   None) for kc in range(NCH)],
                         deps=[t_wv, yt_tok, sz_tok])
                trow = mt0 + mc * 128 + 2048
                t_rot = pb.dma("sync", ROT.rearrange("p (k c) -> p k c", k=4),
                               io["rot"][:, trow:trow + 128, :].transpose([1, 0, 2]),
                               deps=[env.get("rot_free"), last_split], channel="rot")
                env["rot_free"] = None
                cv = (mt0 + mc * 128 - t0) // 128
                ev = pb.comp("vector", lambda e, cv=cv: e.tensor_copy(
                    VTOK[:, cv * 512:(cv + 1) * 512], ctx.bank(2)), deps=[tv])
                ctx.bank_tok[2] = ev
                prev_part = None
                for part, bank, tok, csoff in (("q", 0, tq, 0), ("k", 1, tk, 128)):
                    t_sq = pb.comp("scalar", lambda e, bank=bank: e.activation(
                        QTMP, ctx.bank(bank), AF.Square), deps=[tok, prev_part])
                    t1s = pb.comp("vector", lambda e: e.tensor_reduce(
                        S2R[:, 0:8], QTMP.rearrange("p (h x) -> p h x", h=8),
                        axis=AX.X, op=ALU.add), deps=[t_sq])
                    t2s = pb.comp("scalar", lambda e: e.activation(
                        S2R[:, 8:16], S2R[:, 0:8], AF.Sqrt), deps=[t1s])
                    t3s = pb.comp("vector", lambda e: e.tensor_scalar_max(
                        S2R[:, 8:16], S2R[:, 8:16], 1e-12), deps=[t2s])
                    t4s = pb.comp("vector", lambda e: e.reciprocal(
                        S2R[:, 0:8], S2R[:, 8:16]), deps=[t3s])
                    t5s = pb.comp("vector", lambda e, bank=bank: e.tensor_tensor(
                        GT2.rearrange("p (h x) -> p h x", h=8),
                        ctx.bank(bank).rearrange("p (h x) -> p h x", h=8),
                        S2R[:, 0:8].unsqueeze(2).broadcast_to([128, 8, 64]),
                        op=ALU.mult), deps=[t4s, tok])
                    ctx.bank_tok[bank] = t5s
                    qof = 0 if part == "q" else 512
                    t6s = pb.comp("vector", lambda e, qof=qof: e.tensor_tensor(
                        GT2, GT2, BQS[:, qof:qof + 512], op=ALU.mult),
                        deps=[t5s, cb1, cb2])
                    cs0 = 0 if csoff == 0 else 128
                    t7 = pb.comp("vector", lambda e, cs0=cs0: e.tensor_tensor(
                        QTMP.rearrange("p (h x) -> p h x", h=8),
                        GT2.rearrange("p (h x) -> p h x", h=8),
                        ROT[:, cs0:cs0 + 64].unsqueeze(1).broadcast_to([128, 8, 64]),
                        op=ALU.mult), deps=[t6s, t_rot])
                    t8 = pb.comp("vector", lambda e, cs0=cs0: e.tensor_tensor(
                        ETMP.rearrange("p (h x) -> p h x", h=8)[:, :, 0:32],
                        GT2.rearrange("p (h x) -> p h x", h=8)[:, :, 32:64],
                        ROT[:, cs0 + 64:cs0 + 96].unsqueeze(1)
                        .broadcast_to([128, 8, 32]), op=ALU.mult),
                        deps=[t6s, t_rot])
                    t9 = pb.comp("vector", lambda e, cs0=cs0: e.tensor_tensor(
                        ETMP.rearrange("p (h x) -> p h x", h=8)[:, :, 32:64],
                        GT2.rearrange("p (h x) -> p h x", h=8)[:, :, 0:32],
                        ROT[:, cs0 + 96:cs0 + 128].unsqueeze(1)
                        .broadcast_to([128, 8, 32]), op=ALU.mult),
                        deps=[t6s, t_rot])
                    t10 = pb.comp("vector", lambda e: e.tensor_add(
                        KPB, QTMP, ETMP),
                        deps=[t7, t8, t9, prev_tt, env.get("ot_dma"),
                              env.get("xao_last")])
                    prev_part = t10
                    dst = QTH if part == "q" else KTH
                    for hc in range(4):
                        tt = pb.dma("scalar", TMP128,
                                    KPB[:, hc * 128:(hc + 1) * 128],
                                    deps=[t10, last_split], transpose=True,
                                    channel="tp")
                        colA = (2 * hc) * GSPAN + (mt0 + mc * 128 - t0)
                        colB = (2 * hc + 1) * GSPAN + (mt0 + mc * 128 - t0)
                        sp1 = pb.dma("sync", dst[0:64, colA:colA + 128],
                                     TMP128[0:64, :], deps=[tt], channel="sp")
                        sp2 = pb.dma("sync", dst[0:64, colB:colB + 128],
                                     TMP128[64:128, :], deps=[tt], channel="sp")
                        last_split = sp2
                        prev_tt = tt
        qkv_done = last_split
        if env.get("qkv_only"):
            return []

        has_w0 = any(w == 0 for w in wlist)
        mwar = [tq, tk, tv]
        tmsk_n2 = pb.dma("sync", WQS.rearrange("p (st m) -> p st m", m=512),
                         io["masks"][1].rearrange("(st p) m -> p st m", p=128),
                         deps=mwar, channel="mskA")
        tmsk_n1 = pb.dma("sync", WVS.rearrange("p (ms k) -> p ms k", k=1024),
                         io["masksT"][1].rearrange("(ms p) k -> p ms k", p=128),
                         deps=mwar, channel="mskB")
        tmsk_02 = tmsk_01 = None
        if has_w0:
            tmsk_02 = pb.dma("sync", WKS.rearrange("p (st m) -> p st m", m=512),
                             io["masks"][0].rearrange("(st p) m -> p st m", p=128),
                             deps=mwar, channel="mskC")
            tmsk_01 = pb.dma("sync",
                             YT[:, 0:4096].rearrange("p (ms k) -> p ms k", k=1024),
                             io["masksT"][0].rearrange("(ms p) k -> p ms k", p=128),
                             deps=mwar, channel="mskD")

        pb.phase = f"L{l}.attn"
        tadds_all = []
        for w in wlist:
            widx = 0 if w == 0 else 1
            mS = WQS if widx else WKS
            mS_tok = tmsk_n2 if widx else tmsk_02
            mT = WVS if widx else YT
            mT_tok = tmsk_n1 if widx else tmsk_01
            for h in range(8):
                qcol = lambda m: h * GSPAN + (w * 512 - t0) + m
                kcol = lambda sx: h * GSPAN + ((w - 1) * 512 - t0) + sx
                pb.phase = f"L{l}.attn.p1"
                nm_toks = []
                for msub in range(4):
                    tS1 = mmg(ctx, 4, [(ctx.bank(4),
                                        QTH[0:64, qcol(msub * 128):qcol(msub * 128) + 128],
                                        KTH[0:64, kcol(0):kcol(0) + 512], None)],
                              deps=[qkv_done])
                    tS2 = mmg(ctx, 5, [(ctx.bank(5),
                                        QTH[0:64, qcol(msub * 128):qcol(msub * 128) + 128],
                                        KTH[0:64, kcol(512):kcol(512) + 512], None)],
                              deps=[qkv_done])
                    tta = pb.comp("vector", lambda e, msub=msub, mT=mT:
                                  e.tensor_tensor(
                                      ctx.PS[:, 2048:3072], ctx.PS[:, 2048:3072],
                                      mT[:, msub * 1024:(msub + 1) * 1024],
                                      op=ALU.add),
                                  deps=[tS1, tS2, mT_tok])
                    tnm = pb.comp("vector", lambda e, msub=msub: e.tensor_reduce(
                        NMC[:, msub:msub + 1], ctx.PS[:, 2048:3072],
                        axis=AX.X, op=ALU.max, negate=True), deps=[tta])
                    ctx.bank_tok[4] = tnm
                    ctx.bank_tok[5] = tnm
                    nm_toks.append(tnm)
                tcb = pb.comp("vector", lambda e: e.tensor_copy(NMCB[:, 0:4],
                                                                NMC[:, 0:4]),
                              deps=nm_toks)
                pb.phase = f"L{l}.attn.tr"
                ttr = []
                for msub in range(4):
                    tt = pb.comp("tensor", lambda e, msub=msub: e.transpose(
                        ctx.bank_bf(6)[0:1, msub * 128:(msub + 1) * 128],
                        NMCB[:, msub:msub + 1], ident),
                        deps=[tcb, ctx.bank_tok[6]])
                    ttr.append(tt)
                tmxr = pb.comp("scalar", lambda e: e.copy(MXR[0:1, 0:512],
                                                          ctx.bank_bf(6)[0:1, 0:512]),
                               deps=ttr)
                ctx.bank_tok[6] = tmxr
                pb.phase = f"L{l}.attn.p2"
                lsum_done = None
                o_toks = []
                e_toks = []
                for st in range(8):
                    bkS = st % 2
                    tS = mmg(ctx, bkS, [
                        (ctx.bank(bkS),
                         KTH[0:64, kcol(st * 128):kcol(st * 128) + 128],
                         QTH[0:64, qcol(0):qcol(0) + 512], None),
                        (ctx.bank(bkS), ones_row, MXR[0:1, 0:512], None)],
                        deps=[tmxr, qkv_done], independent=True)
                    tadd = pb.comp("vector", lambda e, bkS=bkS, st=st, mS=mS:
                                   e.tensor_tensor(
                                       ETMP, ctx.bank(bkS),
                                       mS[:, st * 512:(st + 1) * 512],
                                       op=ALU.add),
                                   deps=[tS, mS_tok,
                                         e_toks[-1] if e_toks else None])
                    ctx.bank_tok[bkS] = tadd
                    tex = pb.comp("scalar", lambda e, st=st: e.activation(
                        EB[:, st * 512:(st + 1) * 512], ETMP, AF.Exp), deps=[tadd])
                    e_toks.append(tex)
                    tadds_all.append(tadd)
                pb.phase = f"L{l}.attn.ls"
                tls = mmg(ctx, 2, [(ctx.bank(2)[0:1, 0:512], ones_col,
                                    EB[:, st * 512:(st + 1) * 512], None)
                                   for st in range(8)],
                          deps=e_toks, cont=False)
                tfl = pb.comp("vector", lambda e: e.tensor_scalar_max(
                    RNR[0:1, 0:512], ctx.bank(2)[0:1, 0:512], 1e-35), deps=[tls])
                ctx.bank_tok[2] = tfl
                trc = pb.comp("vector", lambda e: e.reciprocal(
                    RNR[0:1, 0:512], RNR[0:1, 0:512]), deps=[tfl])
                trb = pb.comp("vector", lambda e: e.tensor_copy(
                    RNB[0:1, 0:512], RNR[0:1, 0:512]), deps=[trc])
                tbr = mmg(ctx, 3, [(ctx.bank(3), ones_row, RNB[0:1, 0:512], None)],
                          deps=[trb])
                pb.phase = f"L{l}.attn.o"
                oterms = []
                for st in range(8):
                    cv = ((w - 1) * 512 + st * 128 - t0) // 128
                    oterms.append((ctx.bank(6)[0:64, 0:512],
                                   VTOK[:, cv * 512 + h * 64:cv * 512 + h * 64 + 64],
                                   EB[:, st * 512:(st + 1) * 512], None))
                t_o = mmg(ctx, 6, oterms, deps=e_toks + [ctx.bank_tok[6]])
                toc = pb.comp("scalar", lambda e: e.copy(OTMP, ctx.bank(6)[0:64, 0:512]),
                              deps=[t_o])
                ctx.bank_tok[6] = toc
                hc = h // 2
                if h % 2 == 0:
                    tno = pb.comp("vector", lambda e, hc=hc: e.tensor_tensor(
                        OT[0:64, hc * 512:(hc + 1) * 512], OTMP,
                        ctx.bank(3)[0:64, 0:512], op=ALU.mult),
                        deps=[toc, tbr, env.get("ot_free")])
                    ctx.bank_tok[3] = tno
                    o_done = tno
                else:
                    tno = pb.comp("vector", lambda e: e.tensor_tensor(
                        KPB[0:64, 0:512], OTMP, ctx.bank(3)[0:64, 0:512],
                        op=ALU.mult),
                        deps=[toc, tbr, env.get("ot_dma"), env.get("xao_last")])
                    ctx.bank_tok[3] = tno
                    o_done = pb.dma("sync", OT[64:128, hc * 512:(hc + 1) * 512],
                                    KPB[0:64, 0:512], deps=[tno, env.get("ot_free")],
                                    channel="ot")
                    env["ot_dma"] = o_done
                env["o_done_" + str(h)] = o_done
            pb.phase = f"L{l}.wo"
            o_all = [env["o_done_" + str(h)] for h in range(8)]
            ycol = w * 512 + yoff
            wo_oc_last = env.get("xao_last")
            res_consumer = env.get("res_last")
            for oc in range(NCH):
                if ysrc_f is None:
                    yw_tok = pb.dma("gpsimd", RESF,
                                    rearr_kpm(ysrc)[:, oc, ycol:ycol + 512],
                                    deps=[res_consumer, o_all[-1]],
                                    channel="ycast")
                else:
                    yw_tok = pb.dma("sync", RESF,
                                    rearr_kpm(ysrc_f)[:, oc, ycol:ycol + 512],
                                    deps=[res_consumer, o_all[-1]],
                                    channel="resf")
                sl = (sb["SLA"], sb["SLB"])[oc % 2]
                ch = ("sla", "slb")[oc % 2]
                tld = pb.dma("gpsimd", sl[:, 0:512],
                             io["wo_f"][l].rearrange("(ic kp) o -> kp ic o", kp=128)
                             [:, :, oc * 128:(oc + 1) * 128],
                             deps=[ctx.slab_last[ch]], channel=ch)
                bk = 7
                terms = [(ctx.bank(bk), sl[:, ic * 128:(ic + 1) * 128],
                          OT[:, ic * 512:(ic + 1) * 512], None) for ic in range(4)]
                tm = mmg(ctx, bk, terms, deps=[tld] + o_all)
                ctx.slab_last[ch] = tm
                tadd = pb.comp("vector", lambda e, bk=bk, oc=oc: e.tensor_tensor(
                    GT2, ctx.bank(bk), RESF, op=ALU.add),
                    deps=[tm, yw_tok, wo_oc_last])
                res_consumer = tadd
                ctx.bank_tok[bk] = tadd
                tcb16 = pb.comp("vector", lambda e: e.tensor_copy(KPB, GT2),
                                deps=[tadd])
                col = w * 512 + env["wo_off"]
                d1 = pb.dma("sync", rearr_kpm(env["wo_f32"])[:, oc, col:col + 512],
                            GT2, deps=[tadd], channel="xao")
                d2 = pb.dma("sync", rearr_kpm(env["wo_b16"])[:, oc, col:col + 512],
                            KPB, deps=[tcb16], channel="xao")
                wo_oc_last = d2
            env["ot_free"] = tm
            env["xao_last"] = wo_oc_last
            env["res_last"] = res_consumer
            wo_last = wo_oc_last
        env["msk_free"] = (pb.psem["vector"], pb.cnt["vector"])

    pb.phase = f"L{l}.ffn"
    pb.barrier()
    final = env["final"]
    out_dmas = []

    def do_ffn_tile(a, b, ffn_prev):
        wdt = b - a
        ca = a + env["wo_off"]
        rf_tok = pb.dma("sync",
                        RAW2F.rearrange("p (c m) -> p c m", m=513)[:, :, 0:wdt + 1],
                        rearr_kpm(env["wo_f32"])[:, :, ca - 1:ca + wdt],
                        deps=[env.get("rawf_free"), ffn_prev], channel="raw2f")
        env["rawf_free"] = None
        sz_tok = None
        if a <= 0 < b:
            p0 = -a

            def szw2(e, sem, p0=p0):
                for kc in range(4, 8):
                    e.tensor_scalar_mul(RAW2F[:, kc * 513 + p0:kc * 513 + p0 + 1],
                                        RAW2F[:, kc * 513 + p0:kc * 513 + p0 + 1],
                                        SZC).then_inc(sem, 1)
            sz_tok = pb.comp_n("vector", szw2, 4, deps=[rf_tok])
        s1_tok = None
        if final:
            s1_tok = pb.dma("sync", S1T.rearrange("p (c m) -> p c m", c=8)
                            [:, :, 0:wdt],
                            rearr_kpm(io["sc_s1"])[:, :, a:a + wdt],
                            deps=[env.get("s1_free"), ffn_prev], channel="s1i",
                            allow_slow_non_contiguous=(wdt < 8))

        def rawsh(c):
            off = 1 if c < 4 else 0
            return RAW2F[:, c * 513 + off:c * 513 + off + wdt]
        t_xn, lnfree = emit_ln(ctx, rawsh, (sz_tok or rf_tok),
                               lambda c: YT[:, c * 512:c * 512 + wdt], mt=wdt)
        aff = t_xn
        for c in range(NCH):
            a1t = pb.comp("vector", lambda e, c=c: e.tensor_scalar_mul(
                YT[:, c * 512:c * 512 + wdt], YT[:, c * 512:c * 512 + wdt],
                LNG[:, l * 8 + c:l * 8 + c + 1]), deps=[aff])
            aff = pb.comp("vector", lambda e, c=c: e.tensor_scalar_add(
                YT[:, c * 512:c * 512 + wdt], YT[:, c * 512:c * 512 + wdt],
                LNB[:, l * 8 + c:l * 8 + c + 1]), deps=[a1t])
        h2_tok = aff
        rh_last = None
        b1_tok = None
        for fc in range(43):
            sl = (sb["SLA"], sb["SLB"])[fc % 2]
            ch = ("sla", "slb")[fc % 2]
            tld = pb.dma("gpsimd", sl,
                         rearr_kpm(io["w1p"][l])[:, :, fc * 128:(fc + 1) * 128],
                         deps=[ctx.slab_last[ch]], channel=ch)
            trb = pb.dma("gpsimd", B1R,
                         io["b1p"][l][fc * 128:(fc + 1) * 128].unsqueeze(0),
                         deps=[b1_tok or rh_last], channel="rb1")
            bk = fc % 2
            terms = [(ctx.bank(bk)[:, 0:wdt], sl[:, kc * 128:(kc + 1) * 128],
                      YT[:, kc * 512:kc * 512 + wdt], None) for kc in range(NCH)]
            terms.append((ctx.bank(bk)[:, 0:wdt], B1R, ones512[0:1, 0:wdt], None))
            tm1 = mmg(ctx, bk, terms, deps=[tld, trb, h2_tok])
            b1_tok = tm1
            ctx.slab_last[ch] = tm1
            tgl = pb.comp("scalar", lambda e, bk=bk, wdt=wdt: e.activation(
                ETMP[0:64, 0:wdt], ctx.bank(bk)[64:128, 0:wdt], AF.Gelu),
                deps=[tm1, rh_last])
            p = fc // 2
            r0 = (fc % 2) * 64
            tpr = pb.comp("vector", lambda e, bk=bk, p=p, r0=r0, wdt=wdt:
                          e.tensor_tensor(
                              HP[r0:r0 + 64, p * 512:p * 512 + wdt],
                              ctx.bank(bk)[0:64, 0:wdt],
                              ETMP[0:64, 0:wdt], op=ALU.mult), deps=[tgl])
            ctx.bank_tok[bk] = tpr
            rh_last = tpr
        w2_tok = None
        for oc in range(NCH):
            w2a = pb.dma("gpsimd", WQS[:, 0:21 * 128],
                         io["w2p"][l][0:2688].rearrange("(pk kp) o -> kp pk o", kp=128)
                         [:, :, oc * 128:(oc + 1) * 128],
                         deps=[w2_tok, env.get("w2_free")], channel="u1")
            w2b = pb.dma("gpsimd", WQS[0:64, 21 * 128:22 * 128],
                         io["w2p"][l][2688:2752, oc * 128:(oc + 1) * 128],
                         deps=[w2_tok, env.get("w2_free")], channel="u1")
            env["w2_free"] = None
            bk = 2 + oc % 2
            terms = [(ctx.bank(bk)[:, 0:wdt], WQS[:, pk * 128:(pk + 1) * 128],
                      HP[:, pk * 512:pk * 512 + wdt], None) for pk in range(21)]
            terms.append((ctx.bank(bk)[:, 0:wdt], WQS[0:64, 21 * 128:22 * 128],
                          HP[0:64, 21 * 512:21 * 512 + wdt], None))
            terms.append((ctx.bank(bk)[:, 0:wdt],
                          B2R[:, oc * 128:(oc + 1) * 128], ones512[0:1, 0:wdt],
                          None))
            tm2 = mmg(ctx, bk, terms, deps=[w2b, rh_last, t_b2])
            w2_tok = tm2
            tr1 = pb.comp("vector", lambda e, bk=bk, oc=oc, wdt=wdt: e.tensor_tensor(
                GT2[:, 0:wdt], ctx.bank(bk)[:, 0:wdt],
                RAW2F[:, oc * 513 + 1:oc * 513 + 1 + wdt], op=ALU.add),
                deps=[tm2, out_dmas[-1] if out_dmas else None])
            if final:
                tr2 = pb.comp("vector", lambda e, oc=oc, wdt=wdt: e.tensor_tensor(
                    GT2[:, 0:wdt], GT2[:, 0:wdt],
                    S1T[:, oc * 512:oc * 512 + wdt], op=ALU.add),
                    deps=[tr1, s1_tok])
                ctx.bank_tok[bk] = tr2
                dd = pb.dma("sync", rearr_kpm(io["out"])[:, oc, a:a + wdt],
                            GT2[:, 0:wdt], deps=[tr2], channel="outo",
                            allow_slow_non_contiguous=(wdt < 8))
                out_dmas.append(dd)
            else:
                ctx.bank_tok[bk] = tr1
                cb16 = pb.comp("vector", lambda e, wdt=wdt: e.tensor_copy(
                    KPB[:, 0:wdt], GT2[:, 0:wdt]), deps=[tr1])
                fcol = a + env["fo_off"]
                dd = pb.dma("sync",
                            rearr_kpm(env["ffn_f32"])[:, oc, fcol:fcol + wdt],
                            GT2[:, 0:wdt], deps=[tr1], channel="outo",
                            allow_slow_non_contiguous=(wdt < 8))
                dd = pb.dma("sync",
                            rearr_kpm(env["ffn_b16"])[:, oc, fcol:fcol + wdt],
                            KPB[:, 0:wdt], deps=[cb16], channel="outo",
                            allow_slow_non_contiguous=(wdt < 8))
                out_dmas.append(dd)
        env["s1_free"] = None if not final else w2_tok
        return out_dmas[-1]

    ffn_prev = None
    for (a, b) in env["ffn_tiles"]:
        ffn_prev = do_ffn_tile(a, b, ffn_prev)
    return out_dmas


# version 06
# speedup vs baseline: 1.3182x; 1.3182x over previous
import numpy as np
import ml_dtypes
import concourse.bass as bass
import concourse.mybir as mybir
from concourse.bass_utils import run_bass_kernel_spmd

F32 = mybir.dt.float32
BF16 = mybir.dt.bfloat16
I32 = mybir.dt.int32
AF = mybir.ActivationFunctionType
ALU = mybir.AluOpType
AX = mybir.AxisListType
ET = mybir.EngineType

B, N, D = 4, 4096, 1024
HALF = 2048
NCH = 8
MT = 512
NMT = 4
DH = 64
BLOCK_M = 64
SEG = 256
DIL = 4
EPS_LN = 1e-5
WINDOW = 512
HID = 2730
QK_SCALE = 8.0
NEG = -1e30

ENGINES = ("tensor", "vector", "scalar", "gpsimd", "sync")
BF = ml_dtypes.bfloat16
SB_BASE = 16512



def host_gather_indices(mapping, own_start):
    m = np.asarray(mapping).astype(np.int64)
    qidx = m[np.arange(own_start, own_start + HALF)]
    kidx = np.empty(HALF, dtype=np.int64)
    for j in range(HALF // BLOCK_M):
        gn = (own_start + j * BLOCK_M) // BLOCK_M
        seg = m[gn * BLOCK_M] // SEG
        kidx[j * BLOCK_M:(j + 1) * BLOCK_M] = m[seg * SEG + DIL * np.arange(BLOCK_M)]
    return qidx, kidx


def host_consts():
    c = {}
    c["cn_ident"] = np.eye(128, dtype=BF)
    c["cn_ones_1x128"] = np.ones((1, 128), dtype=BF)
    c["cn_ones_1x512"] = np.ones((1, 512), dtype=BF)
    blk = np.zeros((128, 2), dtype=BF)
    blk[0:64, 0] = 1
    blk[64:128, 1] = 1
    c["cn_blk128x2"] = blk
    blk2 = np.zeros((2, 128), dtype=BF)
    blk2[0, 0:64] = 1
    blk2[1, 64:128] = 1
    c["cn_blk2x128"] = blk2
    for e in range(4):
        s = np.zeros((4, 128), dtype=BF)
        s[e, :] = 1
        c[f"cn_sel{e}"] = s
    return c



class PB:

    def __init__(self, nc):
        self.nc = nc
        self.q = {e: [] for e in ENGINES}
        self.cnt = {e: 0 for e in ENGINES}
        self.psem = {e: nc.alloc_semaphore(f"p_{e}") for e in ENGINES}
        self.chsem = {}
        self.chcnt = {}
        self.wm = {e: {} for e in ENGINES}
        self.want_pid = False
        self.pid = None
        self.ppid = None
        self.phase = "init"
        self.etags = {e: [] for e in ENGINES}

    def _waits(self, eng, deps):
        out = []
        wm = self.wm[eng]
        for d in deps:
            if d is None:
                continue
            sem, val = d
            k = sem.name
            if wm.get(k, 0) >= val:
                continue
            wm[k] = val
            out.append((sem, val))
        return out

    def comp(self, eng, fn, deps=()):
        self.cnt[eng] += 1
        self.etags[eng].append(self.phase)
        tok = (self.psem[eng], self.cnt[eng])
        self.q[eng].append((self._waits(eng, deps), fn, tok, 1))
        return tok

    def comp_n(self, eng, fn, n, deps=()):
        self.cnt[eng] += n
        self.etags[eng].extend([self.phase] * n)
        tok = (self.psem[eng], self.cnt[eng])
        self.q[eng].append((self._waits(eng, deps), fn, tok, None))
        return tok

    CH_ENG = {"slb": "sync"}

    def dma(self, eng, out, in_, deps=(), channel="misc", **kw):
        eng = self.CH_ENG.get(channel, eng)
        if channel not in self.chsem:
            self.chsem[channel] = self.nc.alloc_semaphore(f"c_{channel}")
            self.chcnt[channel] = 0
        self.chcnt[channel] += 1
        tok = (self.chsem[channel], 16 * self.chcnt[channel])

        def fn(e, out=out, in_=in_, kw=kw):
            if callable(out):
                out = out()
            if callable(in_):
                in_ = in_()
            return e.dma_start(out=out, in_=in_, **kw)
        self.q[eng].append((self._waits(eng, deps), fn, tok, 16))
        return tok

    def raw(self, eng, fn, deps=()):
        self.q[eng].append((self._waits(eng, deps), fn, None, 0))

    def waits_only(self, eng, deps):
        ws = self._waits(eng, deps)
        if ws:
            self.q[eng].append((ws, None, None, 0))

    def snapshot(self):
        toks = []
        for e in ENGINES:
            if self.cnt[e]:
                toks.append((self.psem[e], self.cnt[e]))
        for ch, sem in self.chsem.items():
            toks.append((sem, 16 * self.chcnt[ch]))
        return toks

    def barrier(self):
        toks = self.snapshot()
        for e in ENGINES:
            self.waits_only(e, list(toks))

    def realize(self):
        with self.nc.Block() as block:
            for ename in ENGINES:
                ops = self.q[ename]

                def body(eng, ops=ops, ename=ename):
                    if ename == "sync" and self.want_pid:
                        pid = self.nc.partition_id(engines=[ET.SP])
                        self.pid = pid
                        self.ppid = self.nc.s_assert_within(
                            pid + 1 - (pid % 2) * 2, 0, 7,
                            skip_runtime_assert=True)
                    for oi, (waits, fn, tok, inc) in enumerate(ops):
                        for (s, v) in waits:
                            eng.wait_ge(s, v)
                        if fn is None:
                            continue
                        try:
                            if inc is None:
                                fn(eng, tok[0])
                                continue
                            inst = fn(eng)
                        except Exception:
                            import sys
                            print(f"REALIZE FAIL {ename} op {oi}/{len(ops)}",
                                  file=sys.stderr)
                            raise
                        if tok is not None and inst is not None:
                            inst.then_inc(tok[0], inc)
                getattr(block, ename)(body)


class Ctx:
    def __init__(self, nc):
        self.nc = nc
        self.pb = PB(nc)
        self.PS = nc.alloc_psum_tensor("PS", [128, 4096], F32).ap()
        self.PSb = self.PS.bitcast(BF16)
        self.bank_tok = [None] * 8
        self.slab_last = {"sla": None, "slb": None, "u1": None}
        self.sb = {}

    def bank(self, i):
        return self.PS[:, 512 * i: 512 * (i + 1)]

    def bank_bf(self, i):
        return self.PSb[:, 1024 * i: 1024 * (i + 1)]

    def sb_at(self, name, offset, shape, dtype):
        if name not in self.sb:
            self.sb[name] = self.nc.alloc_sbuf_tensor_at(
                name, shape, dtype, offset=SB_BASE + offset).ap()
        return self.sb[name]


def mmg(ctx, bank_i, terms, deps=(), cont=False, independent=False):
    pb = ctx.pb
    alldeps = list(deps)
    if not cont and ctx.bank_tok[bank_i] is not None:
        alldeps.append(ctx.bank_tok[bank_i])

    def fn(t, sem, terms=terms, cont=cont, independent=independent):
        for i, (out_ap, lhsT, rhs, tp) in enumerate(terms):
            kw = {}
            if tp is not None:
                kw["tile_position"] = tp
            st = True if independent else (i == 0 and not cont)
            sp = True if independent else (i == len(terms) - 1)
            t.matmul(out_ap, lhsT, rhs, start=st, stop=sp,
                     skip_group_check=True, **kw).then_inc(sem, 1)
    return pb.comp_n("tensor", fn, len(terms), deps=alldeps)


SBMAP = dict(
    BIG_A=0,
    BIG_B=32768,
    BIG_C=65536,
    BIG_D=98304,
    ATt=131072,
    E1=139264,
    E2=147456,
    RAWF=155648,
    U1=171840,
    SQ=180032,
    STM=184128,
    STB=194368,
    LNT=196416,
    SLA=198464,
    SLB=200512,
    LS=202560,
    LSR=204608,
    NMX=205632,
    BQK=205664,
    BQKQ=205760,
    BV=205792,
    IDENT=207840,
    ONES1x128=208096,
    ONES1x512=208352,
    BLK128x2=209376,
    BLK2x128=209440,
    ONES128x1=209696,
    GW=209728,
    GSC=209792,
    GROWS4=209920,
    GBF=211968,
    GB=211976,
    SB1S=211968,
    SEL=212112,
    BCOMB=212112,
)
SBMAP["GROWS4"] = 209920
SBMAP["BCOMB"] = 210944
SBMAP["GBF"] = 209792 + 64
SBMAP["GB"] = 209792 + 96
SBMAP["GROWE"] = 209792 + 96
SBMAP["GROWE"] = 202560
SBMAP["ONES128x1F"] = 212096
del SBMAP["SEL"]


def emit_ln(ctx, raw, raw_dep, out_bf, mt=MT,
            bank_s=6, bank_q=7, bank_a=4, bank_c=5):
    pb = ctx.pb
    onescol = ctx.sb_at("ONES128x1F", SBMAP["ONES128x1F"], [128, 1], F32)
    onesrow = ctx.sb_at("ONES1x128", SBMAP["ONES1x128"], [1, 128], BF16)
    sq = ctx.sb_at("SQ", SBMAP["SQ"], [128, 1024], F32)
    stm = ctx.sb_at("STM", SBMAP["STM"], [1, 2560], F32)
    stb = ctx.sb_at("STB", SBMAP["STB"], [1, 1024], BF16)
    lnt = ctx.sb_at("LNT", SBMAP["LNT"], [128, 512], F32)
    mean = stm[:, 0:mt]
    var = stm[:, 512:512 + mt]
    rstd = stm[:, 1024:1024 + mt]
    crow = stm[:, 1536:1536 + mt]
    m2 = stm[:, 2048:2048 + mt]
    rstd_b = stb[:, 0:mt]
    crow_b = stb[:, 512:512 + mt]

    t_s = mmg(ctx, bank_s,
              [(ctx.bank(bank_s)[0:1, 0:mt], onescol, raw(c), None)
               for c in range(NCH)], deps=[raw_dep])
    sq_tok = [None, None]
    mm_prev = None
    for c in range(NCH):
        half = (c % 2) * 512
        tk = pb.comp("scalar", lambda e, c=c, half=half: e.activation(
            sq[:, half:half + mt], raw(c), AF.Square),
            deps=[raw_dep, sq_tok[c % 2]])
        dep = [tk]
        if c == 0 and ctx.bank_tok[bank_q] is not None:
            dep.append(ctx.bank_tok[bank_q])
        mm_prev = mmg(ctx, bank_q, [(ctx.bank(bank_q)[0:1, 0:mt], onescol,
                                     sq[:, half:half + mt], None)],
                      deps=dep, cont=(c > 0))
        sq_tok[c % 2] = mm_prev
    t_q = mm_prev

    t1 = pb.comp("scalar", lambda e: e.activation(
        mean, ctx.bank(bank_s)[0:1, 0:mt], AF.Identity, scale=1.0 / D), deps=[t_s])
    ctx.bank_tok[bank_s] = t1
    tm2 = pb.comp("scalar", lambda e: e.activation(m2, mean, AF.Square), deps=[t1])
    t2 = pb.comp("vector", lambda e: e.scalar_tensor_tensor(
        var, ctx.bank(bank_q)[0:1, 0:mt], 1.0 / D, m2,
        op0=ALU.mult, op1=ALU.subtract), deps=[t_q, tm2])
    ctx.bank_tok[bank_q] = t2
    t2b = pb.comp("vector", lambda e: e.tensor_scalar_add(var, var, EPS_LN),
                  deps=[t2])
    t3 = pb.comp("scalar", lambda e: e.activation(var, var, AF.Ln), deps=[t2b])
    t4 = pb.comp("scalar", lambda e: e.activation(rstd, var, AF.Exp, scale=-0.5),
                 deps=[t3])
    t5 = pb.comp("vector", lambda e: e.scalar_tensor_tensor(
        crow, mean, -1.0, rstd, op0=ALU.mult, op1=ALU.mult), deps=[t4])
    tb1 = pb.comp("vector", lambda e: e.tensor_copy(rstd_b, rstd), deps=[t4])
    tb2 = pb.comp("vector", lambda e: e.tensor_copy(crow_b, crow), deps=[t5])
    tA = mmg(ctx, bank_a, [(ctx.bank(bank_a)[:, 0:mt], onesrow, rstd_b, None)],
             deps=[tb1])
    tC = mmg(ctx, bank_c, [(ctx.bank(bank_c)[:, 0:mt], onesrow, crow_b, None)],
             deps=[tb2])
    prev = None
    for c in range(NCH):
        p1 = pb.comp("vector", lambda e, c=c: e.tensor_tensor(
            lnt[:, 0:mt], raw(c), ctx.bank(bank_a)[:, 0:mt], op=ALU.mult),
            deps=[tA, raw_dep, prev])
        p2 = pb.comp("vector", lambda e, c=c: e.tensor_tensor(
            out_bf(c), lnt[:, 0:mt], ctx.bank(bank_c)[:, 0:mt], op=ALU.add),
            deps=[p1, tC])
        prev = p2
    ctx.bank_tok[bank_a] = prev
    ctx.bank_tok[bank_c] = prev
    return prev, [prev, t_s, t_q]


def rearr_kpm(dram_2d):
    return dram_2d.rearrange("(kc kp) m -> kp kc m", kp=128)


def apv(x):
    return x() if callable(x) else x


def emit_block0(ctx, io, ph1_only=False, stage=9):
    nc, pb = ctx.nc, ctx.pb
    sa = ctx.sb_at
    BIG_A = sa("BIG_A", SBMAP["BIG_A"], [128, 16384], BF16)
    BIG_B = sa("BIG_B", SBMAP["BIG_B"], [128, 16384], BF16)
    BIG_C = sa("BIG_C", SBMAP["BIG_C"], [128, 16384], BF16)
    BIG_D = sa("BIG_D", SBMAP["BIG_D"], [128, 16384], BF16)
    ATt = sa("ATt", SBMAP["ATt"], [128, 4096], BF16)
    Ebuf = sa("E1", SBMAP["E1"], [128, 4096], BF16)
    E2buf = sa("E2", SBMAP["E2"], [128, 4096], BF16)
    RAWF = sa("RAWF", SBMAP["RAWF"], [128, 4096], F32)
    U1 = sa("U1", SBMAP["U1"], [128, 4096], BF16)
    SLA = sa("SLA", SBMAP["SLA"], [128, 1024], BF16)
    SLB = sa("SLB", SBMAP["SLB"], [128, 1024], BF16)
    LS = sa("LS", SBMAP["LS"], [2, 512], F32)
    LSR = sa("LSR", SBMAP["LSR"], [2, 512], BF16)
    NMX = sa("NMX", SBMAP["NMX"], [128, 8], F32)
    BQK = sa("BQK", SBMAP["BQK"], [128, 24], F32)
    BQKQ = sa("BQKQ", SBMAP["BQKQ"], [128, 8], F32)
    BV = sa("BV", SBMAP["BV"], [1, 1024], BF16)
    IDENT = sa("IDENT", SBMAP["IDENT"], [128, 128], BF16)
    ONES1x128 = sa("ONES1x128", SBMAP["ONES1x128"], [1, 128], BF16)
    ONES1x512 = sa("ONES1x512", SBMAP["ONES1x512"], [1, 512], BF16)
    BLK128x2 = sa("BLK128x2", SBMAP["BLK128x2"], [128, 2], BF16)
    BLK2x128 = sa("BLK2x128", SBMAP["BLK2x128"], [2, 128], BF16)
    sa("ONES128x1", SBMAP["ONES128x1"], [128, 1], BF16)
    GW = sa("GW", SBMAP["GW"], [128, 32], BF16)
    GSC = sa("GSC", SBMAP["GSC"], [128, 16], F32)
    GBF = sa("GBF", SBMAP["GBF"], [128, 4], BF16)
    GB = sa("GB", SBMAP["GB"], [1, 4], BF16)
    GROWS4 = sa("GROWS4", SBMAP["GROWS4"], [4, 1024], BF16)
    GROWE = sa("GROWE", SBMAP["GROWE"], [1, 1024], BF16)
    SB1S = sa("SB1S", SBMAP["SB1S"], [128, 32], F32)

    pb.phase = "b0.init"
    tk_ones = pb.comp("gpsimd", lambda e: e.memset(
        ctx.sb["ONES128x1"], 1.0))
    sa("ONES128x1F", SBMAP["ONES128x1F"], [128, 1], F32)
    tk_onesf = pb.comp("gpsimd", lambda e: e.memset(ctx.sb["ONES128x1F"], 1.0))
    t_id = pb.dma("gpsimd", IDENT, io["cn_ident"], channel="const")
    t_o1 = pb.dma("gpsimd", ONES1x128, io["cn_ones_1x128"], channel="const")
    t_o5 = pb.dma("gpsimd", ONES1x512, io["cn_ones_1x512"], channel="const")
    t_bk = pb.dma("gpsimd", BLK128x2, io["cn_blk128x2"], channel="const")
    t_bk2 = pb.dma("gpsimd", BLK2x128, io["cn_blk2x128"], channel="const")
    t_bq = pb.dma("gpsimd", BQK, io["bqkv_pc"], channel="const")

    t_bv = pb.dma("gpsimd", BV, io["b_qkv"][2048:3072].unsqueeze(0), channel="const")
    t_gw = pb.dma("gpsimd", GW.rearrange("p (kc e) -> p kc e", kc=8),
                  io["gate_w"].rearrange("(kc kp) e -> kp kc e", kp=128),
                  channel="const")
    t_gb = pb.dma("gpsimd", GB, io["gate_b"].unsqueeze(0), channel="const")
    pb.barrier()
    t_bqs = pb.comp("scalar", lambda e: e.mul(BQKQ, BQK[:, 0:8], 0.125))

    pb.phase = "b0.ph1"
    def ln_into(src_dram, dst_big, free0):
        toks = []
        free = list(free0)
        for mi in range(NMT):
            dep = pb.dma("sync", RAWF.rearrange("p (c m) -> p c m", c=8),
                         rearr_kpm(src_dram)[:, :, mi * MT:(mi + 1) * MT],
                         deps=free, channel="rawf")
            last, free = emit_ln(
                ctx, lambda c: RAWF[:, c * MT:(c + 1) * MT], dep,
                lambda c, mi=mi: dst_big[:, c * 2048 + mi * MT:c * 2048 + (mi + 1) * MT])
            toks.append(last)
        return toks[-1], free

    xnq_tok, _lnfree_q = ln_into(io["xqT"], BIG_A, [])
    if stage <= 1:
        return []

    def fm_proj(w_cols_n0, dst_big, bias_col, scale, src_big, src_tok,
                banks=(0, 1, 2, 3)):
        out_toks = []
        for ni in range(NCH):
            sl = (SLA, SLB)[ni % 2]
            ch = ("sla", "slb")[ni % 2]
            tld = pb.dma("gpsimd", sl,
                         rearr_kpm(io["w_qkv"])[:, :, w_cols_n0 + ni * 128:
                                                w_cols_n0 + (ni + 1) * 128],
                         deps=[ctx.slab_last[ch]], channel=ch)
            last_mm = None
            for mi in range(NMT):
                bk = banks[mi % len(banks)]
                terms = [(ctx.bank(bk), sl[:, kc * 128:(kc + 1) * 128],
                          src_big[:, kc * 2048 + mi * MT:kc * 2048 + (mi + 1) * MT],
                          None) for kc in range(NCH)]
                tm = mmg(ctx, bk, terms, deps=[tld, src_tok])
                te = pb.comp("scalar", lambda e, bk=bk, ni=ni, mi=mi: e.activation(
                    dst_big[:, ni * 2048 + mi * MT:ni * 2048 + (mi + 1) * MT],
                    ctx.bank(bk), AF.Identity, bias=bias_col(ni), scale=scale),
                    deps=[tm])
                ctx.bank_tok[bk] = te
                out_toks.append(te)
                last_mm = tm
            ctx.slab_last[ch] = last_mm
        return out_toks[-1]

    q_tok = fm_proj(0, BIG_B, lambda ni: BQKQ[:, ni:ni + 1], 0.125,
                    BIG_A, xnq_tok)
    if stage <= 2:
        return []

    xnkv_tok, _lnfree_kv = ln_into(io["xkvT"], BIG_C, _lnfree_q)
    k_tok = fm_proj(1024, BIG_D, lambda ni: BQK[:, 8 + ni:9 + ni], 1.0,
                    BIG_C, xnkv_tok)
    if stage <= 3:
        return []

    v_toks = []
    vw_tok = None
    for nh in range(2):
        vw_tok = pb.dma("gpsimd", U1,
                        rearr_kpm(io["w_qkv"])[:, :, 2048 + nh * 512:2048 + (nh + 1) * 512],
                        deps=[ctx.slab_last["u1"], q_tok], channel="u1")
        for tc in range(16):
            bk = tc % 4
            terms = [(ctx.bank(bk), BIG_C[:, kc * 2048 + tc * 128:kc * 2048 + (tc + 1) * 128],
                      U1[:, kc * 512:(kc + 1) * 512], None) for kc in range(NCH)]
            terms.append((ctx.bank(bk), ONES1x128, BV[:, nh * 512:(nh + 1) * 512], None))
            tm = mmg(ctx, bk, terms, deps=[vw_tok, xnkv_tok, t_bv, q_tok])
            ctx.slab_last["u1"] = tm
            te = pb.comp("vector", lambda e, bk=bk, tc=tc, nh=nh: e.tensor_copy(
                BIG_A[:, tc * 1024 + nh * 512:tc * 1024 + (nh + 1) * 512],
                ctx.bank(bk)), deps=[tm])
            ctx.bank_tok[bk] = te
            v_toks.append(te)
    v_tok = v_toks[-1]
    if stage <= 4:
        return []

    QT, KT, VT = BIG_B, BIG_D, BIG_A
    at_dma = None
    last_exp = None
    last_r = None
    last_br = None
    prev_ta = None
    VT2t = E2buf
    for mi in range(NMT):
        vsw1 = pb.dma("sync", VT2t[0:64, :], VT[64:128, mi * 4096:(mi + 1) * 4096],
                      deps=[v_tok, prev_ta], channel="e2s")
        vsw_tok = pb.dma("sync", VT2t[64:128, :], VT[0:64, mi * 4096:(mi + 1) * 4096],
                         deps=[v_tok, prev_ta], channel="e2s")
        hp_toks = []
        for hp in range(NCH):
            bk_s = hp % 2
            bk_l = 2
            bk_r = 3
            bk_o = 4 + hp % 2
            col = lambda j: hp * 2048 + mi * MT + j * 64
            terms = []
            for j in range(8):
                for h2 in range(2):
                    b0 = h2 * 64
                    terms.append((ctx.bank(bk_s)[b0:b0 + 64, j * 64:j * 64 + 64],
                                  KT[b0:b0 + 64, col(j):col(j) + 64],
                                  QT[b0:b0 + 64, col(j):col(j) + 64],
                                  (b0, b0)))
            t_s = mmg(ctx, bk_s, terms, deps=[q_tok, k_tok, at_dma],
                      independent=True)
            t_mx = pb.comp("vector", lambda e, bk_s=bk_s: e.tensor_reduce(
                NMX, ctx.bank(bk_s).rearrange("p (j x) -> p j x", j=8),
                axis=AX.X, op=ALU.max, negate=True), deps=[t_s, last_exp])
            e_toks = []
            for j in range(8):
                tk = pb.comp("scalar", lambda e, j=j, hp=hp, bk_s=bk_s: e.activation(
                    Ebuf[:, hp * 512 + j * 64:hp * 512 + (j + 1) * 64],
                    ctx.bank(bk_s)[:, j * 64:j * 64 + 64], AF.Exp,
                    bias=NMX[:, j:j + 1]), deps=[t_mx])
                e_toks.append(tk)
            ctx.bank_tok[bk_s] = e_toks[-1]
            if stage <= 5:
                last_exp = e_toks[-1]
                prev_ta = e_toks[-1]
                hp_toks.append(e_toks[-1])
                continue
            lterms = [(ctx.bank(bk_l)[0:2, j * 64:j * 64 + 64], BLK128x2,
                       Ebuf[:, hp * 512 + j * 64:hp * 512 + (j + 1) * 64], None)
                      for j in range(8)]
            t_l = mmg(ctx, bk_l, lterms, deps=e_toks, independent=True)
            t_eps = pb.comp("vector", lambda e, bk_l=bk_l: e.tensor_scalar_add(
                LS, ctx.bank(bk_l)[0:2, :], 1e-6), deps=[t_l, last_r])
            ctx.bank_tok[bk_l] = t_eps
            t_r0 = pb.comp("vector", lambda e: e.reciprocal(LS, LS),
                           deps=[t_eps, last_br])
            t_r = pb.comp("vector", lambda e: e.tensor_copy(LSR, LS), deps=[t_r0])
            t_br = mmg(ctx, bk_r, [(ctx.bank(bk_r), BLK2x128, LSR, None)],
                       deps=[t_r])
            t_en = pb.comp("vector", lambda e, hp=hp, bk_r=bk_r: e.tensor_tensor(
                Ebuf[:, hp * 512:(hp + 1) * 512], Ebuf[:, hp * 512:(hp + 1) * 512],
                ctx.bank(bk_r), op=ALU.mult), deps=[t_br] + e_toks)
            ctx.bank_tok[bk_r] = t_en
            if stage <= 6:
                last_exp = e_toks[-1]
                last_r = t_r
                last_br = t_br
                prev_ta = t_en
                hp_toks.append(t_en)
                continue
            aterms = []
            for j in range(8):
                tcl = j // 2
                for h2 in range(2):
                    h = 2 * hp + h2
                    b0 = h2 * 64
                    src = VT if (j % 2) == h2 else VT2t
                    off = mi * 4096 if src is VT else 0
                    aterms.append((
                        ctx.bank(bk_o)[b0:b0 + 64, j * 64:j * 64 + 64],
                        src[b0:b0 + 64, off + tcl * 1024 + h * 64:
                            off + tcl * 1024 + h * 64 + 64],
                        Ebuf[b0:b0 + 64, hp * 512 + j * 64:hp * 512 + (j + 1) * 64],
                        (b0, b0)))
            t_a = mmg(ctx, bk_o, aterms, deps=[t_en, vsw_tok, v_tok],
                      independent=True)
            t_ev = pb.comp("scalar", lambda e, hp=hp, bk_o=bk_o: e.copy(
                ATt[:, hp * 512:(hp + 1) * 512], ctx.bank(bk_o)), deps=[t_a])
            ctx.bank_tok[bk_o] = t_ev
            hp_toks.append(t_ev)
            last_exp = e_toks[-1]
            last_r = t_r
            last_br = t_br
            prev_ta = t_a
        if stage >= 7:
            at_dma = pb.dma("sync", rearr_kpm(io["sc_A"])[:, :, mi * MT:(mi + 1) * MT],
                            ATt.rearrange("p (c m) -> p c m", c=8),
                            deps=hp_toks + [at_dma], channel="ascA")
        else:
            at_dma = None

    pb.barrier()
    if ph1_only or stage <= 8:
        return []

    AT2 = sa("AT2", SBMAP["E1"], [128, 8192], BF16)
    X1ACC = sa("X1ACC", SBMAP["BIG_D"], [128, 8192], F32)
    BCOMB = sa("BCOMB", SBMAP["ATt"], [1, 1024], BF16)
    GROWE = sa("GROWE", SBMAP["LS"], [1, 1024], BF16)
    RB1S = sa("RB1S", SBMAP["ATt"] + 2048, [1, 128], BF16)
    SELs = [sa(f"SEL{e}", SBMAP["LSR"] + 256 * e, [4, 128], BF16)
            for e in range(4)]

    def xn_ap(c, q):
        return BIG_B[:, c * 1024 + q * 512:c * 1024 + q * 512 + 512]

    def xg_ap(c, q):
        return BIG_B[:, 8192 + c * 1024 + q * 512:8192 + c * 1024 + q * 512 + 512]

    def rh_ap(fc, q):
        big = BIG_A if fc < 16 else BIG_C
        f = fc % 16
        return big[:, f * 1024 + q * 512:f * 1024 + q * 512 + 512]

    def acc_ap(oc, q):
        return X1ACC[:, oc * 1024 + q * 512:oc * 1024 + q * 512 + 512]

    rowa = RAWF[0:1, 0:1024]
    rowb = RAWF[0:1, 1024:2048]
    rowc = RAWF[0:1, 2048:3072]
    ta = pb.dma("gpsimd", rowa, io["sb2"][0].unsqueeze(0))
    tb = pb.dma("gpsimd", rowb, io["sb2"][1].unsqueeze(0))
    tc = pb.dma("gpsimd", rowc, io["b_proj"].unsqueeze(0))
    t1 = pb.comp("vector", lambda e: e.tensor_add(rowa, rowa, rowb), deps=[tc])
    bcomb_tok = pb.comp("vector", lambda e: e.scalar_tensor_tensor(
        BCOMB, rowa, 0.5, rowc, op0=ALU.mult, op1=ALU.add), deps=[t1, tc])
    sel_toks = [pb.dma("gpsimd", SELs[e], io[f"cn_sel{e}"], channel="sel")
                for e in range(4)]
    sel_last = sel_toks[-1]

    x1_dmas = []
    mp_prev_done = None
    x1_prev = None
    for mp in range(2):
        pb.phase = "b0.ph2.gate"
        gate_toks = []
        xn_toks = []
        raw_free = [bcomb_tok]
        for q in range(2):
            mi = 2 * mp + q
            x0_dep = pb.dma("sync", RAWF.rearrange("p (c m) -> p c m", c=8),
                            rearr_kpm(io["xownT"])[:, :, mi * MT:(mi + 1) * MT],
                            deps=raw_free + [mp_prev_done, x1_prev],
                            channel="rawf")
            t_xn, raw_free = emit_ln(ctx, lambda c: RAWF[:, c * MT:(c + 1) * MT],
                                     x0_dep, lambda c, q=q: xn_ap(c, q))
            xn_toks.append(t_xn)
            for mc in range(4):
                bkg = 6
                gterms = [(ctx.bank(bkg)[:, 0:4],
                           xn_ap(kc, q)[:, mc * 128:(mc + 1) * 128],
                           GW[:, kc * 4:(kc + 1) * 4], None) for kc in range(NCH)]
                gterms.append((ctx.bank(bkg)[:, 0:4], ONES1x128, GB, None))
                tg = mmg(ctx, bkg, gterms, deps=[t_xn, t_gw, t_gb])
                g0 = pb.comp("vector", lambda e, bkg=bkg: e.tensor_reduce(
                    GSC[:, 0:1], ctx.bank(bkg)[:, 0:4], axis=AX.X, op=ALU.max,
                    negate=True), deps=[tg])
                g1 = pb.comp("scalar", lambda e, bkg=bkg: e.activation(
                    GSC[:, 4:8], ctx.bank(bkg)[:, 0:4], AF.Exp,
                    bias=GSC[:, 0:1]), deps=[g0])
                ctx.bank_tok[bkg] = g1
                g2 = pb.comp("vector", lambda e: e.tensor_reduce(
                    GSC[:, 1:2], GSC[:, 4:8], axis=AX.X, op=ALU.add), deps=[g1])
                g3 = pb.comp("vector", lambda e: e.reciprocal(
                    GSC[:, 2:3], GSC[:, 1:2]), deps=[g2])
                g4 = pb.comp("vector", lambda e: e.tensor_scalar_mul(
                    GSC[:, 4:8], GSC[:, 4:8], GSC[:, 2:3]), deps=[g3])
                g5 = pb.comp("vector", lambda e: e.tensor_reduce(
                    GSC[:, 8:9], GSC[:, 4:8], axis=AX.X, op=ALU.max), deps=[g4])
                g6 = pb.comp("vector", lambda e: e.tensor_scalar(
                    GSC[:, 9:13], GSC[:, 4:8], GSC[:, 8:9], None,
                    op0=ALU.is_ge), deps=[g5])
                g7 = pb.comp("vector", lambda e: e.scalar_tensor_tensor(
                    GSC[:, 9:13], GSC[:, 9:13], -1e9, GSC[:, 4:8],
                    op0=ALU.mult, op1=ALU.add), deps=[g6])
                g8 = pb.comp("vector", lambda e: e.tensor_reduce(
                    GSC[:, 13:14], GSC[:, 9:13], axis=AX.X, op=ALU.max), deps=[g7])
                g9 = pb.comp("vector", lambda e: e.tensor_scalar(
                    GSC[:, 9:13], GSC[:, 4:8], GSC[:, 13:14], None,
                    op0=ALU.is_ge), deps=[g8])
                g10 = pb.comp("vector", lambda e: e.tensor_tensor(
                    GBF, GSC[:, 4:8], GSC[:, 9:13], op=ALU.mult), deps=[g9])
                bkt = 7
                tt = pb.comp("tensor", lambda e, bkt=bkt: e.transpose(
                    ctx.bank_bf(bkt)[0:4, 0:128], GBF, IDENT),
                    deps=[g10, t_id, ctx.bank_tok[bkt]])
                tcp = pb.comp("scalar", lambda e, mc=mc, q=q, bkt=bkt: e.copy(
                    GROWS4[:, q * 512 + mc * 128:q * 512 + (mc + 1) * 128],
                    ctx.bank_bf(bkt)[0:4, 0:128]), deps=[tt])
                ctx.bank_tok[bkt] = tcp
                gate_toks.append(tcp)

        pb.phase = "b0.ph2.at2"
        at2_tok = pb.dma("sync", AT2.rearrange("p (c q m) -> p c q m", c=8, q=2),
                         rearr_kpm(io["sc_A"])[:, :, mp * 1024:(mp + 1) * 1024]
                         .rearrange("p c (q m) -> p c q m", q=2),
                         deps=[at_dma, mp_prev_done], channel="at2")

        pb.phase = "b0.ph2.base"
        base_last = None
        for q in range(2):
            mi = 2 * mp + q
            x0_dep = pb.dma("sync", RAWF.rearrange("p (c m) -> p c m", c=8),
                            rearr_kpm(io["xownT"])[:, :, mi * MT:(mi + 1) * MT],
                            deps=raw_free + gate_toks[-1:], channel="rawf")
            raw_free = []
            for oc in range(NCH):
                sl = (SLA, SLB)[oc % 2]
                ch = ("sla", "slb")[oc % 2]
                tld = pb.dma("gpsimd", sl,
                             rearr_kpm(io["w_proj"])[:, :, oc * 128:(oc + 1) * 128],
                             deps=[ctx.slab_last[ch]], channel=ch)
                bk = oc % 4
                terms = [(ctx.bank(bk), sl[:, kc * 128:(kc + 1) * 128],
                          AT2[:, kc * 1024 + q * 512:kc * 1024 + q * 512 + 512],
                          None) for kc in range(NCH)]
                terms.append((ctx.bank(bk), BCOMB[:, oc * 128:(oc + 1) * 128],
                              ONES1x512, None))
                tm = mmg(ctx, bk, terms, deps=[tld, at2_tok, bcomb_tok, t_o5])
                ctx.slab_last[ch] = tm
                te = pb.comp("vector", lambda e, bk=bk, oc=oc, q=q: e.tensor_tensor(
                    acc_ap(oc, q), ctx.bank(bk), RAWF[:, oc * MT:(oc + 1) * MT],
                    op=ALU.add), deps=[tm, x0_dep, x1_prev])
                ctx.bank_tok[bk] = te
                base_last = te
            raw_free = [base_last]

        pb.phase = "b0.ph2.experts"
        prev_done = base_last
        for kind, ne in (("r", 4), ("s", 2)):
            for e in range(ne):
                if kind == "r":
                    w1d = io["rw1"][e]
                    w2d = io["rw2"][e]
                    xg_toks = []
                    ge_toks = []
                    for q in range(2):
                        bkg = 6
                        tbg = mmg(ctx, bkg, [(ctx.bank(bkg), SELs[e],
                                              GROWS4[:, q * 512:(q + 1) * 512],
                                              None)],
                                  deps=[sel_last, gate_toks[-1], prev_done])
                        tge = pb.comp("vector", lambda e2, q=q, bkg=bkg:
                                      e2.tensor_copy(
                                          GROWE[:, q * 512:(q + 1) * 512],
                                          ctx.bank(bkg)[0:1, :]),
                                      deps=[tbg, prev_done])
                        ge_toks.append(tge)
                        for c in range(NCH):
                            txg = pb.comp("vector", lambda e2, c=c, q=q, bkg=bkg:
                                          e2.tensor_tensor(
                                              xg_ap(c, q), xn_ap(c, q),
                                              ctx.bank(bkg), op=ALU.mult),
                                          deps=[tbg, prev_done, xn_toks[q]])
                            xg_toks.append(txg)
                        ctx.bank_tok[bkg] = xg_toks[-1]
                    rhs_src = xg_ap
                    ge_dep = ge_toks[-1]
                    rb1_dram = io["rb1"][e]
                    rb2_tok = pb.dma("gpsimd", BV, io["rb2"][e].unsqueeze(0),
                                     deps=[prev_done], channel="bv")
                    stage1_extra = xg_toks[-1]
                else:
                    w1d = io["sw1"][e]
                    w2d = io["sw2"][e]
                    rhs_src = xn_ap
                    ge_dep = None
                    rb1_dram = None
                    tsb = pb.dma("gpsimd", SB1S, io["sb1_pc"][e],
                                 deps=[prev_done], channel="sb1")
                    sb1_tok = pb.comp("scalar", lambda e2: e2.mul(SB1S, SB1S, 0.5),
                                      deps=[tsb])
                    stage1_extra = sb1_tok

                rh_last = None
                for fc in range(32):
                    sl = (SLA, SLB)[fc % 2]
                    ch = ("sla", "slb")[fc % 2]
                    tld = pb.dma("gpsimd", sl,
                                 rearr_kpm(w1d)[:, :, fc * 128:(fc + 1) * 128],
                                 deps=[ctx.slab_last[ch], prev_done], channel=ch)
                    if kind == "r":
                        trb = pb.dma("gpsimd", RB1S,
                                     rb1_dram[fc * 128:(fc + 1) * 128].unsqueeze(0),
                                     deps=[rh_last, prev_done], channel="rb1")
                    for q in range(2):
                        bk = fc % 2
                        terms = [(ctx.bank(bk), sl[:, kc * 128:(kc + 1) * 128],
                                  rhs_src(kc, q), None) for kc in range(NCH)]
                        if kind == "r":
                            terms.append((ctx.bank(bk), RB1S,
                                          GROWE[:, q * 512:(q + 1) * 512], None))
                            dep2 = [trb, ge_dep]
                        else:
                            dep2 = []
                        tm = mmg(ctx, bk, terms,
                                 deps=[tld, stage1_extra] + dep2)
                        if kind == "r":
                            if fc % 2 == 0:
                                te = pb.comp("vector", lambda e2, bk=bk, fc=fc, q=q:
                                             e2.tensor_scalar_max(
                                                 rh_ap(fc, q), ctx.bank(bk), 0.0),
                                             deps=[tm])
                            else:
                                te = pb.comp("scalar", lambda e2, bk=bk, fc=fc, q=q:
                                             e2.activation(rh_ap(fc, q),
                                                           ctx.bank(bk), AF.Relu),
                                             deps=[tm])
                        else:
                            te = pb.comp("scalar", lambda e2, bk=bk, fc=fc, q=q:
                                         e2.activation(
                                             rh_ap(fc, q), ctx.bank(bk), AF.Relu,
                                             bias=SB1S[:, fc:fc + 1], scale=0.5),
                                         deps=[tm, sb1_tok])
                        ctx.bank_tok[bk] = te
                        rh_last = te
                    ctx.slab_last[ch] = tm

                w2_tok = None
                st2_last = None
                for oc in range(NCH):
                    w2_tok = pb.dma("gpsimd", U1,
                                    w2d.rearrange("(fc kp) o -> kp fc o", kp=128)
                                    [:, :, oc * 128:(oc + 1) * 128],
                                    deps=[ctx.slab_last["u1"], rh_last],
                                    channel="u1")
                    for q in range(2):
                        bk = 2 + (oc * 2 + q) % 2
                        terms = [(ctx.bank(bk), U1[:, fc * 128:(fc + 1) * 128],
                                  rh_ap(fc, q), None) for fc in range(32)]
                        if kind == "r":
                            terms.append((ctx.bank(bk),
                                          BV[:, oc * 128:(oc + 1) * 128],
                                          GROWE[:, q * 512:(q + 1) * 512], None))
                            dep2 = [rb2_tok, ge_dep]
                        else:
                            dep2 = []
                        tm = mmg(ctx, bk, terms, deps=[w2_tok, rh_last] + dep2)
                        ctx.slab_last["u1"] = tm
                        te = pb.comp("vector", lambda e2, bk=bk, oc=oc, q=q:
                                     e2.tensor_tensor(
                                         acc_ap(oc, q), acc_ap(oc, q),
                                         ctx.bank(bk), op=ALU.add),
                                     deps=[tm, base_last, prev_done])
                        ctx.bank_tok[bk] = te
                        st2_last = te
                prev_done = st2_last

        for q in range(2):
            dd = pb.dma("sync",
                        (lambda x1o=io["x1out"], mp=mp, q=q:
                         rearr_kpm(apv(x1o))[:, :, (2 * mp + q) * MT:
                                             (2 * mp + q + 1) * MT]),
                        X1ACC.rearrange("p (c q m) -> p c q m", c=8, q=2)[:, :, q, :],
                        deps=[prev_done], channel="x1o")
            x1_dmas.append(dd)
        x1_prev = x1_dmas[-1]
        mp_prev_done = prev_done

    return x1_dmas



def build_block0(ph1_only=False, stage=9):
    nc = bass.Bass("TRN2", target_bir_lowering=False, num_devices=8)
    io = {}

    def d_in(name, shape):
        io[name] = nc.dram_tensor(name, list(shape), F32, kind="ExternalInput").ap()

    def d_in_bf(name, shape):
        io[name] = nc.dram_tensor(name, list(shape), BF16, kind="ExternalInput").ap()

    d_in("xownT", [D, HALF])
    d_in("xqT", [D, HALF])
    d_in("xkvT", [D, HALF])
    d_in_bf("w_qkv", [D, 3 * D])
    d_in("b_qkv", [3 * D])
    d_in_bf("w_proj", [D, D])
    d_in("b_proj", [D])
    d_in("gate_w", [D, 4])
    d_in("gate_b", [4])
    d_in_bf("sw1", [2, D, 4096])
    d_in("sb1", [2, 4096])
    d_in_bf("sw2", [2, 4096, D])
    d_in("sb2", [2, D])
    d_in_bf("rw1", [4, D, 4096])
    d_in("rb1", [4, 4096])
    d_in_bf("rw2", [4, 4096, D])
    d_in("rb2", [4, D])
    d_in("bqkv_pc", [128, 24])
    d_in("sb1_pc", [2, 128, 32])
    for k in host_consts():
        d_in_bf(k, host_consts()[k].shape)
    io["sc_A"] = nc.dram_tensor("sc_A", [D, HALF], BF16, kind="Internal").ap()
    io["x1out"] = nc.dram_tensor("x1out", [D, HALF], F32, kind="ExternalOutput").ap()

    ctx = Ctx(nc)
    x1_dmas = emit_block0(ctx, io, ph1_only=ph1_only, stage=stage)
    if not x1_dmas:
        ctx.pb.barrier()
    ctx.pb.waits_only("sync", x1_dmas[-1:])
    ctx.pb.realize()
    return nc


def stage_block0_inputs(inputs, core):
    b, h = core // 2, core % 2
    own = h * HALF
    x0 = np.asarray(inputs["L0_attn_x"])
    qidx, kidx = host_gather_indices(np.asarray(inputs["L0_attn_mapping"]), own)
    cm = {k: np.ascontiguousarray(v) for k, v in host_consts().items()}
    m = dict(
        xownT=np.ascontiguousarray(x0[b, own:own + HALF, :].T),
        xqT=np.ascontiguousarray(x0[b, qidx, :].T),
        xkvT=np.ascontiguousarray(x0[b, kidx, :].T),
        w_qkv=np.asarray(inputs["L0_attn_Wqkv"]).astype(BF),
        b_qkv=np.asarray(inputs["L0_attn_bqkv"]),
        w_proj=np.asarray(inputs["L0_attn_Wproj"]).astype(BF),
        b_proj=np.asarray(inputs["L0_attn_bproj"]),
        gate_w=np.asarray(inputs["L0_ffn_gate_w"]),
        gate_b=np.asarray(inputs["L0_ffn_gate_b"]),
        sw1=np.asarray(inputs["L0_ffn_sw1"]).astype(BF),
        sb1=np.asarray(inputs["L0_ffn_sb1"]),
        sw2=np.asarray(inputs["L0_ffn_sw2"]).astype(BF),
        sb2=np.asarray(inputs["L0_ffn_sb2"]),
        rw1=np.asarray(inputs["L0_ffn_rw1"]).astype(BF),
        rb1=np.asarray(inputs["L0_ffn_rb1"]),
        rw2=np.asarray(inputs["L0_ffn_rw2"]).astype(BF),
        rb2=np.asarray(inputs["L0_ffn_rb2"]),
        bqkv_pc=np.ascontiguousarray(
            np.asarray(inputs["L0_attn_bqkv"]).reshape(24, 128).T),
        sb1_pc=np.ascontiguousarray(
            np.asarray(inputs["L0_ffn_sb1"]).reshape(2, 32, 128).transpose(0, 2, 1)),
        **cm,
    )
    return m


def run_block0(inputs):
    nc = build_block0()
    in_maps = [stage_block0_inputs(inputs, c) for c in range(8)]
    res = run_bass_kernel_spmd(nc, in_maps, core_ids=list(range(8)))
    x1 = np.zeros((B, N, D), dtype=np.float32)
    for c in range(8):
        b, h = c // 2, c % 2
        x1[b, h * HALF:(h + 1) * HALF, :] = res.results[c]["x1out"].T
    return x1


def run_block1(inputs, x1):
    nc = build_block1()
    in_maps = [stage_block1_inputs(inputs, c, x1) for c in range(8)]
    res = run_bass_kernel_spmd(nc, in_maps, core_ids=list(range(8)))
    out = np.zeros((B, N, D), dtype=np.float32)
    for c in range(8):
        b, h = c // 2, c % 2
        out[b, h * HALF:(h + 1) * HALF, :] = res.results[c]["out"].T
    return out


def build_fused():
    nc = bass.Bass("TRN2", target_bir_lowering=False, num_devices=8)
    io = {}

    def d_in(name, shape, dtype=F32):
        io[name] = nc.dram_tensor(name, list(shape), dtype,
                                  kind="ExternalInput").ap()

    for name, shape in [
            ("xownT", [D, HALF]), ("xqT", [D, HALF]), ("xkvT", [D, HALF]),
            ("b_qkv", [3 * D]),
            ("b_proj", [D]), ("gate_w", [D, 4]), ("gate_b", [4]),
            ("sb1", [2, 4096]),
            ("sb2", [2, D]), ("rb1", [4, 4096]),
            ("rb2", [4, D]),
            ("bqkv_pc", [128, 24]), ("sb1_pc", [2, 128, 32]),
            ("bg", [D]),
            ("bp", [D]), ("bg_pc", [128, 8]),
            ("bp_pc", [128, 8]), ("szero", [128, 1]),
            ("lng_pc", [2, 128, 8]), ("lnb_pc", [2, 128, 8]),
            ("b1p", [2, 5504]),
            ("b2f", [2, D]),
            ("qs_row", [2, 1, 512]), ("ks_row", [2, 1, 512]),
            ("rot", [4, 4096, 64])]:
        d_in(name, shape)
    for name, shape in [
            ("w_qkv", [D, 3 * D]), ("w_proj", [D, D]),
            ("sw1", [2, D, 4096]), ("sw2", [2, 4096, D]),
            ("rw1", [4, D, 4096]), ("rw2", [4, 4096, D]),
            ("w_qkv1", [D, 3 * D]), ("wg", [D, D]), ("wp", [D, D]),
            ("wqkv_f", [2, D, 1536]), ("wo_f", [2, 512, D]),
            ("w1p", [2, D, 5504]), ("w2p", [2, 2752, D])]:
        d_in(name, shape, BF16)
    d_in("masks", [4, 1024, 512], BF16)
    d_in("masksT", [4, 512, 1024], BF16)
    d_in("nonce", [1, 1], I32)
    for k, v in host_consts().items():
        d_in(k, v.shape, BF16)
    io["sc_A"] = nc.dram_tensor("sc_A", [D, HALF], BF16, kind="Internal").ap()
    for nm, shp, dt in [("sc_xn1b", [D, 4104], BF16), ("sc_kp", [4096, D], BF16),
                        ("sc_v", [4096, 1040], BF16), ("sc_s1", [D, HALF], F32),
                        ("sc_xa", [D, 3588], F32), ("sc_xab", [D, 3588], BF16),
                        ("sc_y1", [D, 3080], F32), ("sc_y1b", [D, 3080], BF16),
                        ("sc_xb", [D, 2564], F32), ("sc_xbb", [D, 2564], BF16)]:
        io[nm] = nc.dram_tensor(nm, shp, dt, kind="Internal").ap()
    io["out"] = nc.dram_tensor("out", [D, HALF], F32, kind="ExternalOutput").ap()
    x1loc = nc.dram_tensor("x1loc", [D, HALF], F32, kind="Internal").ap()
    x1prt = nc.dram_tensor("x1prt", [D, HALF], F32, kind="Internal").ap()
    x1sh = nc.dram_tensor("x1sh", [8, D, HALF], F32, kind="Internal",
                          addr_space="Shared").ap()
    flags = nc.dram_tensor("flags", [8, 16], I32, kind="Internal",
                           addr_space="Shared").ap()

    ctx = Ctx(nc)
    build_fused.last_ctx = ctx
    pb = ctx.pb
    pb.want_pid = True
    io["x1out"] = x1loc
    io["x1self"] = x1loc
    io["x1part"] = x1prt
    x1_dmas = emit_block0(ctx, io)

    NB = ctx.sb_at("NB", 206656, [1, 1], I32)
    PFLG = ctx.sb_at("PFLG", 206688, [1, 1], I32)
    exs = nc.alloc_semaphore("exs")
    expoll = nc.alloc_semaphore("expoll")

    def spin(sy):
        sy.dma_start(out=NB, in_=io["nonce"]).then_inc(exs, 16)
        sy.dma_start(out=x1sh[pb.pid], in_=x1loc).then_inc(exs, 16)
        sy.wait_ge(exs, 32)
        sy.dma_start(out=flags[pb.pid, 0:1].unsqueeze(0),
                     in_=NB).then_inc(exs, 16)
        sy.wait_ge(exs, 48)
        with (sy.register("rn") as rn, sy.register("rf") as rf,
              sy.register("rit") as rit, sy.register("rcont") as rcont,
              sy.register("rexp") as rexp, sy.register("rtmp") as rtmp):
            sy.load(rn, NB)
            sy.reg_mov(rit, 0)
            sy.reg_mov(rcont, 1)
            sy.reg_mov(rexp, 0)
            with sy.While(rcont):
                sy.reg_add(rexp, rexp, 16)
                sy.dma_start(out=PFLG, in_=flags[pb.ppid, 0:1].unsqueeze(0)
                             ).then_inc(expoll, 16)
                sy.wait_ge(expoll, rexp)
                sy.load(rf, PFLG)
                sy.reg_add(rit, rit, 1)
                sy.reg_alu(rcont, rf, rn, ALU.not_equal)
                sy.reg_alu(rtmp, rit, 200000, ALU.is_lt)
                sy.reg_alu(rcont, rcont, rtmp, ALU.bitwise_and)
        sy.dma_start(out=x1prt, in_=x1sh[pb.ppid]).then_inc(exs, 16)
        sy.wait_ge(exs, 64)
    pb.raw("sync", spin, deps=x1_dmas)

    out_dmas = emit_block1(ctx, io, stage=19)
    pb.waits_only("sync", out_dmas[-1:])
    pb.realize()
    return nc


def stage_fused_inputs(inputs, core, nonce):
    m = stage_block0_inputs(inputs, core)
    m.update(stage_block1_inputs(inputs, core, None))
    m["nonce"] = nonce
    return m


def kernel(**inputs):
    nc = build_fused()
    nonce = np.array([[int(np.random.default_rng().integers(1 << 20, 1 << 30))]],
                     dtype=np.int32)
    in_maps = [stage_fused_inputs(inputs, c, nonce) for c in range(8)]
    res = run_bass_kernel_spmd(nc, in_maps, core_ids=list(range(8)))
    out = np.zeros((B, N, D), dtype=np.float32)
    for c in range(8):
        b, h = c // 2, c % 2
        out[b, h * HALF:(h + 1) * HALF, :] = res.results[c]["out"].T
    return out


TOFF = 2049
GSPAN = 2560
XA_OFF = 1536
Y1_OFF = 1028
XB_OFF = 512

SB2 = dict(
    QTH=0,
    KTH=40960,
    VTOK=81920,
    WQS=102400,
    WKS=110592,
    WVS=118784,
    YT=126976,
    EB=135296,
    MSK2=143488,
    OT=145536,
    OTMP=149632,
    M2=150656,
    SCR=167040,
)



def emit_block1(ctx, io, stage=19):
    nc, pb = ctx.nc, ctx.pb
    sa = ctx.sb_at
    QTH = sa("QTH", SB2["QTH"], [64, 8 * GSPAN], BF16)
    KTH = sa("KTH", SB2["KTH"], [64, 8 * GSPAN], BF16)
    VTOK = sa("VTOK", SB2["VTOK"], [128, 20 * 512], BF16)
    WQS = sa("WQS", SB2["WQS"], [128, 4096], BF16)
    WKS = sa("WKS", SB2["WKS"], [128, 4096], BF16)
    WVS = sa("WVS", SB2["WVS"], [128, 4096], BF16)
    YT = sa("YT", SB2["YT"], [128, 4160], BF16)
    EB = sa("EB", SB2["EB"], [128, 4096], BF16)
    MSK2 = sa("MSK2", SB2["MSK2"], [128, 1024], BF16)
    OT = sa("OT", SB2["OT"], [128, 2048], BF16)
    OTMP = sa("OTMP", SB2["OTMP"], [64, 512], BF16)
    MSKT = sa("MSKT", 202560, [128, 2048], BF16)
    RAW2 = sa("RAW2", SB2["VTOK"], [128, 4096], F32)
    RAW2F = sa("RAW2F", SB2["QTH"], [128, 4104], F32)
    HP = sa("HP", SB2["QTH"] + 16448, [128, 11264], BF16)
    S1T = sa("S1T", SB2["KTH"], [128, 4096], F32)
    XNown = sa("XNown", SB2["QTH"], [128, 16384], BF16)
    QP = sa("QP", SB2["KTH"], [128, 16384], BF16)
    m = SB2["M2"]

    def mget(name, shape, dtype):
        nonlocal m
        nbytes = int(np.prod(shape[1:])) * mybir.dt.size(dtype)
        ap = sa(name, m, shape, dtype)
        m += (nbytes + 31) // 32 * 32
        assert m <= 167040, (name, m)
        return ap

    KVS = mget("KVS", [128, 2048], BF16)
    scr = SB2["SCR"]
    GT2 = sa("GT2", scr, [128, 512], F32)
    QTMP = sa("QTMP", scr + 2048, [128, 512], F32)
    ETMP = sa("ETMP", scr + 4096, [128, 512], F32)
    RNR = sa("RNR", scr + 6144, [128, 512], F32)
    VPB = sa("VPB", scr + 8192, [128, 1040], BF16)
    KPB = sa("KPB", scr + 10272, [128, 512], BF16)
    RNB = sa("RNB", scr + 11296, [128, 512], BF16)
    QON = sa("QON", SB2["EB"], [128, 4096], BF16)
    ONESF = mget("ONESF", [128, 128], BF16)
    S2R = mget("S2R", [128, 16], F32)
    BQS = mget("BQS", [128, 1024], BF16)
    MXR = mget("MXR", [1, 512], BF16)
    NMC = mget("NMC", [128, 32], F32)
    NMCB = mget("NMCB", [128, 32], BF16)
    ROT = mget("ROT", [128, 256], F32)
    GPC = mget("GPC", [128, 8], F32)
    BPC = mget("BPC", [128, 8], F32)
    LNG = mget("LNG", [128, 16], F32)
    LNB = mget("LNB", [128, 16], F32)
    B2R = mget("B2R", [1, 1024], BF16)
    B1R = mget("B1R", [1, 128], BF16)
    bgr_off = m
    BGR = mget("BGR", [1, 1024], BF16)
    RESF = sa("RESF", bgr_off, [128, 512], F32)
    m = bgr_off + 2048
    QSR = mget("QSR", [1, 1024], BF16)
    SZC = mget("SZC", [128, 1], F32)
    ZC8 = mget("ZC8", [128, 8], BF16)
    TR1 = mget("TR1", [128, 128], BF16)

    ones_row = ctx.sb["ONES1x128"]
    onesf_col = ctx.sb["ONES128x1F"]
    ones_col = ctx.sb["ONES128x1"]
    ident = ctx.sb["IDENT"]
    ones512 = ctx.sb["ONES1x512"]

    pb.barrier()
    t_in = []
    t_in.append(pb.comp("gpsimd", lambda e: e.memset(ONESF, 1.0)))
    t_in.append(pb.comp("gpsimd", lambda e: e.memset(ZC8, 0.0)))
    t_in.append(pb.dma("gpsimd", SZC, io["szero"], channel="c2"))
    t_in.append(pb.dma("gpsimd", GPC, io["bg_pc"], channel="c2"))
    t_in.append(pb.dma("gpsimd", BPC, io["bp_pc"], channel="c2"))
    t_in.append(pb.dma("gpsimd", LNG.rearrange("p (l c) -> p l c", l=2),
                       io["lng_pc"].rearrange("l p c -> p l c"), channel="c2"))
    t_in.append(pb.dma("gpsimd", LNB.rearrange("p (l c) -> p l c", l=2),
                       io["lnb_pc"].rearrange("l p c -> p l c"), channel="c2"))
    t_in.append(pb.dma("gpsimd", BGR, io["bg"].unsqueeze(0), channel="c2"))
    pb.barrier()

    XN1B = io["sc_xn1b"]

    pb.phase = "b1.ph3a"
    lnfree = []
    ln_last = None
    for k in range(8):
        c0 = 1 + k * 512
        src = io["x1part"] if k < 4 else io["x1self"]
        s0 = (c0 - 1) % 2048
        dep = pb.dma("sync", RAW2.rearrange("p (c m) -> p c m", c=8),
                     (lambda src=src, s0=s0:
                      rearr_kpm(apv(src))[:, :, s0:s0 + 512]),
                     deps=lnfree + [ln_last],
                     channel="rawf")
        t_xn, lnfree = emit_ln(ctx, lambda c: RAW2[:, c * MT:(c + 1) * MT], dep,
                               lambda c: YT[:, c * MT:(c + 1) * MT])
        ln_last = pb.dma("sync", rearr_kpm(XN1B)[:, :, c0:c0 + 512],
                         YT[:, 0:4096].rearrange("p (c m) -> p c m", c=8),
                         deps=[t_xn, ln_last], channel="xn1o")
    z0 = pb.dma("sync", rearr_kpm(XN1B)[:, :, 0:1],
                ZC8.rearrange("p (c m) -> p c m", m=1), deps=[ln_last],
                channel="zc", allow_slow_non_contiguous=True)
    xn1_done = [z0, ln_last]
    pb.barrier()
    if stage <= 11:
        return []

    pb.phase = "b1.ph3b"
    vones = pb.comp("gpsimd", lambda e: e.memset(VPB, 1.0))
    kvz = pb.comp("gpsimd", lambda e: e.memset(KVS, 0.0))
    slabs_tok = None
    xt_dep = None
    kp_dma = None
    for nh in range(2):
        d0 = [slabs_tok]
        tw_g = pb.dma("gpsimd", WQS, rearr_kpm(io["wg"])[:, :, nh * 512:(nh + 1) * 512],
                      deps=d0, channel="sla")
        tw_k = pb.dma("gpsimd", WKS,
                      rearr_kpm(io["w_qkv1"])[:, :, 1024 + nh * 512:1536 + nh * 512],
                      deps=d0, channel="slb")
        tw_v = pb.dma("gpsimd", WVS,
                      rearr_kpm(io["w_qkv1"])[:, :, 2048 + nh * 512:2560 + nh * 512],
                      deps=d0, channel="u1")
        xt_tok = None
        for k8 in range(8):
            c0 = 1 + k8 * 512
            xt_tok = pb.dma("sync", YT[:, 0:4096].rearrange("p (c m) -> p c m", c=8),
                            rearr_kpm(XN1B)[:, :, c0:c0 + 512],
                            deps=[xt_dep] + xn1_done, channel="rawf")
            for mc in range(4):
                l0 = k8 * 512 + mc * 128
                lt = lambda kc, mc=mc: YT[:, kc * 512 + mc * 128:kc * 512 + (mc + 1) * 128]
                tg = mmg(ctx, 0, [(ctx.bank(0), lt(kc), WQS[:, kc * 512:(kc + 1) * 512],
                                   None) for kc in range(NCH)] +
                         [(ctx.bank(0), ones_row, BGR[:, nh * 512:(nh + 1) * 512], None)],
                         deps=[tw_g, xt_tok])
                tk = mmg(ctx, 1, [(ctx.bank(1), lt(kc), WKS[:, kc * 512:(kc + 1) * 512],
                                   None) for kc in range(NCH)],
                         deps=[tw_k, xt_tok])
                tv = mmg(ctx, 2, [(ctx.bank(2), lt(kc), WVS[:, kc * 512:(kc + 1) * 512],
                                   None) for kc in range(NCH)],
                         deps=[tw_v, xt_tok])
                xt_dep = tv
                eg = pb.comp("scalar", lambda e: e.activation(GT2, ctx.bank(0),
                                                              AF.Sigmoid),
                             deps=[tg, kp_dma])
                ctx.bank_tok[0] = eg
                e1 = pb.comp("vector", lambda e: e.tensor_tensor(
                    QTMP, ctx.bank(1), GT2, op=ALU.mult), deps=[tk, eg])
                ctx.bank_tok[1] = e1
                e2 = pb.comp("vector", lambda e: e.tensor_scalar_min(
                    ETMP, QTMP, 0.0), deps=[e1])
                e3 = pb.comp("scalar", lambda e: e.activation(ETMP, ETMP, AF.Exp),
                             deps=[e2])
                e4 = pb.comp("vector", lambda e: e.tensor_scalar_max(
                    QTMP, QTMP, 0.0), deps=[e3, e1])
                e5 = pb.comp("vector", lambda e: e.tensor_add(KPB, ETMP, QTMP),
                             deps=[e4, e3])
                ev = pb.comp("vector", lambda e, nh=nh: e.tensor_copy(
                    VPB[:, nh * 520:(nh + 1) * 520]
                    .rearrange("p (h c) -> p h c", c=65)[:, :, 0:64],
                    ctx.bank(2).rearrange("p (h c) -> p h c", c=64)),
                    deps=[tv, kp_dma])
                ctx.bank_tok[2] = ev
                kp_dma = pb.dma("sync", io["sc_kp"][l0:l0 + 128,
                                                    nh * 512:(nh + 1) * 512],
                                KPB, deps=[e5], channel="kpo")
                kp_dma = pb.dma("sync", io["sc_v"][l0:l0 + 128,
                                                   nh * 520:(nh + 1) * 520],
                                VPB[:, nh * 520:(nh + 1) * 520],
                                deps=[ev, vones], channel="kpo")
        slabs_tok = kp_dma
    kv_evs = None
    for hg in range(4):
        t_ld = None
        kv_toks = {}
        for chunk in range(32):
            l0 = chunk * 128
            tka = pb.dma("sync", YT[:, 0:1024], io["sc_kp"][l0:l0 + 128, :],
                         deps=[t_ld, kp_dma, kv_evs, xt_dep], channel="rawf")
            tvb = pb.dma("sync", YT[:, 1024:2064],
                         io["sc_v"][l0:l0 + 128, :],
                         deps=[t_ld, kp_dma, kv_evs, xt_dep], channel="rawf")
            for hl in range(4):
                h = hg * 4 + hl
                bk = 4 + hl
                kv_toks[hl] = mmg(
                    ctx, bk,
                    [(ctx.bank(bk)[0:64, 0:65],
                      YT[:, h * 64:(h + 1) * 64],
                      YT[:, 1024 + h * 65:1024 + (h + 1) * 65], None)],
                    deps=[tvb], cont=(chunk > 0))
            t_ld = kv_toks[3]
        evs = []
        for hl in range(4):
            h = hg * 4 + hl
            evs.append(pb.comp("scalar", lambda e, h=h, hl=hl: e.copy(
                KVS[0:64, h * 128:h * 128 + 65],
                ctx.bank(4 + hl)[0:64, 0:65]), deps=[kv_toks[hl]]))
            ctx.bank_tok[4 + hl] = evs[-1]
        kv_evs = evs[-1]
    kvsw = pb.dma("sync", KVS[64:128, :], KVS[0:64, :], deps=[kv_evs, kvz],
                  channel="zc")
    pb.barrier()
    if stage <= 12:
        return []

    pb.phase = "b1.ph3c"
    xo_tok = pb.dma("sync", XNown.rearrange("p (c m) -> p c m", c=8),
                    rearr_kpm(XN1B)[:, :, TOFF:TOFF + 2048], channel="rawf")
    s1_dmas = []
    prev_f5 = None
    a1_last = None
    for mi in range(NMT):
        for ni in range(NCH):
            sl = (ctx.sb["SLA"], ctx.sb["SLB"])[ni % 2]
            ch = ("sla", "slb")[ni % 2]
            tldq = pb.dma("gpsimd", sl,
                          rearr_kpm(io["w_qkv1"])[:, :, ni * 128:(ni + 1) * 128],
                          deps=[ctx.slab_last[ch]], channel=ch)
            bk = ni % 2
            tmq = mmg(ctx, bk, [(ctx.bank(bk),
                                 sl[:, kc * 128:(kc + 1) * 128],
                                 XNown[:, kc * 2048 + mi * MT:kc * 2048 + (mi + 1) * MT],
                                 None) for kc in range(NCH)],
                      deps=[tldq, xo_tok])
            eq = pb.comp("scalar", lambda e, bk=bk: e.copy(QTMP, ctx.bank(bk)),
                         deps=[tmq, prev_f5,
                               s1_dmas[-1] if s1_dmas else None])
            ctx.bank_tok[bk] = eq
            tldg = pb.dma("gpsimd", sl,
                          rearr_kpm(io["wg"])[:, :, ni * 128:(ni + 1) * 128],
                          deps=[tmq], channel=ch)
            tmg = mmg(ctx, bk, [(ctx.bank(bk),
                                 sl[:, kc * 128:(kc + 1) * 128],
                                 XNown[:, kc * 2048 + mi * MT:kc * 2048 + (mi + 1) * MT],
                                 None) for kc in range(NCH)],
                      deps=[tldg, eq])
            ctx.slab_last[ch] = tmg
            gg = pb.comp("scalar", lambda e, bk=bk, ni=ni: e.activation(
                GT2, ctx.bank(bk), AF.Sigmoid, bias=GPC[:, ni:ni + 1]),
                deps=[tmg, prev_f5])
            ctx.bank_tok[bk] = gg
            f1 = pb.comp("vector", lambda e: e.tensor_mul(QTMP, QTMP, GT2),
                         deps=[gg, eq])
            f2 = pb.comp("vector", lambda e: e.tensor_scalar_min(ETMP, QTMP, 0.0),
                         deps=[f1])
            f3 = pb.comp("scalar", lambda e: e.activation(ETMP, ETMP, AF.Exp),
                         deps=[f2])
            f4 = pb.comp("vector", lambda e: e.tensor_scalar_max(QTMP, QTMP, 0.0),
                         deps=[f3, f1])
            f5 = pb.comp("vector", lambda e, ni=ni, mi=mi: e.tensor_add(
                QP[:, ni * 2048 + mi * MT:ni * 2048 + (mi + 1) * MT], ETMP, QTMP),
                deps=[f4])
            prev_f5 = f5
        qp_tok = f5
        prev_qon = None
        for hp in range(NCH):
            b_qo = 2
            b_nm = 3
            terms = []
            for h2 in range(2):
                h = 2 * hp + h2
                b0 = h2 * 64
                terms.append((ctx.bank(b_qo)[b0:b0 + 64, 0:512],
                              KVS[b0:b0 + 64, h * 128:h * 128 + 64],
                              QP[b0:b0 + 64, hp * 2048 + mi * MT:hp * 2048 + (mi + 1) * MT],
                              (b0, b0)))
            t_qo = mmg(ctx, b_qo, terms, deps=[qp_tok, kvsw], independent=True)
            nterms = []
            for h2 in range(2):
                h = 2 * hp + h2
                b0 = h2 * 64
                nterms.append((ctx.bank(b_nm)[b0:b0 + 1, 0:512],
                               KVS[b0:b0 + 64, h * 128 + 64:h * 128 + 65],
                               QP[b0:b0 + 64, hp * 2048 + mi * MT:hp * 2048 + (mi + 1) * MT],
                               (b0, b0)))
            t_nm = mmg(ctx, b_nm, nterms, deps=[qp_tok, kvsw], independent=True)
            r1 = pb.comp("vector", lambda e, b_nm=b_nm: e.tensor_scalar_add(
                RNR[0:1, :], ctx.bank(b_nm)[0:1, 0:512], 1e-8), deps=[t_nm])
            r2 = pb.comp("vector", lambda e, b_nm=b_nm: e.tensor_scalar_add(
                RNR[64:65, :], ctx.bank(b_nm)[64:65, 0:512], 1e-8), deps=[t_nm])
            ctx.bank_tok[b_nm] = r2
            r3 = pb.comp("vector", lambda e: e.reciprocal(RNR[0:1, :], RNR[0:1, :]),
                         deps=[r1])
            r4 = pb.comp("vector", lambda e: e.reciprocal(RNR[64:65, :], RNR[64:65, :]),
                         deps=[r2])
            r5 = pb.comp("vector", lambda e: e.tensor_copy(RNB[0:1, :], RNR[0:1, :]),
                         deps=[r3])
            r6 = pb.comp("vector", lambda e: e.tensor_copy(RNB[64:65, :], RNR[64:65, :]),
                         deps=[r4])
            bterms = [(ctx.bank(b_nm)[0:64, 0:512], ONESF[0:1, 0:64], RNB[0:1, :],
                       (0, 0)),
                      (ctx.bank(b_nm)[64:128, 0:512], ONESF[64:65, 0:64],
                       RNB[64:65, :], (64, 64))]
            t_br = mmg(ctx, b_nm, bterms, deps=[r5, r6], independent=True)
            qoc = pb.comp("scalar", lambda e, b_qo=b_qo: e.copy(QTMP, ctx.bank(b_qo)),
                          deps=[t_qo, prev_qon])
            ctx.bank_tok[b_qo] = qoc
            qon = pb.comp("vector", lambda e, hp=hp, b_nm=b_nm: e.tensor_tensor(
                QON[:, hp * 512:(hp + 1) * 512], QTMP, ctx.bank(b_nm), op=ALU.mult),
                deps=[qoc, t_br, a1_last])
            ctx.bank_tok[b_nm] = qon
            prev_qon = qon
        x1_dep = pb.dma("sync", RAW2.rearrange("p (c m) -> p c m", c=8),
                        (lambda xs=io["x1self"], mi=mi:
                         rearr_kpm(apv(xs))[:, :, mi * MT:(mi + 1) * MT]),
                        deps=[xo_tok, s1_dmas[-1] if s1_dmas else None],
                        channel="raw2f")
        for oc in range(NCH):
            sl = (ctx.sb["SLA"], ctx.sb["SLB"])[oc % 2]
            ch = ("sla", "slb")[oc % 2]
            tld = pb.dma("gpsimd", sl,
                         rearr_kpm(io["wp"])[:, :, oc * 128:(oc + 1) * 128],
                         deps=[ctx.slab_last[ch]], channel=ch)
            bk = 6 + oc % 2
            terms = [(ctx.bank(bk), sl[:, kc * 128:(kc + 1) * 128],
                      QON[:, kc * 512:(kc + 1) * 512], None) for kc in range(NCH)]
            tm = mmg(ctx, bk, terms, deps=[tld, qon])
            ctx.slab_last[ch] = tm
            a1_last = tm
            te = pb.comp("vector", lambda e, bk=bk, oc=oc: e.scalar_tensor_tensor(
                ETMP, ctx.bank(bk), 1.0, RAW2[:, oc * MT:(oc + 1) * MT],
                op0=ALU.mult, op1=ALU.add),
                deps=[tm, x1_dep, s1_dmas[-1] if s1_dmas else None])
            tb = pb.comp("vector", lambda e, oc=oc: e.tensor_scalar_add(
                ETMP, ETMP, BPC[:, oc:oc + 1]), deps=[te])
            ctx.bank_tok[bk] = tb
            dd = pb.dma("sync",
                        rearr_kpm(io["sc_s1"])[:, oc, mi * MT:(mi + 1) * MT],
                        ETMP, deps=[tb], channel="s1o")
            s1_dmas.append(dd)
    pb.barrier()
    if stage <= 13:
        return []

    envs = [
        dict(ysrc_b=XN1B, ysrc_f=None, yoff=TOFF,
             groups=[(-2048, 512, [-3, -2, -1, 0]), (0, 2048, [1, 2, 3])],
             wo_f32=io["sc_xa"], wo_b16=io["sc_xab"], wo_off=XA_OFF,
             ffn_tiles=[(-1025, -513), (-513, -1), (-1, 511), (511, 1023),
                        (1023, 1535), (1535, 2047), (2047, 2048)],
             ffn_f32=io["sc_y1"], ffn_b16=io["sc_y1b"], fo_off=Y1_OFF,
             final=False, wq_free=None, ysrc_ready=None),
        dict(ysrc_b=io["sc_y1b"], ysrc_f=io["sc_y1"], yoff=Y1_OFF,
             groups=[(-1024, 512, [-1, 0]), (0, 2048, [1, 2, 3])],
             wo_f32=io["sc_xb"], wo_b16=io["sc_xbb"], wo_off=XB_OFF,
             ffn_tiles=[(0, 512), (512, 1024), (1024, 1536), (1536, 2048)],
             final=True, wq_free=None, ysrc_ready=None),
    ]
    out_dmas = None
    if stage == 131:
        envs[0]["qkv_only"] = True
        emit_l1ffn_layer(ctx, io, 0, envs[0])
        pb.barrier()
        return []
    for l in (0, 1):
        if stage <= 13 + l:
            return []
        out_dmas = emit_l1ffn_layer(ctx, io, l, envs[l])
        envs[1]["wq_free"] = (ctx.pb.psem["tensor"], ctx.pb.cnt["tensor"])
        pb.barrier()
    return out_dmas


def build_block1(stage=19):
    nc = bass.Bass("TRN2", target_bir_lowering=False, num_devices=8)
    io = {}

    def d_in(name, shape, dtype=F32):
        io[name] = nc.dram_tensor(name, list(shape), dtype, kind="ExternalInput").ap()

    d_in("x1self", [D, HALF])
    d_in("x1part", [D, HALF])
    d_in("w_qkv1", [D, 3 * D], BF16)
    d_in("wg", [D, D], BF16)
    d_in("bg", [D])
    d_in("wp", [D, D], BF16)
    d_in("bp", [D])
    d_in("bg_pc", [128, 8])
    d_in("bp_pc", [128, 8])
    d_in("szero", [128, 1])
    d_in("lng_pc", [2, 128, 8])
    d_in("lnb_pc", [2, 128, 8])
    d_in("wqkv_f", [2, D, 1536], BF16)
    d_in("wo_f", [2, 512, D], BF16)
    d_in("w1p", [2, D, 5504], BF16)
    d_in("b1p", [2, 5504])
    d_in("w2p", [2, 2752, D], BF16)
    d_in("b2f", [2, D])
    d_in("qs_row", [2, 1, 512])
    d_in("ks_row", [2, 1, 512])
    d_in("rot", [4, 4096, 64])
    d_in("masks", [4, 1024, 512], BF16)
    d_in("masksT", [4, 512, 1024], BF16)
    for k, v in host_consts().items():
        d_in(k, v.shape, BF16)
    io["sc_xn1b"] = nc.dram_tensor("sc_xn1b", [D, 4104], BF16, kind="Internal").ap()
    io["sc_kp"] = nc.dram_tensor("sc_kp", [4096, D], BF16, kind="Internal").ap()
    io["sc_v"] = nc.dram_tensor("sc_v", [4096, 1040], BF16, kind="Internal").ap()
    io["sc_s1"] = nc.dram_tensor("sc_s1", [D, HALF], F32, kind="Internal").ap()
    io["sc_xa"] = nc.dram_tensor("sc_xa", [D, 3588], F32, kind="Internal").ap()
    io["sc_xab"] = nc.dram_tensor("sc_xab", [D, 3588], BF16, kind="Internal").ap()
    io["sc_y1"] = nc.dram_tensor("sc_y1", [D, 3080], F32, kind="Internal").ap()
    io["sc_y1b"] = nc.dram_tensor("sc_y1b", [D, 3080], BF16, kind="Internal").ap()
    io["sc_xb"] = nc.dram_tensor("sc_xb", [D, 2564], F32, kind="Internal").ap()
    io["sc_xbb"] = nc.dram_tensor("sc_xbb", [D, 2564], BF16, kind="Internal").ap()
    io["out"] = nc.dram_tensor("out", [D, HALF], F32, kind="ExternalOutput").ap()
    if stage <= 13:
        io["dbg_s1"] = nc.dram_tensor("dbg_s1", [D, HALF], F32,
                                      kind="ExternalOutput").ap()
        io["sc_s1"] = io["dbg_s1"]

    ctx = Ctx(nc)
    sa = ctx.sb_at
    IDENT = sa("IDENT", SBMAP["IDENT"], [128, 128], BF16)
    ONES1x128 = sa("ONES1x128", SBMAP["ONES1x128"], [1, 128], BF16)
    ONES1x512 = sa("ONES1x512", SBMAP["ONES1x512"], [1, 512], BF16)
    sa("ONES128x1", SBMAP["ONES128x1"], [128, 1], BF16)
    sa("ONES128x1F", SBMAP["ONES128x1F"], [128, 1], F32)
    sa("SLA", SBMAP["SLA"], [128, 1024], BF16)
    sa("SLB", SBMAP["SLB"], [128, 1024], BF16)
    pb = ctx.pb
    pb.comp("gpsimd", lambda e: e.memset(ctx.sb["ONES128x1"], 1.0))
    pb.comp("gpsimd", lambda e: e.memset(ctx.sb["ONES128x1F"], 1.0))
    pb.dma("gpsimd", IDENT, io["cn_ident"], channel="const")
    pb.dma("gpsimd", ONES1x128, io["cn_ones_1x128"], channel="const")
    pb.dma("gpsimd", ONES1x512, io["cn_ones_1x512"], channel="const")
    out_dmas = emit_block1(ctx, io, stage=stage)
    if out_dmas:
        ctx.pb.waits_only("sync", out_dmas[-1:])
    else:
        ctx.pb.barrier()
    ctx.pb.realize()
    return nc


def host_rotary(own_start):
    t = (np.arange(-2048, 2048) + own_start) % N
    tt = t.astype(np.float64)
    half_idx = np.arange(0, 64, 2, dtype=np.float64)
    inv_freq = 1.0 / 10000.0 ** (half_idx / 64)
    base = (half_idx + 0.4 * 64) / (1.4 * 64)
    power = (tt - N // 2) / float(WINDOW // 2)
    fr = tt[:, None] * inv_freq[None, :]
    sc = base[None, :] ** power[:, None]
    cos = np.cos(np.concatenate([fr, fr], 1))
    sin = np.sin(np.concatenate([fr, fr], 1))
    scale = np.concatenate([sc, sc], 1)
    sgn = np.concatenate([-np.ones((1, 32)), np.ones((1, 32))], 1)
    return np.stack([cos * scale, sin * scale * sgn,
                     cos / scale, sin / scale * sgn]).astype(np.float32)


def host_l1_masks(is_second_half):
    sarr = np.arange(WINDOW)[:, None]
    marr = np.arange(WINDOW)[None, :]
    prev = np.where(sarr >= marr, 0.0, NEG).astype(np.float32)
    curr = np.where(sarr <= marr, 0.0, NEG).astype(np.float32)
    normal = np.concatenate([prev, curr], axis=0)
    masked0 = np.concatenate([np.full((WINDOW, WINDOW), NEG, np.float32), curr],
                             axis=0)
    out = np.stack([normal] * 4, axis=0)
    if not is_second_half:
        out[0] = masked0
    return out.astype(BF)


def host_l1_masksT(is_second_half):
    m = np.asarray(host_l1_masks(is_second_half), np.float32)
    return np.ascontiguousarray(m.transpose(0, 2, 1)).astype(BF)


def host_geglu_permute(w1, b1):
    L = w1.shape[0]
    w1p = np.zeros((L, 1024, 5504), w1.dtype)
    b1p = np.zeros((L, 5504), b1.dtype)
    c, fc = 0, 0
    while c < HID:
        wdt = min(64, HID - c)
        w1p[:, :, fc * 128:fc * 128 + wdt] = w1[:, :, c:c + wdt]
        w1p[:, :, fc * 128 + 64:fc * 128 + 64 + wdt] = w1[:, :, HID + c:HID + c + wdt]
        b1p[:, fc * 128:fc * 128 + wdt] = b1[:, c:c + wdt]
        b1p[:, fc * 128 + 64:fc * 128 + 64 + wdt] = b1[:, HID + c:HID + c + wdt]
        c += 64
        fc += 1
    return w1p, b1p


def host_w2_pad(w2):
    L = w2.shape[0]
    w2p = np.zeros((L, 2752, 1024), w2.dtype)
    c, fc = 0, 0
    while c < HID:
        wdt = min(64, HID - c)
        w2p[:, fc * 64:fc * 64 + wdt] = w2[:, c:c + wdt]
        c += 64
        fc += 1
    return w2p


def stage_block1_inputs(inputs, core, x1):
    b, h = core // 2, core % 2
    own = h * HALF
    cm = {k: np.ascontiguousarray(v) for k, v in host_consts().items()}
    w1p, b1p = host_geglu_permute(np.asarray(inputs["L1_ffn_w1"]),
                                  np.asarray(inputs["L1_ffn_b1"]))
    qs = np.asarray(inputs["L1_ffn_q_scale"])
    ks = np.asarray(inputs["L1_ffn_k_scale"])
    m = dict(
        w_qkv1=np.asarray(inputs["L1_attn_Wqkv"]).astype(BF),
        wg=np.asarray(inputs["L1_attn_Wg"]).astype(BF),
        bg=np.asarray(inputs["L1_attn_bg"]),
        wp=np.asarray(inputs["L1_attn_Wp"]).astype(BF),
        bp=np.asarray(inputs["L1_attn_bp"]),
        bg_pc=np.ascontiguousarray(np.asarray(inputs["L1_attn_bg"]).reshape(8, 128).T),
        bp_pc=np.ascontiguousarray(np.asarray(inputs["L1_attn_bp"]).reshape(8, 128).T),
        szero=np.full((128, 1), 0.0 if h == 0 else 1.0, np.float32),
        lng_pc=np.ascontiguousarray(np.asarray(inputs["L1_ffn_ln_g"])
                                    .reshape(2, 8, 128).transpose(0, 2, 1)),
        lnb_pc=np.ascontiguousarray(np.asarray(inputs["L1_ffn_ln_b"])
                                    .reshape(2, 8, 128).transpose(0, 2, 1)),
        wqkv_f=np.asarray(inputs["L1_ffn_wqkv"]).astype(BF),
        wo_f=np.asarray(inputs["L1_ffn_wo"]).astype(BF),
        w1p=w1p.astype(BF),
        b1p=b1p,
        w2p=host_w2_pad(np.asarray(inputs["L1_ffn_w2"])).astype(BF),
        b2f=np.asarray(inputs["L1_ffn_b2"]),
        qs_row=np.ascontiguousarray(
            np.tile(qs, (1, 8)).reshape(2, 1, 512) * QK_SCALE),
        ks_row=np.ascontiguousarray(np.tile(ks, (1, 8)).reshape(2, 1, 512)),
        rot=host_rotary(own),
        masks=host_l1_masks(h == 1),
        masksT=host_l1_masksT(h == 1),
        **cm,
    )
    if x1 is not None:
        m["x1self"] = np.ascontiguousarray(x1[b, own:own + HALF, :].T)
        m["x1part"] = np.ascontiguousarray(
            x1[b, (own + HALF) % N:(own + HALF) % N + HALF, :].T)
    return m


def emit_l1ffn_layer(ctx, io, l, env):
    nc, pb = ctx.nc, ctx.pb
    sb = ctx.sb
    QTH, KTH, VTOK = sb["QTH"], sb["KTH"], sb["VTOK"]
    WQS, WKS, WVS = sb["WQS"], sb["WKS"], sb["WVS"]
    YT, EB, MSK2, OT, OTMP = sb["YT"], sb["EB"], sb["MSK2"], sb["OT"], sb["OTMP"]
    MSKT = sb["MSKT"]
    HP, RAW2F, S1T = sb["HP"], sb["RAW2F"], sb["S1T"]
    GT2, QTMP, ETMP, RNR, RNB = sb["GT2"], sb["QTMP"], sb["ETMP"], sb["RNR"], sb["RNB"]
    RESF = sb["RESF"]
    KPB, VPB = sb["KPB"], sb["VPB"]
    S2R, BQS, ROT = sb["S2R"], sb["BQS"], sb["ROT"]
    LNG, LNB, B2R, B1R, SZC = sb["LNG"], sb["LNB"], sb["B2R"], sb["B1R"], sb["SZC"]
    MXRrow = sb["RNB"]
    ones_row = sb["ONES1x128"]
    ones_col = sb["ONES128x1"]
    ones512 = sb["ONES1x512"]
    ident = sb["IDENT"]
    TMP128 = sb["TR1"]
    NMC = sb["NMC"]
    NMCB = sb["NMCB"]
    MXR = sb["MXR"]

    ysrc = env["ysrc_b"]
    yoff = env["yoff"]
    ysrc_f = env["ysrc_f"]

    t_qsr = pb.dma("gpsimd", sb["QSR"][0:1, 0:512], io["qs_row"][l], channel="c2")
    t_ksr = pb.dma("gpsimd", sb["QSR"][0:1, 512:1024], io["ks_row"][l],
                   channel="c2")
    t_b2 = pb.dma("gpsimd", B2R, io["b2f"][l].unsqueeze(0), channel="bv")
    bq1 = mmg(ctx, 6, [(ctx.bank(6), ones_row, sb["QSR"][0:1, 0:512], None)],
              deps=[t_ksr])
    cb1 = pb.comp("vector", lambda e: e.tensor_copy(BQS[:, 0:512], ctx.bank(6)),
                  deps=[bq1])
    ctx.bank_tok[6] = cb1
    bq2 = mmg(ctx, 6, [(ctx.bank(6), ones_row, sb["QSR"][0:1, 512:1024], None)],
              deps=[t_ksr])
    cb2 = pb.comp("vector", lambda e: e.tensor_copy(BQS[:, 512:1024], ctx.bank(6)),
                  deps=[bq2])
    ctx.bank_tok[6] = cb2
    pb.barrier()

    wo_last = None
    for (t0, t1, wlist) in env["groups"]:
        for eng in ("vector", "scalar", "tensor"):
            pb.waits_only(eng, [env.get("xao_last"), env.get("res_last")])
        ntile = (t1 - t0) // 512
        wdeps = [env.get("wq_free"), env.get("msk_free")]
        env["wq_free"] = None
        t_wq = pb.dma("gpsimd", WQS, rearr_kpm(io["wqkv_f"][l])[:, :, 0:512],
                      deps=wdeps, channel="sla")
        t_wk = pb.dma("gpsimd", WKS, rearr_kpm(io["wqkv_f"][l])[:, :, 512:1024],
                      deps=wdeps, channel="slb")
        t_wv = pb.dma("gpsimd", WVS, rearr_kpm(io["wqkv_f"][l])[:, :, 1024:1536],
                      deps=wdeps, channel="u1")
        pb.phase = f"L{l}.qkv"
        yt_tok = env.get("msk_free")
        last_split = None
        prev_tt = None
        for ti in range(ntile):
            mt0 = t0 + ti * 512
            c0 = mt0 + yoff
            yt_tok = pb.dma("sync",
                            YT[:, 0:4160].rearrange("p (c m) -> p c m", m=520)
                            [:, :, 0:513],
                            rearr_kpm(ysrc)[:, :, c0 - 1:c0 + 512],
                            deps=[yt_tok, env["ysrc_ready"]], channel="rawf")
            sz_tok = None
            if mt0 <= 0 < mt0 + 512:
                p0 = -mt0

                def szwrap(e, sem, p0=p0):
                    for kc in range(4, 8):
                        e.tensor_scalar_mul(YT[:, kc * 520 + p0:kc * 520 + p0 + 1],
                                            YT[:, kc * 520 + p0:kc * 520 + p0 + 1],
                                            SZC).then_inc(sem, 1)
                sz_tok = pb.comp_n("vector", szwrap, 4, deps=[yt_tok])
            for mc in range(4):
                def lhs(kc, mc=mc):
                    off = 1 if kc < 4 else 0
                    return YT[:, kc * 520 + off + mc * 128:kc * 520 + off + mc * 128 + 128]
                tq = mmg(ctx, 0, [(ctx.bank(0), lhs(kc), WQS[:, kc * 512:(kc + 1) * 512],
                                   None) for kc in range(NCH)],
                         deps=[t_wq, yt_tok, sz_tok])
                tk = mmg(ctx, 1, [(ctx.bank(1), lhs(kc), WKS[:, kc * 512:(kc + 1) * 512],
                                   None) for kc in range(NCH)],
                         deps=[t_wk, yt_tok, sz_tok])
                tv = mmg(ctx, 2, [(ctx.bank(2), lhs(kc), WVS[:, kc * 512:(kc + 1) * 512],
                                   None) for kc in range(NCH)],
                         deps=[t_wv, yt_tok, sz_tok])
                trow = mt0 + mc * 128 + 2048
                t_rot = pb.dma("sync", ROT.rearrange("p (k c) -> p k c", k=4),
                               io["rot"][:, trow:trow + 128, :].transpose([1, 0, 2]),
                               deps=[env.get("rot_free"), last_split], channel="rot")
                env["rot_free"] = None
                cv = (mt0 + mc * 128 - t0) // 128
                ev = pb.comp("vector", lambda e, cv=cv: e.tensor_copy(
                    VTOK[:, cv * 512:(cv + 1) * 512], ctx.bank(2)), deps=[tv])
                ctx.bank_tok[2] = ev
                prev_part = None
                for part, bank, tok, csoff in (("q", 0, tq, 0), ("k", 1, tk, 128)):
                    t_sq = pb.comp("scalar", lambda e, bank=bank: e.activation(
                        QTMP, ctx.bank(bank), AF.Square), deps=[tok, prev_part])
                    t1s = pb.comp("vector", lambda e: e.tensor_reduce(
                        S2R[:, 0:8], QTMP.rearrange("p (h x) -> p h x", h=8),
                        axis=AX.X, op=ALU.add), deps=[t_sq])
                    t2s = pb.comp("scalar", lambda e: e.activation(
                        S2R[:, 8:16], S2R[:, 0:8], AF.Sqrt), deps=[t1s])
                    t3s = pb.comp("vector", lambda e: e.tensor_scalar_max(
                        S2R[:, 8:16], S2R[:, 8:16], 1e-12), deps=[t2s])
                    t4s = pb.comp("vector", lambda e: e.reciprocal(
                        S2R[:, 0:8], S2R[:, 8:16]), deps=[t3s])
                    t5s = pb.comp("vector", lambda e, bank=bank: e.tensor_tensor(
                        GT2.rearrange("p (h x) -> p h x", h=8),
                        ctx.bank(bank).rearrange("p (h x) -> p h x", h=8),
                        S2R[:, 0:8].unsqueeze(2).broadcast_to([128, 8, 64]),
                        op=ALU.mult), deps=[t4s, tok])
                    ctx.bank_tok[bank] = t5s
                    qof = 0 if part == "q" else 512
                    t6s = pb.comp("vector", lambda e, qof=qof: e.tensor_tensor(
                        GT2, GT2, BQS[:, qof:qof + 512], op=ALU.mult),
                        deps=[t5s, cb1, cb2])
                    cs0 = 0 if csoff == 0 else 128
                    t7 = pb.comp("vector", lambda e, cs0=cs0: e.tensor_tensor(
                        QTMP.rearrange("p (h x) -> p h x", h=8),
                        GT2.rearrange("p (h x) -> p h x", h=8),
                        ROT[:, cs0:cs0 + 64].unsqueeze(1).broadcast_to([128, 8, 64]),
                        op=ALU.mult), deps=[t6s, t_rot])
                    t8 = pb.comp("vector", lambda e, cs0=cs0: e.tensor_tensor(
                        ETMP.rearrange("p (h x) -> p h x", h=8)[:, :, 0:32],
                        GT2.rearrange("p (h x) -> p h x", h=8)[:, :, 32:64],
                        ROT[:, cs0 + 64:cs0 + 96].unsqueeze(1)
                        .broadcast_to([128, 8, 32]), op=ALU.mult),
                        deps=[t6s, t_rot])
                    t9 = pb.comp("vector", lambda e, cs0=cs0: e.tensor_tensor(
                        ETMP.rearrange("p (h x) -> p h x", h=8)[:, :, 32:64],
                        GT2.rearrange("p (h x) -> p h x", h=8)[:, :, 0:32],
                        ROT[:, cs0 + 96:cs0 + 128].unsqueeze(1)
                        .broadcast_to([128, 8, 32]), op=ALU.mult),
                        deps=[t6s, t_rot])
                    t10 = pb.comp("vector", lambda e: e.tensor_add(
                        KPB, QTMP, ETMP),
                        deps=[t7, t8, t9, prev_tt, env.get("ot_dma"),
                              env.get("xao_last")])
                    prev_part = t10
                    dst = QTH if part == "q" else KTH
                    for hc in range(4):
                        tt = pb.dma("scalar", TMP128,
                                    KPB[:, hc * 128:(hc + 1) * 128],
                                    deps=[t10, last_split], transpose=True,
                                    channel="tp")
                        colA = (2 * hc) * GSPAN + (mt0 + mc * 128 - t0)
                        colB = (2 * hc + 1) * GSPAN + (mt0 + mc * 128 - t0)
                        sp1 = pb.dma("sync", dst[0:64, colA:colA + 128],
                                     TMP128[0:64, :], deps=[tt], channel="sp")
                        sp2 = pb.dma("sync", dst[0:64, colB:colB + 128],
                                     TMP128[64:128, :], deps=[tt], channel="sp")
                        last_split = sp2
                        prev_tt = tt
        qkv_done = last_split
        if env.get("qkv_only"):
            return []

        has_w0 = any(w == 0 for w in wlist)
        mwar = [tq, tk, tv]
        tmsk_n2 = pb.dma("sync", WQS.rearrange("p (st m) -> p st m", m=512),
                         io["masks"][1].rearrange("(st p) m -> p st m", p=128),
                         deps=mwar, channel="mskA")
        tmsk_n1 = pb.dma("sync", WVS.rearrange("p (ms k) -> p ms k", k=1024),
                         io["masksT"][1].rearrange("(ms p) k -> p ms k", p=128),
                         deps=mwar, channel="mskB")
        tmsk_02 = tmsk_01 = None
        if has_w0:
            tmsk_02 = pb.dma("sync", WKS.rearrange("p (st m) -> p st m", m=512),
                             io["masks"][0].rearrange("(st p) m -> p st m", p=128),
                             deps=mwar, channel="mskC")
            tmsk_01 = pb.dma("sync",
                             YT[:, 0:4096].rearrange("p (ms k) -> p ms k", k=1024),
                             io["masksT"][0].rearrange("(ms p) k -> p ms k", p=128),
                             deps=mwar, channel="mskD")

        pb.phase = f"L{l}.attn"
        tadds_all = []
        for w in wlist:
            widx = 0 if w == 0 else 1
            mS = WQS if widx else WKS
            mS_tok = tmsk_n2 if widx else tmsk_02
            mT = WVS if widx else YT
            mT_tok = tmsk_n1 if widx else tmsk_01
            for h in range(8):
                qcol = lambda m: h * GSPAN + (w * 512 - t0) + m
                kcol = lambda sx: h * GSPAN + ((w - 1) * 512 - t0) + sx
                pb.phase = f"L{l}.attn.p1"
                nm_toks = []
                for msub in range(4):
                    tS1 = mmg(ctx, 4, [(ctx.bank(4),
                                        QTH[0:64, qcol(msub * 128):qcol(msub * 128) + 128],
                                        KTH[0:64, kcol(0):kcol(0) + 512], None)],
                              deps=[qkv_done])
                    tS2 = mmg(ctx, 5, [(ctx.bank(5),
                                        QTH[0:64, qcol(msub * 128):qcol(msub * 128) + 128],
                                        KTH[0:64, kcol(512):kcol(512) + 512], None)],
                              deps=[qkv_done])
                    tta = pb.comp("vector", lambda e, msub=msub, mT=mT:
                                  e.tensor_tensor(
                                      ctx.PS[:, 2048:3072], ctx.PS[:, 2048:3072],
                                      mT[:, msub * 1024:(msub + 1) * 1024],
                                      op=ALU.add),
                                  deps=[tS1, tS2, mT_tok])
                    tnm = pb.comp("vector", lambda e, msub=msub: e.tensor_reduce(
                        NMC[:, msub:msub + 1], ctx.PS[:, 2048:3072],
                        axis=AX.X, op=ALU.max, negate=True), deps=[tta])
                    ctx.bank_tok[4] = tnm
                    ctx.bank_tok[5] = tnm
                    nm_toks.append(tnm)
                tcb = pb.comp("vector", lambda e: e.tensor_copy(NMCB[:, 0:4],
                                                                NMC[:, 0:4]),
                              deps=nm_toks)
                pb.phase = f"L{l}.attn.tr"
                ttr = []
                for msub in range(4):
                    tt = pb.comp("tensor", lambda e, msub=msub: e.transpose(
                        ctx.bank_bf(6)[0:1, msub * 128:(msub + 1) * 128],
                        NMCB[:, msub:msub + 1], ident),
                        deps=[tcb, ctx.bank_tok[6]])
                    ttr.append(tt)
                tmxr = pb.comp("scalar", lambda e: e.copy(MXR[0:1, 0:512],
                                                          ctx.bank_bf(6)[0:1, 0:512]),
                               deps=ttr)
                ctx.bank_tok[6] = tmxr
                pb.phase = f"L{l}.attn.p2"
                lsum_done = None
                o_toks = []
                e_toks = []
                for st in range(8):
                    bkS = st % 2
                    tS = mmg(ctx, bkS, [
                        (ctx.bank(bkS),
                         KTH[0:64, kcol(st * 128):kcol(st * 128) + 128],
                         QTH[0:64, qcol(0):qcol(0) + 512], None),
                        (ctx.bank(bkS), ones_row, MXR[0:1, 0:512], None)],
                        deps=[tmxr, qkv_done], independent=True)
                    tadd = pb.comp("vector", lambda e, bkS=bkS, st=st, mS=mS:
                                   e.tensor_tensor(
                                       ETMP, ctx.bank(bkS),
                                       mS[:, st * 512:(st + 1) * 512],
                                       op=ALU.add),
                                   deps=[tS, mS_tok,
                                         e_toks[-1] if e_toks else None])
                    ctx.bank_tok[bkS] = tadd
                    tex = pb.comp("scalar", lambda e, st=st: e.activation(
                        EB[:, st * 512:(st + 1) * 512], ETMP, AF.Exp), deps=[tadd])
                    e_toks.append(tex)
                    tadds_all.append(tadd)
                pb.phase = f"L{l}.attn.ls"
                tls = mmg(ctx, 2, [(ctx.bank(2)[0:1, 0:512], ones_col,
                                    EB[:, st * 512:(st + 1) * 512], None)
                                   for st in range(8)],
                          deps=e_toks, cont=False)
                tfl = pb.comp("vector", lambda e: e.tensor_scalar_max(
                    RNR[0:1, 0:512], ctx.bank(2)[0:1, 0:512], 1e-35), deps=[tls])
                ctx.bank_tok[2] = tfl
                trc = pb.comp("vector", lambda e: e.reciprocal(
                    RNR[0:1, 0:512], RNR[0:1, 0:512]), deps=[tfl])
                trb = pb.comp("vector", lambda e: e.tensor_copy(
                    RNB[0:1, 0:512], RNR[0:1, 0:512]), deps=[trc])
                tbr = mmg(ctx, 3, [(ctx.bank(3), ones_row, RNB[0:1, 0:512], None)],
                          deps=[trb])
                pb.phase = f"L{l}.attn.o"
                oterms = []
                for st in range(8):
                    cv = ((w - 1) * 512 + st * 128 - t0) // 128
                    oterms.append((ctx.bank(6)[0:64, 0:512],
                                   VTOK[:, cv * 512 + h * 64:cv * 512 + h * 64 + 64],
                                   EB[:, st * 512:(st + 1) * 512], None))
                t_o = mmg(ctx, 6, oterms, deps=e_toks + [ctx.bank_tok[6]])
                toc = pb.comp("scalar", lambda e: e.copy(OTMP, ctx.bank(6)[0:64, 0:512]),
                              deps=[t_o])
                ctx.bank_tok[6] = toc
                hc = h // 2
                if h % 2 == 0:
                    tno = pb.comp("vector", lambda e, hc=hc: e.tensor_tensor(
                        OT[0:64, hc * 512:(hc + 1) * 512], OTMP,
                        ctx.bank(3)[0:64, 0:512], op=ALU.mult),
                        deps=[toc, tbr, env.get("ot_free")])
                    ctx.bank_tok[3] = tno
                    o_done = tno
                else:
                    tno = pb.comp("vector", lambda e: e.tensor_tensor(
                        KPB[0:64, 0:512], OTMP, ctx.bank(3)[0:64, 0:512],
                        op=ALU.mult),
                        deps=[toc, tbr, env.get("ot_dma"), env.get("xao_last")])
                    ctx.bank_tok[3] = tno
                    o_done = pb.dma("sync", OT[64:128, hc * 512:(hc + 1) * 512],
                                    KPB[0:64, 0:512], deps=[tno, env.get("ot_free")],
                                    channel="ot")
                    env["ot_dma"] = o_done
                env["o_done_" + str(h)] = o_done
            pb.phase = f"L{l}.wo"
            o_all = [env["o_done_" + str(h)] for h in range(8)]
            ycol = w * 512 + yoff
            wo_oc_last = env.get("xao_last")
            res_consumer = env.get("res_last")
            for oc in range(NCH):
                if ysrc_f is None:
                    yw_tok = pb.dma("gpsimd", RESF,
                                    rearr_kpm(ysrc)[:, oc, ycol:ycol + 512],
                                    deps=[res_consumer, o_all[-1]],
                                    channel="ycast")
                else:
                    yw_tok = pb.dma("sync", RESF,
                                    rearr_kpm(ysrc_f)[:, oc, ycol:ycol + 512],
                                    deps=[res_consumer, o_all[-1]],
                                    channel="resf")
                sl = (sb["SLA"], sb["SLB"])[oc % 2]
                ch = ("sla", "slb")[oc % 2]
                tld = pb.dma("gpsimd", sl[:, 0:512],
                             io["wo_f"][l].rearrange("(ic kp) o -> kp ic o", kp=128)
                             [:, :, oc * 128:(oc + 1) * 128],
                             deps=[ctx.slab_last[ch]], channel=ch)
                bk = 7
                terms = [(ctx.bank(bk), sl[:, ic * 128:(ic + 1) * 128],
                          OT[:, ic * 512:(ic + 1) * 512], None) for ic in range(4)]
                tm = mmg(ctx, bk, terms, deps=[tld] + o_all)
                ctx.slab_last[ch] = tm
                tadd = pb.comp("vector", lambda e, bk=bk, oc=oc: e.tensor_tensor(
                    GT2, ctx.bank(bk), RESF, op=ALU.add),
                    deps=[tm, yw_tok, wo_oc_last])
                res_consumer = tadd
                ctx.bank_tok[bk] = tadd
                tcb16 = pb.comp("vector", lambda e: e.tensor_copy(KPB, GT2),
                                deps=[tadd])
                col = w * 512 + env["wo_off"]
                d1 = pb.dma("sync", rearr_kpm(env["wo_f32"])[:, oc, col:col + 512],
                            GT2, deps=[tadd], channel="xao")
                d2 = pb.dma("sync", rearr_kpm(env["wo_b16"])[:, oc, col:col + 512],
                            KPB, deps=[tcb16], channel="xao")
                wo_oc_last = d2
            env["ot_free"] = tm
            env["xao_last"] = wo_oc_last
            env["res_last"] = res_consumer
            wo_last = wo_oc_last
        env["msk_free"] = (pb.psem["vector"], pb.cnt["vector"])

    pb.phase = f"L{l}.ffn"
    pb.barrier()
    final = env["final"]
    out_dmas = []

    def do_ffn_tile(a, b, ffn_prev):
        wdt = b - a
        ca = a + env["wo_off"]
        rf_tok = pb.dma("sync",
                        RAW2F.rearrange("p (c m) -> p c m", m=513)[:, :, 0:wdt + 1],
                        rearr_kpm(env["wo_f32"])[:, :, ca - 1:ca + wdt],
                        deps=[env.get("rawf_free"), ffn_prev], channel="raw2f")
        env["rawf_free"] = None
        sz_tok = None
        if a <= 0 < b:
            p0 = -a

            def szw2(e, sem, p0=p0):
                for kc in range(4, 8):
                    e.tensor_scalar_mul(RAW2F[:, kc * 513 + p0:kc * 513 + p0 + 1],
                                        RAW2F[:, kc * 513 + p0:kc * 513 + p0 + 1],
                                        SZC).then_inc(sem, 1)
            sz_tok = pb.comp_n("vector", szw2, 4, deps=[rf_tok])
        s1_tok = None
        if final:
            s1_tok = pb.dma("sync", S1T.rearrange("p (c m) -> p c m", c=8)
                            [:, :, 0:wdt],
                            rearr_kpm(io["sc_s1"])[:, :, a:a + wdt],
                            deps=[env.get("s1_free"), ffn_prev], channel="s1i",
                            allow_slow_non_contiguous=(wdt < 8))

        def rawsh(c):
            off = 1 if c < 4 else 0
            return RAW2F[:, c * 513 + off:c * 513 + off + wdt]
        t_xn, lnfree = emit_ln(ctx, rawsh, (sz_tok or rf_tok),
                               lambda c: YT[:, c * 512:c * 512 + wdt], mt=wdt)
        aff = t_xn
        for c in range(NCH):
            a1t = pb.comp("vector", lambda e, c=c: e.tensor_scalar_mul(
                YT[:, c * 512:c * 512 + wdt], YT[:, c * 512:c * 512 + wdt],
                LNG[:, l * 8 + c:l * 8 + c + 1]), deps=[aff])
            aff = pb.comp("vector", lambda e, c=c: e.tensor_scalar_add(
                YT[:, c * 512:c * 512 + wdt], YT[:, c * 512:c * 512 + wdt],
                LNB[:, l * 8 + c:l * 8 + c + 1]), deps=[a1t])
        h2_tok = aff
        rh_last = None
        b1_tok = None
        for fc in range(43):
            sl = (sb["SLA"], sb["SLB"])[fc % 2]
            ch = ("sla", "slb")[fc % 2]
            tld = pb.dma("gpsimd", sl,
                         rearr_kpm(io["w1p"][l])[:, :, fc * 128:(fc + 1) * 128],
                         deps=[ctx.slab_last[ch]], channel=ch)
            trb = pb.dma("gpsimd", B1R,
                         io["b1p"][l][fc * 128:(fc + 1) * 128].unsqueeze(0),
                         deps=[b1_tok or rh_last], channel="rb1")
            bk = fc % 2
            terms = [(ctx.bank(bk)[:, 0:wdt], sl[:, kc * 128:(kc + 1) * 128],
                      YT[:, kc * 512:kc * 512 + wdt], None) for kc in range(NCH)]
            terms.append((ctx.bank(bk)[:, 0:wdt], B1R, ones512[0:1, 0:wdt], None))
            tm1 = mmg(ctx, bk, terms, deps=[tld, trb, h2_tok])
            b1_tok = tm1
            ctx.slab_last[ch] = tm1
            tgl = pb.comp("scalar", lambda e, bk=bk, wdt=wdt: e.activation(
                ETMP[0:64, 0:wdt], ctx.bank(bk)[64:128, 0:wdt], AF.Gelu),
                deps=[tm1, rh_last])
            p = fc // 2
            r0 = (fc % 2) * 64
            tpr = pb.comp("vector", lambda e, bk=bk, p=p, r0=r0, wdt=wdt:
                          e.tensor_tensor(
                              HP[r0:r0 + 64, p * 512:p * 512 + wdt],
                              ctx.bank(bk)[0:64, 0:wdt],
                              ETMP[0:64, 0:wdt], op=ALU.mult), deps=[tgl])
            ctx.bank_tok[bk] = tpr
            rh_last = tpr
        w2_tok = None
        for oc in range(NCH):
            w2a = pb.dma("gpsimd", WQS[:, 0:21 * 128],
                         io["w2p"][l][0:2688].rearrange("(pk kp) o -> kp pk o", kp=128)
                         [:, :, oc * 128:(oc + 1) * 128],
                         deps=[w2_tok, env.get("w2_free")], channel="u1")
            w2b = pb.dma("gpsimd", WQS[0:64, 21 * 128:22 * 128],
                         io["w2p"][l][2688:2752, oc * 128:(oc + 1) * 128],
                         deps=[w2_tok, env.get("w2_free")], channel="u1")
            env["w2_free"] = None
            bk = 2 + oc % 2
            terms = [(ctx.bank(bk)[:, 0:wdt], WQS[:, pk * 128:(pk + 1) * 128],
                      HP[:, pk * 512:pk * 512 + wdt], None) for pk in range(21)]
            terms.append((ctx.bank(bk)[:, 0:wdt], WQS[0:64, 21 * 128:22 * 128],
                          HP[0:64, 21 * 512:21 * 512 + wdt], None))
            terms.append((ctx.bank(bk)[:, 0:wdt],
                          B2R[:, oc * 128:(oc + 1) * 128], ones512[0:1, 0:wdt],
                          None))
            tm2 = mmg(ctx, bk, terms, deps=[w2b, rh_last, t_b2])
            w2_tok = tm2
            tr1 = pb.comp("vector", lambda e, bk=bk, oc=oc, wdt=wdt: e.tensor_tensor(
                GT2[:, 0:wdt], ctx.bank(bk)[:, 0:wdt],
                RAW2F[:, oc * 513 + 1:oc * 513 + 1 + wdt], op=ALU.add),
                deps=[tm2, out_dmas[-1] if out_dmas else None])
            if final:
                tr2 = pb.comp("vector", lambda e, oc=oc, wdt=wdt: e.tensor_tensor(
                    GT2[:, 0:wdt], GT2[:, 0:wdt],
                    S1T[:, oc * 512:oc * 512 + wdt], op=ALU.add),
                    deps=[tr1, s1_tok])
                ctx.bank_tok[bk] = tr2
                dd = pb.dma("sync", rearr_kpm(io["out"])[:, oc, a:a + wdt],
                            GT2[:, 0:wdt], deps=[tr2], channel="outo",
                            allow_slow_non_contiguous=(wdt < 8))
                out_dmas.append(dd)
            else:
                ctx.bank_tok[bk] = tr1
                cb16 = pb.comp("vector", lambda e, wdt=wdt: e.tensor_copy(
                    KPB[:, 0:wdt], GT2[:, 0:wdt]), deps=[tr1])
                fcol = a + env["fo_off"]
                dd = pb.dma("sync",
                            rearr_kpm(env["ffn_f32"])[:, oc, fcol:fcol + wdt],
                            GT2[:, 0:wdt], deps=[tr1], channel="outo",
                            allow_slow_non_contiguous=(wdt < 8))
                dd = pb.dma("sync",
                            rearr_kpm(env["ffn_b16"])[:, oc, fcol:fcol + wdt],
                            KPB[:, 0:wdt], deps=[cb16], channel="outo",
                            allow_slow_non_contiguous=(wdt < 8))
                out_dmas.append(dd)
        env["s1_free"] = None if not final else w2_tok
        return out_dmas[-1]

    ffn_prev = None
    for (a, b) in env["ffn_tiles"]:
        ffn_prev = do_ffn_tile(a, b, ffn_prev)
    return out_dmas


# version 07
# speedup vs baseline: 1.5865x; 1.2036x over previous
import numpy as np
import ml_dtypes
import concourse.bass as bass
import concourse.mybir as mybir
from concourse.bass_utils import run_bass_kernel_spmd

F32 = mybir.dt.float32
BF16 = mybir.dt.bfloat16
I32 = mybir.dt.int32
AF = mybir.ActivationFunctionType
ALU = mybir.AluOpType
AX = mybir.AxisListType
ET = mybir.EngineType

B, N, D = 4, 4096, 1024
HALF = 2048
NCH = 8
MT = 512
NMT = 4
DH = 64
BLOCK_M = 64
SEG = 256
DIL = 4
EPS_LN = 1e-5
WINDOW = 512
HID = 2730
QK_SCALE = 8.0
NEG = -1e30

ENGINES = ("tensor", "vector", "scalar", "gpsimd", "sync")
BF = ml_dtypes.bfloat16
SB_BASE = 16512



def host_gather_indices(mapping, own_start):
    m = np.asarray(mapping).astype(np.int64)
    qidx = m[np.arange(own_start, own_start + HALF)]
    kidx = np.empty(HALF, dtype=np.int64)
    for j in range(HALF // BLOCK_M):
        gn = (own_start + j * BLOCK_M) // BLOCK_M
        seg = m[gn * BLOCK_M] // SEG
        kidx[j * BLOCK_M:(j + 1) * BLOCK_M] = m[seg * SEG + DIL * np.arange(BLOCK_M)]
    return qidx, kidx


def host_consts():
    c = {}
    c["cn_ident"] = np.eye(128, dtype=BF)
    c["cn_ones_1x128"] = np.ones((1, 128), dtype=BF)
    c["cn_ones_1x512"] = np.ones((1, 512), dtype=BF)
    blk = np.zeros((128, 2), dtype=BF)
    blk[0:64, 0] = 1
    blk[64:128, 1] = 1
    c["cn_blk128x2"] = blk
    blk2 = np.zeros((2, 128), dtype=BF)
    blk2[0, 0:64] = 1
    blk2[1, 64:128] = 1
    c["cn_blk2x128"] = blk2
    for e in range(4):
        s = np.zeros((4, 128), dtype=BF)
        s[e, :] = 1
        c[f"cn_sel{e}"] = s
    return c



class PB:

    def __init__(self, nc):
        self.nc = nc
        self.q = {e: [] for e in ENGINES}
        self.cnt = {e: 0 for e in ENGINES}
        self.psem = {e: nc.alloc_semaphore(f"p_{e}") for e in ENGINES}
        self.chsem = {}
        self.chcnt = {}
        self.wm = {e: {} for e in ENGINES}
        self.want_pid = False
        self.pid = None
        self.ppid = None
        self.phase = "init"
        self.etags = {e: [] for e in ENGINES}

    def _waits(self, eng, deps):
        out = []
        wm = self.wm[eng]
        for d in deps:
            if d is None:
                continue
            sem, val = d
            k = sem.name
            if wm.get(k, 0) >= val:
                continue
            wm[k] = val
            out.append((sem, val))
        return out

    def comp(self, eng, fn, deps=()):
        self.cnt[eng] += 1
        self.etags[eng].append(self.phase)
        tok = (self.psem[eng], self.cnt[eng])
        self.q[eng].append((self._waits(eng, deps), fn, tok, 1))
        return tok

    def comp_n(self, eng, fn, n, deps=()):
        self.cnt[eng] += n
        self.etags[eng].extend([self.phase] * n)
        tok = (self.psem[eng], self.cnt[eng])
        self.q[eng].append((self._waits(eng, deps), fn, tok, None))
        return tok

    CH_ENG = {"slb": "sync"}

    def dma(self, eng, out, in_, deps=(), channel="misc", **kw):
        eng = self.CH_ENG.get(channel, eng)
        if channel not in self.chsem:
            self.chsem[channel] = self.nc.alloc_semaphore(f"c_{channel}")
            self.chcnt[channel] = 0
        self.chcnt[channel] += 1
        tok = (self.chsem[channel], 16 * self.chcnt[channel])

        def fn(e, out=out, in_=in_, kw=kw):
            if callable(out):
                out = out()
            if callable(in_):
                in_ = in_()
            return e.dma_start(out=out, in_=in_, **kw)
        self.q[eng].append((self._waits(eng, deps), fn, tok, 16))
        return tok

    def raw(self, eng, fn, deps=()):
        self.q[eng].append((self._waits(eng, deps), fn, None, 0))

    def waits_only(self, eng, deps):
        ws = self._waits(eng, deps)
        if ws:
            self.q[eng].append((ws, None, None, 0))

    def snapshot(self):
        toks = []
        for e in ENGINES:
            if self.cnt[e]:
                toks.append((self.psem[e], self.cnt[e]))
        for ch, sem in self.chsem.items():
            toks.append((sem, 16 * self.chcnt[ch]))
        return toks

    def barrier(self):
        toks = self.snapshot()
        for e in ENGINES:
            self.waits_only(e, list(toks))

    def realize(self):
        with self.nc.Block() as block:
            for ename in ENGINES:
                ops = self.q[ename]

                def body(eng, ops=ops, ename=ename):
                    if ename == "sync" and self.want_pid:
                        pid = self.nc.partition_id(engines=[ET.SP])
                        self.pid = pid
                        self.ppid = self.nc.s_assert_within(
                            pid + 1 - (pid % 2) * 2, 0, 7,
                            skip_runtime_assert=True)
                    for oi, (waits, fn, tok, inc) in enumerate(ops):
                        for (s, v) in waits:
                            eng.wait_ge(s, v)
                        if fn is None:
                            continue
                        try:
                            if inc is None:
                                fn(eng, tok[0])
                                continue
                            inst = fn(eng)
                        except Exception:
                            import sys
                            print(f"REALIZE FAIL {ename} op {oi}/{len(ops)}",
                                  file=sys.stderr)
                            raise
                        if tok is not None and inst is not None:
                            inst.then_inc(tok[0], inc)
                getattr(block, ename)(body)


class Ctx:
    def __init__(self, nc):
        self.nc = nc
        self.pb = PB(nc)
        self.PS = nc.alloc_psum_tensor("PS", [128, 4096], F32).ap()
        self.PSb = self.PS.bitcast(BF16)
        self.bank_tok = [None] * 8
        self.slab_last = {"sla": None, "slb": None, "u1": None}
        self.sb = {}

    def bank(self, i):
        return self.PS[:, 512 * i: 512 * (i + 1)]

    def bank_bf(self, i):
        return self.PSb[:, 1024 * i: 1024 * (i + 1)]

    def sb_at(self, name, offset, shape, dtype):
        if name not in self.sb:
            self.sb[name] = self.nc.alloc_sbuf_tensor_at(
                name, shape, dtype, offset=SB_BASE + offset).ap()
        return self.sb[name]


def mmg(ctx, bank_i, terms, deps=(), cont=False, independent=False):
    pb = ctx.pb
    alldeps = list(deps)
    if not cont and ctx.bank_tok[bank_i] is not None:
        alldeps.append(ctx.bank_tok[bank_i])

    def fn(t, sem, terms=terms, cont=cont, independent=independent):
        for i, (out_ap, lhsT, rhs, tp) in enumerate(terms):
            kw = {}
            if tp is not None:
                kw["tile_position"] = tp
            st = True if independent else (i == 0 and not cont)
            sp = True if independent else (i == len(terms) - 1)
            t.matmul(out_ap, lhsT, rhs, start=st, stop=sp,
                     skip_group_check=True, **kw).then_inc(sem, 1)
    return pb.comp_n("tensor", fn, len(terms), deps=alldeps)


SBMAP = dict(
    BIG_A=0,
    BIG_B=32768,
    BIG_C=65536,
    BIG_D=98304,
    ATt=131072,
    E1=139264,
    E2=147456,
    RAWF=155648,
    U1=171840,
    SQ=180032,
    STM=184128,
    STB=194368,
    LNT=196416,
    SLA=198464,
    SLB=200512,
    LS=202560,
    LSR=204608,
    NMX=205632,
    BQK=205664,
    BQKQ=205760,
    BV=205792,
    IDENT=207840,
    ONES1x128=208096,
    ONES1x512=208352,
    BLK128x2=209376,
    BLK2x128=209440,
    ONES128x1=209696,
    GW=209728,
    GSC=209792,
    GROWS4=209920,
    GBF=211968,
    GB=211976,
    SB1S=211968,
    SEL=212112,
    BCOMB=212112,
)
SBMAP["GROWS4"] = 209920
SBMAP["BCOMB"] = 210944
SBMAP["GBF"] = 209792 + 64
SBMAP["GB"] = 209792 + 96
SBMAP["GROWE"] = 209792 + 96
SBMAP["GROWE"] = 202560
SBMAP["ONES128x1F"] = 212096
del SBMAP["SEL"]


def emit_ln(ctx, raw, raw_dep, out_bf, mt=MT,
            bank_s=6, bank_q=7, bank_a=4, bank_c=5):
    pb = ctx.pb
    onescol = ctx.sb_at("ONES128x1F", SBMAP["ONES128x1F"], [128, 1], F32)
    onesrow = ctx.sb_at("ONES1x128", SBMAP["ONES1x128"], [1, 128], BF16)
    sq = ctx.sb_at("SQ", SBMAP["SQ"], [128, 1024], F32)
    stm = ctx.sb_at("STM", SBMAP["STM"], [1, 2560], F32)
    stb = ctx.sb_at("STB", SBMAP["STB"], [1, 1024], BF16)
    lnt = ctx.sb_at("LNT", SBMAP["LNT"], [128, 512], F32)
    mean = stm[:, 0:mt]
    var = stm[:, 512:512 + mt]
    rstd = stm[:, 1024:1024 + mt]
    crow = stm[:, 1536:1536 + mt]
    m2 = stm[:, 2048:2048 + mt]
    rstd_b = stb[:, 0:mt]
    crow_b = stb[:, 512:512 + mt]

    t_s = mmg(ctx, bank_s,
              [(ctx.bank(bank_s)[0:1, 0:mt], onescol, raw(c), None)
               for c in range(NCH)], deps=[raw_dep])
    sq_tok = [None, None]
    mm_prev = None
    for c in range(NCH):
        half = (c % 2) * 512
        tk = pb.comp("scalar", lambda e, c=c, half=half: e.activation(
            sq[:, half:half + mt], raw(c), AF.Square),
            deps=[raw_dep, sq_tok[c % 2]])
        dep = [tk]
        if c == 0 and ctx.bank_tok[bank_q] is not None:
            dep.append(ctx.bank_tok[bank_q])
        mm_prev = mmg(ctx, bank_q, [(ctx.bank(bank_q)[0:1, 0:mt], onescol,
                                     sq[:, half:half + mt], None)],
                      deps=dep, cont=(c > 0))
        sq_tok[c % 2] = mm_prev
    t_q = mm_prev

    t1 = pb.comp("scalar", lambda e: e.activation(
        mean, ctx.bank(bank_s)[0:1, 0:mt], AF.Identity, scale=1.0 / D), deps=[t_s])
    ctx.bank_tok[bank_s] = t1
    tm2 = pb.comp("scalar", lambda e: e.activation(m2, mean, AF.Square), deps=[t1])
    t2 = pb.comp("vector", lambda e: e.scalar_tensor_tensor(
        var, ctx.bank(bank_q)[0:1, 0:mt], 1.0 / D, m2,
        op0=ALU.mult, op1=ALU.subtract), deps=[t_q, tm2])
    ctx.bank_tok[bank_q] = t2
    t2b = pb.comp("vector", lambda e: e.tensor_scalar_add(var, var, EPS_LN),
                  deps=[t2])
    t3 = pb.comp("scalar", lambda e: e.activation(var, var, AF.Ln), deps=[t2b])
    t4 = pb.comp("scalar", lambda e: e.activation(rstd, var, AF.Exp, scale=-0.5),
                 deps=[t3])
    t5 = pb.comp("vector", lambda e: e.scalar_tensor_tensor(
        crow, mean, -1.0, rstd, op0=ALU.mult, op1=ALU.mult), deps=[t4])
    tb1 = pb.comp("vector", lambda e: e.tensor_copy(rstd_b, rstd), deps=[t4])
    tb2 = pb.comp("vector", lambda e: e.tensor_copy(crow_b, crow), deps=[t5])
    tA = mmg(ctx, bank_a, [(ctx.bank(bank_a)[:, 0:mt], onesrow, rstd_b, None)],
             deps=[tb1])
    tC = mmg(ctx, bank_c, [(ctx.bank(bank_c)[:, 0:mt], onesrow, crow_b, None)],
             deps=[tb2])
    prev = None
    for c in range(NCH):
        p1 = pb.comp("vector", lambda e, c=c: e.tensor_tensor(
            lnt[:, 0:mt], raw(c), ctx.bank(bank_a)[:, 0:mt], op=ALU.mult),
            deps=[tA, raw_dep, prev])
        p2 = pb.comp("vector", lambda e, c=c: e.tensor_tensor(
            out_bf(c), lnt[:, 0:mt], ctx.bank(bank_c)[:, 0:mt], op=ALU.add),
            deps=[p1, tC])
        prev = p2
    ctx.bank_tok[bank_a] = prev
    ctx.bank_tok[bank_c] = prev
    return prev, [prev, t_s, t_q]


def rearr_kpm(dram_2d):
    return dram_2d.rearrange("(kc kp) m -> kp kc m", kp=128)


def apv(x):
    return x() if callable(x) else x


def emit_block0(ctx, io, ph1_only=False, stage=9):
    nc, pb = ctx.nc, ctx.pb
    sa = ctx.sb_at
    BIG_A = sa("BIG_A", SBMAP["BIG_A"], [128, 16384], BF16)
    BIG_B = sa("BIG_B", SBMAP["BIG_B"], [128, 16384], BF16)
    BIG_C = sa("BIG_C", SBMAP["BIG_C"], [128, 16384], BF16)
    BIG_D = sa("BIG_D", SBMAP["BIG_D"], [128, 16384], BF16)
    ATt = sa("ATt", SBMAP["ATt"], [128, 4096], BF16)
    Ebuf = sa("E1", SBMAP["E1"], [128, 4096], BF16)
    E2buf = sa("E2", SBMAP["E2"], [128, 4096], BF16)
    RAWF = sa("RAWF", SBMAP["RAWF"], [128, 4096], F32)
    U1 = sa("U1", SBMAP["U1"], [128, 4096], BF16)
    SLA = sa("SLA", SBMAP["SLA"], [128, 1024], BF16)
    SLB = sa("SLB", SBMAP["SLB"], [128, 1024], BF16)
    LS = sa("LS", SBMAP["LS"], [2, 512], F32)
    LSR = sa("LSR", SBMAP["LSR"], [2, 512], BF16)
    NMX = sa("NMX", SBMAP["NMX"], [128, 8], F32)
    BQK = sa("BQK", SBMAP["BQK"], [128, 24], F32)
    BQKQ = sa("BQKQ", SBMAP["BQKQ"], [128, 8], F32)
    BV = sa("BV", SBMAP["BV"], [1, 1024], BF16)
    IDENT = sa("IDENT", SBMAP["IDENT"], [128, 128], BF16)
    ONES1x128 = sa("ONES1x128", SBMAP["ONES1x128"], [1, 128], BF16)
    ONES1x512 = sa("ONES1x512", SBMAP["ONES1x512"], [1, 512], BF16)
    BLK128x2 = sa("BLK128x2", SBMAP["BLK128x2"], [128, 2], BF16)
    BLK2x128 = sa("BLK2x128", SBMAP["BLK2x128"], [2, 128], BF16)
    sa("ONES128x1", SBMAP["ONES128x1"], [128, 1], BF16)
    GW = sa("GW", SBMAP["GW"], [128, 32], BF16)
    GSC = sa("GSC", SBMAP["GSC"], [128, 16], F32)
    GBF = sa("GBF", SBMAP["GBF"], [128, 4], BF16)
    GB = sa("GB", SBMAP["GB"], [1, 4], BF16)
    GROWS4 = sa("GROWS4", SBMAP["GROWS4"], [4, 1024], BF16)
    GROWE = sa("GROWE", SBMAP["GROWE"], [1, 1024], BF16)
    SB1S = sa("SB1S", SBMAP["SB1S"], [128, 32], F32)

    pb.phase = "b0.init"
    tk_ones = pb.comp("gpsimd", lambda e: e.memset(
        ctx.sb["ONES128x1"], 1.0))
    sa("ONES128x1F", SBMAP["ONES128x1F"], [128, 1], F32)
    tk_onesf = pb.comp("gpsimd", lambda e: e.memset(ctx.sb["ONES128x1F"], 1.0))
    t_id = pb.dma("gpsimd", IDENT, io["cn_ident"], channel="const")
    t_o1 = pb.dma("gpsimd", ONES1x128, io["cn_ones_1x128"], channel="const")
    t_o5 = pb.dma("gpsimd", ONES1x512, io["cn_ones_1x512"], channel="const")
    t_bk = pb.dma("gpsimd", BLK128x2, io["cn_blk128x2"], channel="const")
    t_bk2 = pb.dma("gpsimd", BLK2x128, io["cn_blk2x128"], channel="const")
    t_bq = pb.dma("gpsimd", BQK, io["bqkv_pc"], channel="const")

    t_bv = pb.dma("gpsimd", BV, io["b_qkv"][2048:3072].unsqueeze(0), channel="const")
    t_gw = pb.dma("gpsimd", GW.rearrange("p (kc e) -> p kc e", kc=8),
                  io["gate_w"].rearrange("(kc kp) e -> kp kc e", kp=128),
                  channel="const")
    t_gb = pb.dma("gpsimd", GB, io["gate_b"].unsqueeze(0), channel="const")
    pb.barrier()
    t_bqs = pb.comp("scalar", lambda e: e.mul(BQKQ, BQK[:, 0:8], 0.125))

    pb.phase = "b0.ph1"
    def ln_into(src_dram, dst_big, free0):
        toks = []
        free = list(free0)
        for mi in range(NMT):
            dep = pb.dma("sync", RAWF.rearrange("p (c m) -> p c m", c=8),
                         rearr_kpm(src_dram)[:, :, mi * MT:(mi + 1) * MT],
                         deps=free, channel="rawf")
            last, free = emit_ln(
                ctx, lambda c: RAWF[:, c * MT:(c + 1) * MT], dep,
                lambda c, mi=mi: dst_big[:, c * 2048 + mi * MT:c * 2048 + (mi + 1) * MT])
            toks.append(last)
        return toks[-1], free

    xnq_tok, _lnfree_q = ln_into(io["xqT"], BIG_A, [])
    if stage <= 1:
        return []

    def fm_proj(w_cols_n0, dst_big, bias_col, scale, src_big, src_tok,
                banks=(0, 1, 2, 3)):
        out_toks = []
        for ni in range(NCH):
            sl = (SLA, SLB)[ni % 2]
            ch = ("sla", "slb")[ni % 2]
            tld = pb.dma("gpsimd", sl,
                         rearr_kpm(io["w_qkv"])[:, :, w_cols_n0 + ni * 128:
                                                w_cols_n0 + (ni + 1) * 128],
                         deps=[ctx.slab_last[ch]], channel=ch)
            last_mm = None
            for mi in range(NMT):
                bk = banks[mi % len(banks)]
                terms = [(ctx.bank(bk), sl[:, kc * 128:(kc + 1) * 128],
                          src_big[:, kc * 2048 + mi * MT:kc * 2048 + (mi + 1) * MT],
                          None) for kc in range(NCH)]
                tm = mmg(ctx, bk, terms, deps=[tld, src_tok])
                te = pb.comp("scalar", lambda e, bk=bk, ni=ni, mi=mi: e.activation(
                    dst_big[:, ni * 2048 + mi * MT:ni * 2048 + (mi + 1) * MT],
                    ctx.bank(bk), AF.Identity, bias=bias_col(ni), scale=scale),
                    deps=[tm])
                ctx.bank_tok[bk] = te
                out_toks.append(te)
                last_mm = tm
            ctx.slab_last[ch] = last_mm
        return out_toks[-1]

    q_tok = fm_proj(0, BIG_B, lambda ni: BQKQ[:, ni:ni + 1], 0.125,
                    BIG_A, xnq_tok)
    if stage <= 2:
        return []

    xnkv_tok, _lnfree_kv = ln_into(io["xkvT"], BIG_C, _lnfree_q)
    k_tok = fm_proj(1024, BIG_D, lambda ni: BQK[:, 8 + ni:9 + ni], 1.0,
                    BIG_C, xnkv_tok)
    if stage <= 3:
        return []

    v_toks = []
    vw_tok = None
    for nh in range(2):
        vw_tok = pb.dma("gpsimd", U1,
                        rearr_kpm(io["w_qkv"])[:, :, 2048 + nh * 512:2048 + (nh + 1) * 512],
                        deps=[ctx.slab_last["u1"], q_tok], channel="u1")
        for tc in range(16):
            bk = tc % 4
            terms = [(ctx.bank(bk), BIG_C[:, kc * 2048 + tc * 128:kc * 2048 + (tc + 1) * 128],
                      U1[:, kc * 512:(kc + 1) * 512], None) for kc in range(NCH)]
            terms.append((ctx.bank(bk), ONES1x128, BV[:, nh * 512:(nh + 1) * 512], None))
            tm = mmg(ctx, bk, terms, deps=[vw_tok, xnkv_tok, t_bv, q_tok])
            ctx.slab_last["u1"] = tm
            te = pb.comp("vector", lambda e, bk=bk, tc=tc, nh=nh: e.tensor_copy(
                BIG_A[:, tc * 1024 + nh * 512:tc * 1024 + (nh + 1) * 512],
                ctx.bank(bk)), deps=[tm])
            ctx.bank_tok[bk] = te
            v_toks.append(te)
    v_tok = v_toks[-1]
    if stage <= 4:
        return []

    QT, KT, VT = BIG_B, BIG_D, BIG_A
    at_dma = None
    last_exp = None
    last_r = None
    last_br = None
    prev_ta = None
    VT2t = E2buf
    for mi in range(NMT):
        vsw1 = pb.dma("sync", VT2t[0:64, :], VT[64:128, mi * 4096:(mi + 1) * 4096],
                      deps=[v_tok, prev_ta], channel="e2s")
        vsw_tok = pb.dma("sync", VT2t[64:128, :], VT[0:64, mi * 4096:(mi + 1) * 4096],
                         deps=[v_tok, prev_ta], channel="e2s")
        hp_toks = []
        for hp in range(NCH):
            bk_s = hp % 2
            bk_l = 2
            bk_r = 3
            bk_o = 4 + hp % 2
            col = lambda j: hp * 2048 + mi * MT + j * 64
            terms = []
            for j in range(8):
                for h2 in range(2):
                    b0 = h2 * 64
                    terms.append((ctx.bank(bk_s)[b0:b0 + 64, j * 64:j * 64 + 64],
                                  KT[b0:b0 + 64, col(j):col(j) + 64],
                                  QT[b0:b0 + 64, col(j):col(j) + 64],
                                  (b0, b0)))
            t_s = mmg(ctx, bk_s, terms, deps=[q_tok, k_tok, at_dma],
                      independent=True)
            t_mx = pb.comp("vector", lambda e, bk_s=bk_s: e.tensor_reduce(
                NMX, ctx.bank(bk_s).rearrange("p (j x) -> p j x", j=8),
                axis=AX.X, op=ALU.max, negate=True), deps=[t_s, last_exp])
            e_toks = []
            for j in range(8):
                tk = pb.comp("scalar", lambda e, j=j, hp=hp, bk_s=bk_s: e.activation(
                    Ebuf[:, hp * 512 + j * 64:hp * 512 + (j + 1) * 64],
                    ctx.bank(bk_s)[:, j * 64:j * 64 + 64], AF.Exp,
                    bias=NMX[:, j:j + 1]), deps=[t_mx])
                e_toks.append(tk)
            ctx.bank_tok[bk_s] = e_toks[-1]
            if stage <= 5:
                last_exp = e_toks[-1]
                prev_ta = e_toks[-1]
                hp_toks.append(e_toks[-1])
                continue
            lterms = [(ctx.bank(bk_l)[0:2, j * 64:j * 64 + 64], BLK128x2,
                       Ebuf[:, hp * 512 + j * 64:hp * 512 + (j + 1) * 64], None)
                      for j in range(8)]
            t_l = mmg(ctx, bk_l, lterms, deps=e_toks, independent=True)
            t_eps = pb.comp("vector", lambda e, bk_l=bk_l: e.tensor_scalar_add(
                LS, ctx.bank(bk_l)[0:2, :], 1e-6), deps=[t_l, last_r])
            ctx.bank_tok[bk_l] = t_eps
            t_r0 = pb.comp("vector", lambda e: e.reciprocal(LS, LS),
                           deps=[t_eps, last_br])
            t_r = pb.comp("vector", lambda e: e.tensor_copy(LSR, LS), deps=[t_r0])
            t_br = mmg(ctx, bk_r, [(ctx.bank(bk_r), BLK2x128, LSR, None)],
                       deps=[t_r])
            t_en = pb.comp("vector", lambda e, hp=hp, bk_r=bk_r: e.tensor_tensor(
                Ebuf[:, hp * 512:(hp + 1) * 512], Ebuf[:, hp * 512:(hp + 1) * 512],
                ctx.bank(bk_r), op=ALU.mult), deps=[t_br] + e_toks)
            ctx.bank_tok[bk_r] = t_en
            if stage <= 6:
                last_exp = e_toks[-1]
                last_r = t_r
                last_br = t_br
                prev_ta = t_en
                hp_toks.append(t_en)
                continue
            aterms = []
            for j in range(8):
                tcl = j // 2
                for h2 in range(2):
                    h = 2 * hp + h2
                    b0 = h2 * 64
                    src = VT if (j % 2) == h2 else VT2t
                    off = mi * 4096 if src is VT else 0
                    aterms.append((
                        ctx.bank(bk_o)[b0:b0 + 64, j * 64:j * 64 + 64],
                        src[b0:b0 + 64, off + tcl * 1024 + h * 64:
                            off + tcl * 1024 + h * 64 + 64],
                        Ebuf[b0:b0 + 64, hp * 512 + j * 64:hp * 512 + (j + 1) * 64],
                        (b0, b0)))
            t_a = mmg(ctx, bk_o, aterms, deps=[t_en, vsw_tok, v_tok],
                      independent=True)
            t_ev = pb.comp("scalar", lambda e, hp=hp, bk_o=bk_o: e.copy(
                ATt[:, hp * 512:(hp + 1) * 512], ctx.bank(bk_o)), deps=[t_a])
            ctx.bank_tok[bk_o] = t_ev
            hp_toks.append(t_ev)
            last_exp = e_toks[-1]
            last_r = t_r
            last_br = t_br
            prev_ta = t_a
        if stage >= 7:
            at_dma = pb.dma("sync", rearr_kpm(io["sc_A"])[:, :, mi * MT:(mi + 1) * MT],
                            ATt.rearrange("p (c m) -> p c m", c=8),
                            deps=hp_toks + [at_dma], channel="ascA")
        else:
            at_dma = None

    pb.barrier()
    if ph1_only or stage <= 8:
        return []

    AT2 = sa("AT2", SBMAP["E1"], [128, 8192], BF16)
    X1ACC = sa("X1ACC", SBMAP["BIG_D"], [128, 8192], F32)
    BCOMB = sa("BCOMB", SBMAP["ATt"], [1, 1024], BF16)
    GROWE = sa("GROWE", SBMAP["LS"], [1, 1024], BF16)
    RB1S = sa("RB1S", SBMAP["ATt"] + 2048, [1, 128], BF16)
    RB1Sb = sa("RB1Sb", SBMAP["ATt"] + 2304, [1, 128], BF16)
    U2 = sa("U2", SBMAP["E1"], [128, 4096], BF16)
    ctx.slab_last.setdefault("u2", None)
    SELs = [sa(f"SEL{e}", SBMAP["LSR"] + 256 * e, [4, 128], BF16)
            for e in range(4)]

    def xn_ap(c, q):
        return BIG_B[:, c * 1024 + q * 512:c * 1024 + q * 512 + 512]

    def xg_ap(c, q):
        return BIG_B[:, 8192 + c * 1024 + q * 512:8192 + c * 1024 + q * 512 + 512]

    def rh_ap(fc, q):
        big = BIG_A if fc < 16 else BIG_C
        f = fc % 16
        return big[:, f * 1024 + q * 512:f * 1024 + q * 512 + 512]

    def acc_ap(oc, q):
        return X1ACC[:, oc * 1024 + q * 512:oc * 1024 + q * 512 + 512]

    rowa = RAWF[0:1, 0:1024]
    rowb = RAWF[0:1, 1024:2048]
    rowc = RAWF[0:1, 2048:3072]
    ta = pb.dma("gpsimd", rowa, io["sb2"][0].unsqueeze(0))
    tb = pb.dma("gpsimd", rowb, io["sb2"][1].unsqueeze(0))
    tc = pb.dma("gpsimd", rowc, io["b_proj"].unsqueeze(0))
    t1 = pb.comp("vector", lambda e: e.tensor_add(rowa, rowa, rowb), deps=[tc])
    bcomb_tok = pb.comp("vector", lambda e: e.scalar_tensor_tensor(
        BCOMB, rowa, 0.5, rowc, op0=ALU.mult, op1=ALU.add), deps=[t1, tc])
    sel_toks = [pb.dma("gpsimd", SELs[e], io[f"cn_sel{e}"], channel="sel")
                for e in range(4)]
    sel_last = sel_toks[-1]

    x1_dmas = []
    mp_prev_done = None
    x1_prev = None
    for mp in range(2):
        pb.phase = "b0.ph2.gate"
        gate_toks = []
        xn_toks = []
        raw_free = [bcomb_tok]
        for q in range(2):
            mi = 2 * mp + q
            x0_dep = pb.dma("sync", RAWF.rearrange("p (c m) -> p c m", c=8),
                            rearr_kpm(io["xownT"])[:, :, mi * MT:(mi + 1) * MT],
                            deps=raw_free + [mp_prev_done, x1_prev],
                            channel="rawf")
            t_xn, raw_free = emit_ln(ctx, lambda c: RAWF[:, c * MT:(c + 1) * MT],
                                     x0_dep, lambda c, q=q: xn_ap(c, q))
            xn_toks.append(t_xn)
            for mc in range(4):
                bkg = 6
                gterms = [(ctx.bank(bkg)[:, 0:4],
                           xn_ap(kc, q)[:, mc * 128:(mc + 1) * 128],
                           GW[:, kc * 4:(kc + 1) * 4], None) for kc in range(NCH)]
                gterms.append((ctx.bank(bkg)[:, 0:4], ONES1x128, GB, None))
                tg = mmg(ctx, bkg, gterms, deps=[t_xn, t_gw, t_gb])
                g0 = pb.comp("vector", lambda e, bkg=bkg: e.tensor_reduce(
                    GSC[:, 0:1], ctx.bank(bkg)[:, 0:4], axis=AX.X, op=ALU.max,
                    negate=True), deps=[tg])
                g1 = pb.comp("scalar", lambda e, bkg=bkg: e.activation(
                    GSC[:, 4:8], ctx.bank(bkg)[:, 0:4], AF.Exp,
                    bias=GSC[:, 0:1]), deps=[g0])
                ctx.bank_tok[bkg] = g1
                g2 = pb.comp("vector", lambda e: e.tensor_reduce(
                    GSC[:, 1:2], GSC[:, 4:8], axis=AX.X, op=ALU.add), deps=[g1])
                g3 = pb.comp("vector", lambda e: e.reciprocal(
                    GSC[:, 2:3], GSC[:, 1:2]), deps=[g2])
                g4 = pb.comp("vector", lambda e: e.tensor_scalar_mul(
                    GSC[:, 4:8], GSC[:, 4:8], GSC[:, 2:3]), deps=[g3])
                g5 = pb.comp("vector", lambda e: e.tensor_reduce(
                    GSC[:, 8:9], GSC[:, 4:8], axis=AX.X, op=ALU.max), deps=[g4])
                g6 = pb.comp("vector", lambda e: e.tensor_scalar(
                    GSC[:, 9:13], GSC[:, 4:8], GSC[:, 8:9], None,
                    op0=ALU.is_ge), deps=[g5])
                g7 = pb.comp("vector", lambda e: e.scalar_tensor_tensor(
                    GSC[:, 9:13], GSC[:, 9:13], -1e9, GSC[:, 4:8],
                    op0=ALU.mult, op1=ALU.add), deps=[g6])
                g8 = pb.comp("vector", lambda e: e.tensor_reduce(
                    GSC[:, 13:14], GSC[:, 9:13], axis=AX.X, op=ALU.max), deps=[g7])
                g9 = pb.comp("vector", lambda e: e.tensor_scalar(
                    GSC[:, 9:13], GSC[:, 4:8], GSC[:, 13:14], None,
                    op0=ALU.is_ge), deps=[g8])
                g10 = pb.comp("vector", lambda e: e.tensor_tensor(
                    GBF, GSC[:, 4:8], GSC[:, 9:13], op=ALU.mult), deps=[g9])
                bkt = 7
                tt = pb.comp("tensor", lambda e, bkt=bkt: e.transpose(
                    ctx.bank_bf(bkt)[0:4, 0:128], GBF, IDENT),
                    deps=[g10, t_id, ctx.bank_tok[bkt]])
                tcp = pb.comp("scalar", lambda e, mc=mc, q=q, bkt=bkt: e.copy(
                    GROWS4[:, q * 512 + mc * 128:q * 512 + (mc + 1) * 128],
                    ctx.bank_bf(bkt)[0:4, 0:128]), deps=[tt])
                ctx.bank_tok[bkt] = tcp
                gate_toks.append(tcp)

        pb.phase = "b0.ph2.at2"
        at2_tok = pb.dma("sync", AT2.rearrange("p (c q m) -> p c q m", c=8, q=2),
                         rearr_kpm(io["sc_A"])[:, :, mp * 1024:(mp + 1) * 1024]
                         .rearrange("p c (q m) -> p c q m", q=2),
                         deps=[at_dma, mp_prev_done, ctx.slab_last["u2"]],
                         channel="at2")

        pb.phase = "b0.ph2.base"
        base_last = None
        for q in range(2):
            mi = 2 * mp + q
            x0_dep = pb.dma("sync", RAWF.rearrange("p (c m) -> p c m", c=8),
                            rearr_kpm(io["xownT"])[:, :, mi * MT:(mi + 1) * MT],
                            deps=raw_free + gate_toks[-1:], channel="rawf")
            raw_free = []
            for oc in range(NCH):
                sl = (SLA, SLB)[oc % 2]
                ch = ("sla", "slb")[oc % 2]
                tld = pb.dma("gpsimd", sl,
                             rearr_kpm(io["w_proj"])[:, :, oc * 128:(oc + 1) * 128],
                             deps=[ctx.slab_last[ch]], channel=ch)
                bk = oc % 4
                terms = [(ctx.bank(bk), sl[:, kc * 128:(kc + 1) * 128],
                          AT2[:, kc * 1024 + q * 512:kc * 1024 + q * 512 + 512],
                          None) for kc in range(NCH)]
                terms.append((ctx.bank(bk), BCOMB[:, oc * 128:(oc + 1) * 128],
                              ONES1x512, None))
                tm = mmg(ctx, bk, terms, deps=[tld, at2_tok, bcomb_tok, t_o5])
                ctx.slab_last[ch] = tm
                te = pb.comp("vector", lambda e, bk=bk, oc=oc, q=q: e.tensor_tensor(
                    acc_ap(oc, q), ctx.bank(bk), RAWF[:, oc * MT:(oc + 1) * MT],
                    op=ALU.add), deps=[tm, x0_dep, x1_prev])
                ctx.bank_tok[bk] = te
                base_last = te
            raw_free = [base_last]

        pb.phase = "b0.ph2.experts"
        prev_done = base_last
        for kind, ne in (("r", 4), ("s", 2)):
            for e in range(ne):
                if kind == "r":
                    w1d = io["rw1"][e]
                    w2d = io["rw2"][e]
                    xg_toks = []
                    ge_toks = []
                    for q in range(2):
                        bkg = 6
                        tbg = mmg(ctx, bkg, [(ctx.bank(bkg), SELs[e],
                                              GROWS4[:, q * 512:(q + 1) * 512],
                                              None)],
                                  deps=[sel_last, gate_toks[-1], prev_done])
                        tge = pb.comp("vector", lambda e2, q=q, bkg=bkg:
                                      e2.tensor_copy(
                                          GROWE[:, q * 512:(q + 1) * 512],
                                          ctx.bank(bkg)[0:1, :]),
                                      deps=[tbg, prev_done])
                        ge_toks.append(tge)
                        for c in range(NCH):
                            txg = pb.comp("vector", lambda e2, c=c, q=q, bkg=bkg:
                                          e2.tensor_tensor(
                                              xg_ap(c, q), xn_ap(c, q),
                                              ctx.bank(bkg), op=ALU.mult),
                                          deps=[tbg, prev_done, xn_toks[q]])
                            xg_toks.append(txg)
                        ctx.bank_tok[bkg] = xg_toks[-1]
                    rhs_src = xg_ap
                    ge_dep = ge_toks[-1]
                    rb1_dram = io["rb1"][e]
                    rb2_tok = pb.dma("gpsimd", BV, io["rb2"][e].unsqueeze(0),
                                     deps=[prev_done], channel="bv")
                    stage1_extra = xg_toks[-1]
                else:
                    w1d = io["sw1"][e]
                    w2d = io["sw2"][e]
                    rhs_src = xn_ap
                    ge_dep = None
                    rb1_dram = None
                    tsb = pb.dma("gpsimd", SB1S, io["sb1_pc"][e],
                                 deps=[prev_done], channel="sb1")
                    sb1_tok = pb.comp("scalar", lambda e2: e2.mul(SB1S, SB1S, 0.5),
                                      deps=[tsb])
                    stage1_extra = sb1_tok

                rh_last = None
                rb_war = [None, None]
                for fc in range(32):
                    sl = (SLA, SLB)[fc % 2]
                    ch = ("sla", "slb")[fc % 2]
                    rbuf = (RB1S, RB1Sb)[fc % 2]
                    if kind == "r":
                        trb = pb.dma("gpsimd", rbuf,
                                     rb1_dram[fc * 128:(fc + 1) * 128].unsqueeze(0),
                                     deps=[rb_war[fc % 2], prev_done],
                                     channel="rb1")
                    tld = pb.dma("gpsimd", sl,
                                 rearr_kpm(w1d)[:, :, fc * 128:(fc + 1) * 128],
                                 deps=[ctx.slab_last[ch], prev_done], channel=ch)
                    for q in range(2):
                        bk = fc % 2
                        terms = [(ctx.bank(bk), sl[:, kc * 128:(kc + 1) * 128],
                                  rhs_src(kc, q), None) for kc in range(NCH)]
                        if kind == "r":
                            terms.append((ctx.bank(bk), rbuf,
                                          GROWE[:, q * 512:(q + 1) * 512], None))
                            dep2 = [trb, ge_dep]
                        else:
                            dep2 = []
                        tm = mmg(ctx, bk, terms,
                                 deps=[tld, stage1_extra] + dep2)
                        if kind == "r":
                            if fc % 2 == 0:
                                te = pb.comp("vector", lambda e2, bk=bk, fc=fc, q=q:
                                             e2.tensor_scalar_max(
                                                 rh_ap(fc, q), ctx.bank(bk), 0.0),
                                             deps=[tm])
                            else:
                                te = pb.comp("scalar", lambda e2, bk=bk, fc=fc, q=q:
                                             e2.activation(rh_ap(fc, q),
                                                           ctx.bank(bk), AF.Relu),
                                             deps=[tm])
                        else:
                            te = pb.comp("scalar", lambda e2, bk=bk, fc=fc, q=q:
                                         e2.activation(
                                             rh_ap(fc, q), ctx.bank(bk), AF.Relu,
                                             bias=SB1S[:, fc:fc + 1], scale=0.5),
                                         deps=[tm, sb1_tok])
                        ctx.bank_tok[bk] = te
                        rh_last = te
                    ctx.slab_last[ch] = tm
                    rb_war[fc % 2] = tm

                st2_last = None
                for oc in range(NCH):
                    ubuf = (U1, U2)[oc % 2]
                    uch = ("u1", "u2")[oc % 2]
                    w2_tok = pb.dma("gpsimd", ubuf,
                                    w2d.rearrange("(fc kp) o -> kp fc o", kp=128)
                                    [:, :, oc * 128:(oc + 1) * 128],
                                    deps=[ctx.slab_last[uch],
                                          base_last if oc < 2 else None],
                                    channel=uch)
                    for q in range(2):
                        bk = 2 + (oc * 2 + q) % 2
                        terms = [(ctx.bank(bk), ubuf[:, fc * 128:(fc + 1) * 128],
                                  rh_ap(fc, q), None) for fc in range(32)]
                        if kind == "r":
                            terms.append((ctx.bank(bk),
                                          BV[:, oc * 128:(oc + 1) * 128],
                                          GROWE[:, q * 512:(q + 1) * 512], None))
                            dep2 = [rb2_tok, ge_dep]
                        else:
                            dep2 = []
                        tm = mmg(ctx, bk, terms, deps=[w2_tok, rh_last] + dep2)
                        ctx.slab_last[uch] = tm
                        te = pb.comp("vector", lambda e2, bk=bk, oc=oc, q=q:
                                     e2.tensor_tensor(
                                         acc_ap(oc, q), acc_ap(oc, q),
                                         ctx.bank(bk), op=ALU.add),
                                     deps=[tm, base_last, prev_done])
                        ctx.bank_tok[bk] = te
                        st2_last = te
                prev_done = st2_last

        for q in range(2):
            dd = pb.dma("sync",
                        (lambda x1o=io["x1out"], mp=mp, q=q:
                         rearr_kpm(apv(x1o))[:, :, (2 * mp + q) * MT:
                                             (2 * mp + q + 1) * MT]),
                        X1ACC.rearrange("p (c q m) -> p c q m", c=8, q=2)[:, :, q, :],
                        deps=[prev_done], channel="x1o")
            x1_dmas.append(dd)
        x1_prev = x1_dmas[-1]
        mp_prev_done = prev_done

    return x1_dmas



def build_block0(ph1_only=False, stage=9):
    nc = bass.Bass("TRN2", target_bir_lowering=False, num_devices=8)
    io = {}

    def d_in(name, shape):
        io[name] = nc.dram_tensor(name, list(shape), F32, kind="ExternalInput").ap()

    def d_in_bf(name, shape):
        io[name] = nc.dram_tensor(name, list(shape), BF16, kind="ExternalInput").ap()

    d_in("xownT", [D, HALF])
    d_in("xqT", [D, HALF])
    d_in("xkvT", [D, HALF])
    d_in_bf("w_qkv", [D, 3 * D])
    d_in("b_qkv", [3 * D])
    d_in_bf("w_proj", [D, D])
    d_in("b_proj", [D])
    d_in("gate_w", [D, 4])
    d_in("gate_b", [4])
    d_in_bf("sw1", [2, D, 4096])
    d_in("sb1", [2, 4096])
    d_in_bf("sw2", [2, 4096, D])
    d_in("sb2", [2, D])
    d_in_bf("rw1", [4, D, 4096])
    d_in("rb1", [4, 4096])
    d_in_bf("rw2", [4, 4096, D])
    d_in("rb2", [4, D])
    d_in("bqkv_pc", [128, 24])
    d_in("sb1_pc", [2, 128, 32])
    for k in host_consts():
        d_in_bf(k, host_consts()[k].shape)
    io["sc_A"] = nc.dram_tensor("sc_A", [D, HALF], BF16, kind="Internal").ap()
    io["x1out"] = nc.dram_tensor("x1out", [D, HALF], F32, kind="ExternalOutput").ap()

    ctx = Ctx(nc)
    x1_dmas = emit_block0(ctx, io, ph1_only=ph1_only, stage=stage)
    if not x1_dmas:
        ctx.pb.barrier()
    ctx.pb.waits_only("sync", x1_dmas[-1:])
    ctx.pb.realize()
    return nc


def stage_block0_inputs(inputs, core):
    b, h = core // 2, core % 2
    own = h * HALF
    x0 = np.asarray(inputs["L0_attn_x"])
    qidx, kidx = host_gather_indices(np.asarray(inputs["L0_attn_mapping"]), own)
    cm = {k: np.ascontiguousarray(v) for k, v in host_consts().items()}
    m = dict(
        xownT=np.ascontiguousarray(x0[b, own:own + HALF, :].T),
        xqT=np.ascontiguousarray(x0[b, qidx, :].T),
        xkvT=np.ascontiguousarray(x0[b, kidx, :].T),
        w_qkv=np.asarray(inputs["L0_attn_Wqkv"]).astype(BF),
        b_qkv=np.asarray(inputs["L0_attn_bqkv"]),
        w_proj=np.asarray(inputs["L0_attn_Wproj"]).astype(BF),
        b_proj=np.asarray(inputs["L0_attn_bproj"]),
        gate_w=np.asarray(inputs["L0_ffn_gate_w"]),
        gate_b=np.asarray(inputs["L0_ffn_gate_b"]),
        sw1=np.asarray(inputs["L0_ffn_sw1"]).astype(BF),
        sb1=np.asarray(inputs["L0_ffn_sb1"]),
        sw2=np.asarray(inputs["L0_ffn_sw2"]).astype(BF),
        sb2=np.asarray(inputs["L0_ffn_sb2"]),
        rw1=np.asarray(inputs["L0_ffn_rw1"]).astype(BF),
        rb1=np.asarray(inputs["L0_ffn_rb1"]),
        rw2=np.asarray(inputs["L0_ffn_rw2"]).astype(BF),
        rb2=np.asarray(inputs["L0_ffn_rb2"]),
        bqkv_pc=np.ascontiguousarray(
            np.asarray(inputs["L0_attn_bqkv"]).reshape(24, 128).T),
        sb1_pc=np.ascontiguousarray(
            np.asarray(inputs["L0_ffn_sb1"]).reshape(2, 32, 128).transpose(0, 2, 1)),
        **cm,
    )
    return m


def run_block0(inputs):
    nc = build_block0()
    in_maps = [stage_block0_inputs(inputs, c) for c in range(8)]
    res = run_bass_kernel_spmd(nc, in_maps, core_ids=list(range(8)))
    x1 = np.zeros((B, N, D), dtype=np.float32)
    for c in range(8):
        b, h = c // 2, c % 2
        x1[b, h * HALF:(h + 1) * HALF, :] = res.results[c]["x1out"].T
    return x1


def run_block1(inputs, x1):
    nc = build_block1()
    in_maps = [stage_block1_inputs(inputs, c, x1) for c in range(8)]
    res = run_bass_kernel_spmd(nc, in_maps, core_ids=list(range(8)))
    out = np.zeros((B, N, D), dtype=np.float32)
    for c in range(8):
        b, h = c // 2, c % 2
        out[b, h * HALF:(h + 1) * HALF, :] = res.results[c]["out"].T
    return out


def build_fused():
    nc = bass.Bass("TRN2", target_bir_lowering=False, num_devices=8)
    io = {}

    def d_in(name, shape, dtype=F32):
        io[name] = nc.dram_tensor(name, list(shape), dtype,
                                  kind="ExternalInput").ap()

    for name, shape in [
            ("xownT", [D, HALF]), ("xqT", [D, HALF]), ("xkvT", [D, HALF]),
            ("b_qkv", [3 * D]),
            ("b_proj", [D]), ("gate_w", [D, 4]), ("gate_b", [4]),
            ("sb1", [2, 4096]),
            ("sb2", [2, D]), ("rb1", [4, 4096]),
            ("rb2", [4, D]),
            ("bqkv_pc", [128, 24]), ("sb1_pc", [2, 128, 32]),
            ("bg", [D]),
            ("bp", [D]), ("bg_pc", [128, 8]),
            ("bp_pc", [128, 8]), ("szero", [128, 1]),
            ("lng_pc", [2, 128, 8]), ("lnb_pc", [2, 128, 8]),
            ("b1p", [2, 5504]),
            ("b2f", [2, D]),
            ("qs_row", [2, 1, 512]), ("ks_row", [2, 1, 512]),
            ("rot", [4, 4096, 64])]:
        d_in(name, shape)
    for name, shape in [
            ("w_qkv", [D, 3 * D]), ("w_proj", [D, D]),
            ("sw1", [2, D, 4096]), ("sw2", [2, 4096, D]),
            ("rw1", [4, D, 4096]), ("rw2", [4, 4096, D]),
            ("w_qkv1", [D, 3 * D]), ("wg", [D, D]), ("wp", [D, D]),
            ("wqkv_f", [2, D, 1536]), ("wo_f", [2, 512, D]),
            ("w1p", [2, D, 5504]), ("w2p", [2, 2752, D])]:
        d_in(name, shape, BF16)
    d_in("masks", [4, 1024, 512], BF16)
    d_in("masksT", [4, 512, 1024], BF16)
    d_in("nonce", [1, 1], I32)
    for k, v in host_consts().items():
        d_in(k, v.shape, BF16)
    io["sc_A"] = nc.dram_tensor("sc_A", [D, HALF], BF16, kind="Internal").ap()
    for nm, shp, dt in [("sc_xn1b", [D, 4104], BF16), ("sc_kp", [4096, D], BF16),
                        ("sc_v", [4096, 1040], BF16), ("sc_s1", [D, HALF], F32),
                        ("sc_xa", [D, 3588], F32), ("sc_xab", [D, 3588], BF16),
                        ("sc_y1", [D, 3080], F32), ("sc_y1b", [D, 3080], BF16),
                        ("sc_xb", [D, 2564], F32), ("sc_xbb", [D, 2564], BF16)]:
        io[nm] = nc.dram_tensor(nm, shp, dt, kind="Internal").ap()
    io["out"] = nc.dram_tensor("out", [D, HALF], F32, kind="ExternalOutput").ap()
    x1loc = nc.dram_tensor("x1loc", [D, HALF], F32, kind="Internal").ap()
    x1prt = nc.dram_tensor("x1prt", [D, HALF], F32, kind="Internal").ap()
    x1sh = nc.dram_tensor("x1sh", [8, D, HALF], F32, kind="Internal",
                          addr_space="Shared").ap()
    flags = nc.dram_tensor("flags", [8, 16], I32, kind="Internal",
                           addr_space="Shared").ap()

    ctx = Ctx(nc)
    build_fused.last_ctx = ctx
    pb = ctx.pb
    pb.want_pid = True
    io["x1out"] = x1loc
    io["x1self"] = x1loc
    io["x1part"] = x1prt
    x1_dmas = emit_block0(ctx, io)

    NB = ctx.sb_at("NB", 206656, [1, 1], I32)
    PFLG = ctx.sb_at("PFLG", 206688, [1, 1], I32)
    exs = nc.alloc_semaphore("exs")
    expoll = nc.alloc_semaphore("expoll")

    def spin(sy):
        sy.dma_start(out=NB, in_=io["nonce"]).then_inc(exs, 16)
        sy.dma_start(out=x1sh[pb.pid], in_=x1loc).then_inc(exs, 16)
        sy.wait_ge(exs, 32)
        sy.dma_start(out=flags[pb.pid, 0:1].unsqueeze(0),
                     in_=NB).then_inc(exs, 16)
        sy.wait_ge(exs, 48)
        with (sy.register("rn") as rn, sy.register("rf") as rf,
              sy.register("rit") as rit, sy.register("rcont") as rcont,
              sy.register("rexp") as rexp, sy.register("rtmp") as rtmp):
            sy.load(rn, NB)
            sy.reg_mov(rit, 0)
            sy.reg_mov(rcont, 1)
            sy.reg_mov(rexp, 0)
            with sy.While(rcont):
                sy.reg_add(rexp, rexp, 16)
                sy.dma_start(out=PFLG, in_=flags[pb.ppid, 0:1].unsqueeze(0)
                             ).then_inc(expoll, 16)
                sy.wait_ge(expoll, rexp)
                sy.load(rf, PFLG)
                sy.reg_add(rit, rit, 1)
                sy.reg_alu(rcont, rf, rn, ALU.not_equal)
                sy.reg_alu(rtmp, rit, 200000, ALU.is_lt)
                sy.reg_alu(rcont, rcont, rtmp, ALU.bitwise_and)
        sy.dma_start(out=x1prt, in_=x1sh[pb.ppid]).then_inc(exs, 16)
        sy.wait_ge(exs, 64)
    pb.raw("sync", spin, deps=x1_dmas)

    out_dmas = emit_block1(ctx, io, stage=19)
    pb.waits_only("sync", out_dmas[-1:])
    pb.realize()
    return nc


def stage_fused_inputs(inputs, core, nonce):
    m = stage_block0_inputs(inputs, core)
    m.update(stage_block1_inputs(inputs, core, None))
    m["nonce"] = nonce
    return m


def kernel(**inputs):
    nc = build_fused()
    nonce = np.array([[int(np.random.default_rng().integers(1 << 20, 1 << 30))]],
                     dtype=np.int32)
    in_maps = [stage_fused_inputs(inputs, c, nonce) for c in range(8)]
    res = run_bass_kernel_spmd(nc, in_maps, core_ids=list(range(8)))
    out = np.zeros((B, N, D), dtype=np.float32)
    for c in range(8):
        b, h = c // 2, c % 2
        out[b, h * HALF:(h + 1) * HALF, :] = res.results[c]["out"].T
    return out


TOFF = 2049
GSPAN = 2560
XA_OFF = 1536
Y1_OFF = 1028
XB_OFF = 512

SB2 = dict(
    QTH=0,
    KTH=40960,
    VTOK=81920,
    WQS=102400,
    WKS=110592,
    WVS=118784,
    YT=126976,
    EB=135296,
    MSK2=143488,
    OT=145536,
    OTMP=149632,
    M2=150656,
    SCR=167040,
)



def emit_block1(ctx, io, stage=19):
    nc, pb = ctx.nc, ctx.pb
    sa = ctx.sb_at
    QTH = sa("QTH", SB2["QTH"], [64, 8 * GSPAN], BF16)
    KTH = sa("KTH", SB2["KTH"], [64, 8 * GSPAN], BF16)
    VTOK = sa("VTOK", SB2["VTOK"], [128, 20 * 512], BF16)
    WQS = sa("WQS", SB2["WQS"], [128, 4096], BF16)
    WKS = sa("WKS", SB2["WKS"], [128, 4096], BF16)
    WVS = sa("WVS", SB2["WVS"], [128, 4096], BF16)
    YT = sa("YT", SB2["YT"], [128, 4160], BF16)
    EB = sa("EB", SB2["EB"], [128, 4096], BF16)
    MSK2 = sa("MSK2", SB2["MSK2"], [128, 1024], BF16)
    OT = sa("OT", SB2["OT"], [128, 2048], BF16)
    OTMP = sa("OTMP", SB2["OTMP"], [64, 512], BF16)
    MSKT = sa("MSKT", 202560, [128, 2048], BF16)
    RAW2 = sa("RAW2", SB2["VTOK"], [128, 4096], F32)
    RAW2F = sa("RAW2F", SB2["QTH"], [128, 4104], F32)
    HP = sa("HP", SB2["QTH"] + 16448, [128, 11264], BF16)
    S1T = sa("S1T", SB2["KTH"], [128, 4096], F32)
    XNown = sa("XNown", SB2["QTH"], [128, 16384], BF16)
    QP = sa("QP", SB2["KTH"], [128, 16384], BF16)
    m = SB2["M2"]

    def mget(name, shape, dtype):
        nonlocal m
        nbytes = int(np.prod(shape[1:])) * mybir.dt.size(dtype)
        ap = sa(name, m, shape, dtype)
        m += (nbytes + 31) // 32 * 32
        assert m <= 167040, (name, m)
        return ap

    KVS = mget("KVS", [128, 2048], BF16)
    scr = SB2["SCR"]
    GT2 = sa("GT2", scr, [128, 512], F32)
    QTMP = sa("QTMP", scr + 2048, [128, 512], F32)
    ETMP = sa("ETMP", scr + 4096, [128, 512], F32)
    RNR = sa("RNR", scr + 6144, [128, 512], F32)
    VPB = sa("VPB", scr + 8192, [128, 1040], BF16)
    KPB = sa("KPB", scr + 10272, [128, 512], BF16)
    RNB = sa("RNB", scr + 11296, [128, 512], BF16)
    QON = sa("QON", SB2["EB"], [128, 4096], BF16)
    ONESF = mget("ONESF", [128, 128], BF16)
    S2R = mget("S2R", [128, 16], F32)
    BQS = mget("BQS", [128, 1024], BF16)
    MXR = mget("MXR", [1, 512], BF16)
    NMC = mget("NMC", [128, 32], F32)
    NMCB = mget("NMCB", [128, 32], BF16)
    ROT = mget("ROT", [128, 256], F32)
    GPC = mget("GPC", [128, 8], F32)
    BPC = mget("BPC", [128, 8], F32)
    LNG = mget("LNG", [128, 16], F32)
    LNB = mget("LNB", [128, 16], F32)
    B2R = mget("B2R", [1, 1024], BF16)
    B1R = mget("B1R", [1, 128], BF16)
    B1RB = mget("B1RB", [1, 128], BF16)
    bgr_off = m
    BGR = mget("BGR", [1, 1024], BF16)
    RESF = sa("RESF", bgr_off, [128, 512], F32)
    m = bgr_off + 2048
    QSR = mget("QSR", [1, 1024], BF16)
    SZC = mget("SZC", [128, 1], F32)
    ZC8 = mget("ZC8", [128, 8], BF16)
    TR1 = mget("TR1", [128, 128], BF16)

    ones_row = ctx.sb["ONES1x128"]
    onesf_col = ctx.sb["ONES128x1F"]
    ones_col = ctx.sb["ONES128x1"]
    ident = ctx.sb["IDENT"]
    ones512 = ctx.sb["ONES1x512"]

    pb.barrier()
    t_in = []
    t_in.append(pb.comp("gpsimd", lambda e: e.memset(ONESF, 1.0)))
    t_in.append(pb.comp("gpsimd", lambda e: e.memset(ZC8, 0.0)))
    t_in.append(pb.dma("gpsimd", SZC, io["szero"], channel="c2"))
    t_in.append(pb.dma("gpsimd", GPC, io["bg_pc"], channel="c2"))
    t_in.append(pb.dma("gpsimd", BPC, io["bp_pc"], channel="c2"))
    t_in.append(pb.dma("gpsimd", LNG.rearrange("p (l c) -> p l c", l=2),
                       io["lng_pc"].rearrange("l p c -> p l c"), channel="c2"))
    t_in.append(pb.dma("gpsimd", LNB.rearrange("p (l c) -> p l c", l=2),
                       io["lnb_pc"].rearrange("l p c -> p l c"), channel="c2"))
    t_in.append(pb.dma("gpsimd", BGR, io["bg"].unsqueeze(0), channel="c2"))
    pb.barrier()

    XN1B = io["sc_xn1b"]

    pb.phase = "b1.ph3a"
    lnfree = []
    ln_last = None
    for k in range(8):
        c0 = 1 + k * 512
        src = io["x1part"] if k < 4 else io["x1self"]
        s0 = (c0 - 1) % 2048
        dep = pb.dma("sync", RAW2.rearrange("p (c m) -> p c m", c=8),
                     (lambda src=src, s0=s0:
                      rearr_kpm(apv(src))[:, :, s0:s0 + 512]),
                     deps=lnfree + [ln_last],
                     channel="rawf")
        t_xn, lnfree = emit_ln(ctx, lambda c: RAW2[:, c * MT:(c + 1) * MT], dep,
                               lambda c: YT[:, c * MT:(c + 1) * MT])
        ln_last = pb.dma("sync", rearr_kpm(XN1B)[:, :, c0:c0 + 512],
                         YT[:, 0:4096].rearrange("p (c m) -> p c m", c=8),
                         deps=[t_xn, ln_last], channel="xn1o")
    z0 = pb.dma("sync", rearr_kpm(XN1B)[:, :, 0:1],
                ZC8.rearrange("p (c m) -> p c m", m=1), deps=[ln_last],
                channel="zc", allow_slow_non_contiguous=True)
    xn1_done = [z0, ln_last]
    pb.barrier()
    if stage <= 11:
        return []

    pb.phase = "b1.ph3b"
    vones = pb.comp("gpsimd", lambda e: e.memset(VPB, 1.0))
    kvz = pb.comp("gpsimd", lambda e: e.memset(KVS, 0.0))
    slabs_tok = None
    xt_dep = None
    kp_dma = None
    for nh in range(2):
        d0 = [slabs_tok]
        tw_g = pb.dma("gpsimd", WQS, rearr_kpm(io["wg"])[:, :, nh * 512:(nh + 1) * 512],
                      deps=d0, channel="sla")
        tw_k = pb.dma("gpsimd", WKS,
                      rearr_kpm(io["w_qkv1"])[:, :, 1024 + nh * 512:1536 + nh * 512],
                      deps=d0, channel="slb")
        tw_v = pb.dma("gpsimd", WVS,
                      rearr_kpm(io["w_qkv1"])[:, :, 2048 + nh * 512:2560 + nh * 512],
                      deps=d0, channel="u1")
        xt_tok = None
        for k8 in range(8):
            c0 = 1 + k8 * 512
            xt_tok = pb.dma("sync", YT[:, 0:4096].rearrange("p (c m) -> p c m", c=8),
                            rearr_kpm(XN1B)[:, :, c0:c0 + 512],
                            deps=[xt_dep] + xn1_done, channel="rawf")
            for mc in range(4):
                l0 = k8 * 512 + mc * 128
                lt = lambda kc, mc=mc: YT[:, kc * 512 + mc * 128:kc * 512 + (mc + 1) * 128]
                tg = mmg(ctx, 0, [(ctx.bank(0), lt(kc), WQS[:, kc * 512:(kc + 1) * 512],
                                   None) for kc in range(NCH)] +
                         [(ctx.bank(0), ones_row, BGR[:, nh * 512:(nh + 1) * 512], None)],
                         deps=[tw_g, xt_tok])
                tk = mmg(ctx, 1, [(ctx.bank(1), lt(kc), WKS[:, kc * 512:(kc + 1) * 512],
                                   None) for kc in range(NCH)],
                         deps=[tw_k, xt_tok])
                tv = mmg(ctx, 2, [(ctx.bank(2), lt(kc), WVS[:, kc * 512:(kc + 1) * 512],
                                   None) for kc in range(NCH)],
                         deps=[tw_v, xt_tok])
                xt_dep = tv
                eg = pb.comp("scalar", lambda e: e.activation(GT2, ctx.bank(0),
                                                              AF.Sigmoid),
                             deps=[tg, kp_dma])
                ctx.bank_tok[0] = eg
                e1 = pb.comp("vector", lambda e: e.tensor_tensor(
                    QTMP, ctx.bank(1), GT2, op=ALU.mult), deps=[tk, eg])
                ctx.bank_tok[1] = e1
                e2 = pb.comp("vector", lambda e: e.tensor_scalar_min(
                    ETMP, QTMP, 0.0), deps=[e1])
                e3 = pb.comp("scalar", lambda e: e.activation(ETMP, ETMP, AF.Exp),
                             deps=[e2])
                e4 = pb.comp("vector", lambda e: e.tensor_scalar_max(
                    QTMP, QTMP, 0.0), deps=[e3, e1])
                e5 = pb.comp("vector", lambda e: e.tensor_add(KPB, ETMP, QTMP),
                             deps=[e4, e3])
                ev = pb.comp("vector", lambda e, nh=nh: e.tensor_copy(
                    VPB[:, nh * 520:(nh + 1) * 520]
                    .rearrange("p (h c) -> p h c", c=65)[:, :, 0:64],
                    ctx.bank(2).rearrange("p (h c) -> p h c", c=64)),
                    deps=[tv, kp_dma])
                ctx.bank_tok[2] = ev
                kp_dma = pb.dma("sync", io["sc_kp"][l0:l0 + 128,
                                                    nh * 512:(nh + 1) * 512],
                                KPB, deps=[e5], channel="kpo")
                kp_dma = pb.dma("sync", io["sc_v"][l0:l0 + 128,
                                                   nh * 520:(nh + 1) * 520],
                                VPB[:, nh * 520:(nh + 1) * 520],
                                deps=[ev, vones], channel="kpo")
        slabs_tok = kp_dma
    kv_evs = None
    for hg in range(4):
        t_ld = None
        kv_toks = {}
        for chunk in range(32):
            l0 = chunk * 128
            tka = pb.dma("sync", YT[:, 0:1024], io["sc_kp"][l0:l0 + 128, :],
                         deps=[t_ld, kp_dma, kv_evs, xt_dep], channel="rawf")
            tvb = pb.dma("sync", YT[:, 1024:2064],
                         io["sc_v"][l0:l0 + 128, :],
                         deps=[t_ld, kp_dma, kv_evs, xt_dep], channel="rawf")
            for hl in range(4):
                h = hg * 4 + hl
                bk = 4 + hl
                kv_toks[hl] = mmg(
                    ctx, bk,
                    [(ctx.bank(bk)[0:64, 0:65],
                      YT[:, h * 64:(h + 1) * 64],
                      YT[:, 1024 + h * 65:1024 + (h + 1) * 65], None)],
                    deps=[tvb], cont=(chunk > 0))
            t_ld = kv_toks[3]
        evs = []
        for hl in range(4):
            h = hg * 4 + hl
            evs.append(pb.comp("scalar", lambda e, h=h, hl=hl: e.copy(
                KVS[0:64, h * 128:h * 128 + 65],
                ctx.bank(4 + hl)[0:64, 0:65]), deps=[kv_toks[hl]]))
            ctx.bank_tok[4 + hl] = evs[-1]
        kv_evs = evs[-1]
    kvsw = pb.dma("sync", KVS[64:128, :], KVS[0:64, :], deps=[kv_evs, kvz],
                  channel="zc")
    pb.barrier()
    if stage <= 12:
        return []

    pb.phase = "b1.ph3c"
    xo_tok = pb.dma("sync", XNown.rearrange("p (c m) -> p c m", c=8),
                    rearr_kpm(XN1B)[:, :, TOFF:TOFF + 2048], channel="rawf")
    s1_dmas = []
    prev_f5 = None
    a1_last = None
    for mi in range(NMT):
        for ni in range(NCH):
            sl = (ctx.sb["SLA"], ctx.sb["SLB"])[ni % 2]
            ch = ("sla", "slb")[ni % 2]
            tldq = pb.dma("gpsimd", sl,
                          rearr_kpm(io["w_qkv1"])[:, :, ni * 128:(ni + 1) * 128],
                          deps=[ctx.slab_last[ch]], channel=ch)
            bk = ni % 2
            tmq = mmg(ctx, bk, [(ctx.bank(bk),
                                 sl[:, kc * 128:(kc + 1) * 128],
                                 XNown[:, kc * 2048 + mi * MT:kc * 2048 + (mi + 1) * MT],
                                 None) for kc in range(NCH)],
                      deps=[tldq, xo_tok])
            eq = pb.comp("scalar", lambda e, bk=bk: e.copy(QTMP, ctx.bank(bk)),
                         deps=[tmq, prev_f5,
                               s1_dmas[-1] if s1_dmas else None])
            ctx.bank_tok[bk] = eq
            tldg = pb.dma("gpsimd", sl,
                          rearr_kpm(io["wg"])[:, :, ni * 128:(ni + 1) * 128],
                          deps=[tmq], channel=ch)
            tmg = mmg(ctx, bk, [(ctx.bank(bk),
                                 sl[:, kc * 128:(kc + 1) * 128],
                                 XNown[:, kc * 2048 + mi * MT:kc * 2048 + (mi + 1) * MT],
                                 None) for kc in range(NCH)],
                      deps=[tldg, eq])
            ctx.slab_last[ch] = tmg
            gg = pb.comp("scalar", lambda e, bk=bk, ni=ni: e.activation(
                GT2, ctx.bank(bk), AF.Sigmoid, bias=GPC[:, ni:ni + 1]),
                deps=[tmg, prev_f5])
            ctx.bank_tok[bk] = gg
            f1 = pb.comp("vector", lambda e: e.tensor_mul(QTMP, QTMP, GT2),
                         deps=[gg, eq])
            f2 = pb.comp("vector", lambda e: e.tensor_scalar_min(ETMP, QTMP, 0.0),
                         deps=[f1])
            f3 = pb.comp("scalar", lambda e: e.activation(ETMP, ETMP, AF.Exp),
                         deps=[f2])
            f4 = pb.comp("vector", lambda e: e.tensor_scalar_max(QTMP, QTMP, 0.0),
                         deps=[f3, f1])
            f5 = pb.comp("vector", lambda e, ni=ni, mi=mi: e.tensor_add(
                QP[:, ni * 2048 + mi * MT:ni * 2048 + (mi + 1) * MT], ETMP, QTMP),
                deps=[f4])
            prev_f5 = f5
        qp_tok = f5
        prev_qon = None
        for hp in range(NCH):
            b_qo = 2
            b_nm = 3
            terms = []
            for h2 in range(2):
                h = 2 * hp + h2
                b0 = h2 * 64
                terms.append((ctx.bank(b_qo)[b0:b0 + 64, 0:512],
                              KVS[b0:b0 + 64, h * 128:h * 128 + 64],
                              QP[b0:b0 + 64, hp * 2048 + mi * MT:hp * 2048 + (mi + 1) * MT],
                              (b0, b0)))
            t_qo = mmg(ctx, b_qo, terms, deps=[qp_tok, kvsw], independent=True)
            nterms = []
            for h2 in range(2):
                h = 2 * hp + h2
                b0 = h2 * 64
                nterms.append((ctx.bank(b_nm)[b0:b0 + 1, 0:512],
                               KVS[b0:b0 + 64, h * 128 + 64:h * 128 + 65],
                               QP[b0:b0 + 64, hp * 2048 + mi * MT:hp * 2048 + (mi + 1) * MT],
                               (b0, b0)))
            t_nm = mmg(ctx, b_nm, nterms, deps=[qp_tok, kvsw], independent=True)
            r1 = pb.comp("vector", lambda e, b_nm=b_nm: e.tensor_scalar_add(
                RNR[0:1, :], ctx.bank(b_nm)[0:1, 0:512], 1e-8), deps=[t_nm])
            r2 = pb.comp("vector", lambda e, b_nm=b_nm: e.tensor_scalar_add(
                RNR[64:65, :], ctx.bank(b_nm)[64:65, 0:512], 1e-8), deps=[t_nm])
            ctx.bank_tok[b_nm] = r2
            r3 = pb.comp("vector", lambda e: e.reciprocal(RNR[0:1, :], RNR[0:1, :]),
                         deps=[r1])
            r4 = pb.comp("vector", lambda e: e.reciprocal(RNR[64:65, :], RNR[64:65, :]),
                         deps=[r2])
            r5 = pb.comp("vector", lambda e: e.tensor_copy(RNB[0:1, :], RNR[0:1, :]),
                         deps=[r3])
            r6 = pb.comp("vector", lambda e: e.tensor_copy(RNB[64:65, :], RNR[64:65, :]),
                         deps=[r4])
            bterms = [(ctx.bank(b_nm)[0:64, 0:512], ONESF[0:1, 0:64], RNB[0:1, :],
                       (0, 0)),
                      (ctx.bank(b_nm)[64:128, 0:512], ONESF[64:65, 0:64],
                       RNB[64:65, :], (64, 64))]
            t_br = mmg(ctx, b_nm, bterms, deps=[r5, r6], independent=True)
            qoc = pb.comp("scalar", lambda e, b_qo=b_qo: e.copy(QTMP, ctx.bank(b_qo)),
                          deps=[t_qo, prev_qon])
            ctx.bank_tok[b_qo] = qoc
            qon = pb.comp("vector", lambda e, hp=hp, b_nm=b_nm: e.tensor_tensor(
                QON[:, hp * 512:(hp + 1) * 512], QTMP, ctx.bank(b_nm), op=ALU.mult),
                deps=[qoc, t_br, a1_last])
            ctx.bank_tok[b_nm] = qon
            prev_qon = qon
        x1_dep = pb.dma("sync", RAW2.rearrange("p (c m) -> p c m", c=8),
                        (lambda xs=io["x1self"], mi=mi:
                         rearr_kpm(apv(xs))[:, :, mi * MT:(mi + 1) * MT]),
                        deps=[xo_tok, s1_dmas[-1] if s1_dmas else None],
                        channel="raw2f")
        for oc in range(NCH):
            sl = (ctx.sb["SLA"], ctx.sb["SLB"])[oc % 2]
            ch = ("sla", "slb")[oc % 2]
            tld = pb.dma("gpsimd", sl,
                         rearr_kpm(io["wp"])[:, :, oc * 128:(oc + 1) * 128],
                         deps=[ctx.slab_last[ch]], channel=ch)
            bk = 6 + oc % 2
            terms = [(ctx.bank(bk), sl[:, kc * 128:(kc + 1) * 128],
                      QON[:, kc * 512:(kc + 1) * 512], None) for kc in range(NCH)]
            tm = mmg(ctx, bk, terms, deps=[tld, qon])
            ctx.slab_last[ch] = tm
            a1_last = tm
            te = pb.comp("vector", lambda e, bk=bk, oc=oc: e.scalar_tensor_tensor(
                ETMP, ctx.bank(bk), 1.0, RAW2[:, oc * MT:(oc + 1) * MT],
                op0=ALU.mult, op1=ALU.add),
                deps=[tm, x1_dep, s1_dmas[-1] if s1_dmas else None])
            tb = pb.comp("vector", lambda e, oc=oc: e.tensor_scalar_add(
                ETMP, ETMP, BPC[:, oc:oc + 1]), deps=[te])
            ctx.bank_tok[bk] = tb
            dd = pb.dma("sync",
                        rearr_kpm(io["sc_s1"])[:, oc, mi * MT:(mi + 1) * MT],
                        ETMP, deps=[tb], channel="s1o")
            s1_dmas.append(dd)
    pb.barrier()
    if stage <= 13:
        return []

    envs = [
        dict(ysrc_b=XN1B, ysrc_f=None, yoff=TOFF,
             groups=[(-2048, 512, [-3, -2, -1, 0]), (0, 2048, [1, 2, 3])],
             wo_f32=io["sc_xa"], wo_b16=io["sc_xab"], wo_off=XA_OFF,
             ffn_tiles=[(-1025, -513), (-513, -1), (-1, 511), (511, 1023),
                        (1023, 1535), (1535, 2047), (2047, 2048)],
             ffn_f32=io["sc_y1"], ffn_b16=io["sc_y1b"], fo_off=Y1_OFF,
             final=False, wq_free=None, ysrc_ready=None),
        dict(ysrc_b=io["sc_y1b"], ysrc_f=io["sc_y1"], yoff=Y1_OFF,
             groups=[(-1024, 512, [-1, 0]), (0, 2048, [1, 2, 3])],
             wo_f32=io["sc_xb"], wo_b16=io["sc_xbb"], wo_off=XB_OFF,
             ffn_tiles=[(0, 512), (512, 1024), (1024, 1536), (1536, 2048)],
             final=True, wq_free=None, ysrc_ready=None),
    ]
    out_dmas = None
    if stage == 131:
        envs[0]["qkv_only"] = True
        emit_l1ffn_layer(ctx, io, 0, envs[0])
        pb.barrier()
        return []
    for l in (0, 1):
        if stage <= 13 + l:
            return []
        out_dmas = emit_l1ffn_layer(ctx, io, l, envs[l])
        envs[1]["wq_free"] = (ctx.pb.psem["tensor"], ctx.pb.cnt["tensor"])
        pb.barrier()
    return out_dmas


def build_block1(stage=19):
    nc = bass.Bass("TRN2", target_bir_lowering=False, num_devices=8)
    io = {}

    def d_in(name, shape, dtype=F32):
        io[name] = nc.dram_tensor(name, list(shape), dtype, kind="ExternalInput").ap()

    d_in("x1self", [D, HALF])
    d_in("x1part", [D, HALF])
    d_in("w_qkv1", [D, 3 * D], BF16)
    d_in("wg", [D, D], BF16)
    d_in("bg", [D])
    d_in("wp", [D, D], BF16)
    d_in("bp", [D])
    d_in("bg_pc", [128, 8])
    d_in("bp_pc", [128, 8])
    d_in("szero", [128, 1])
    d_in("lng_pc", [2, 128, 8])
    d_in("lnb_pc", [2, 128, 8])
    d_in("wqkv_f", [2, D, 1536], BF16)
    d_in("wo_f", [2, 512, D], BF16)
    d_in("w1p", [2, D, 5504], BF16)
    d_in("b1p", [2, 5504])
    d_in("w2p", [2, 2752, D], BF16)
    d_in("b2f", [2, D])
    d_in("qs_row", [2, 1, 512])
    d_in("ks_row", [2, 1, 512])
    d_in("rot", [4, 4096, 64])
    d_in("masks", [4, 1024, 512], BF16)
    d_in("masksT", [4, 512, 1024], BF16)
    for k, v in host_consts().items():
        d_in(k, v.shape, BF16)
    io["sc_xn1b"] = nc.dram_tensor("sc_xn1b", [D, 4104], BF16, kind="Internal").ap()
    io["sc_kp"] = nc.dram_tensor("sc_kp", [4096, D], BF16, kind="Internal").ap()
    io["sc_v"] = nc.dram_tensor("sc_v", [4096, 1040], BF16, kind="Internal").ap()
    io["sc_s1"] = nc.dram_tensor("sc_s1", [D, HALF], F32, kind="Internal").ap()
    io["sc_xa"] = nc.dram_tensor("sc_xa", [D, 3588], F32, kind="Internal").ap()
    io["sc_xab"] = nc.dram_tensor("sc_xab", [D, 3588], BF16, kind="Internal").ap()
    io["sc_y1"] = nc.dram_tensor("sc_y1", [D, 3080], F32, kind="Internal").ap()
    io["sc_y1b"] = nc.dram_tensor("sc_y1b", [D, 3080], BF16, kind="Internal").ap()
    io["sc_xb"] = nc.dram_tensor("sc_xb", [D, 2564], F32, kind="Internal").ap()
    io["sc_xbb"] = nc.dram_tensor("sc_xbb", [D, 2564], BF16, kind="Internal").ap()
    io["out"] = nc.dram_tensor("out", [D, HALF], F32, kind="ExternalOutput").ap()
    if stage <= 13:
        io["dbg_s1"] = nc.dram_tensor("dbg_s1", [D, HALF], F32,
                                      kind="ExternalOutput").ap()
        io["sc_s1"] = io["dbg_s1"]

    ctx = Ctx(nc)
    sa = ctx.sb_at
    IDENT = sa("IDENT", SBMAP["IDENT"], [128, 128], BF16)
    ONES1x128 = sa("ONES1x128", SBMAP["ONES1x128"], [1, 128], BF16)
    ONES1x512 = sa("ONES1x512", SBMAP["ONES1x512"], [1, 512], BF16)
    sa("ONES128x1", SBMAP["ONES128x1"], [128, 1], BF16)
    sa("ONES128x1F", SBMAP["ONES128x1F"], [128, 1], F32)
    sa("SLA", SBMAP["SLA"], [128, 1024], BF16)
    sa("SLB", SBMAP["SLB"], [128, 1024], BF16)
    pb = ctx.pb
    pb.comp("gpsimd", lambda e: e.memset(ctx.sb["ONES128x1"], 1.0))
    pb.comp("gpsimd", lambda e: e.memset(ctx.sb["ONES128x1F"], 1.0))
    pb.dma("gpsimd", IDENT, io["cn_ident"], channel="const")
    pb.dma("gpsimd", ONES1x128, io["cn_ones_1x128"], channel="const")
    pb.dma("gpsimd", ONES1x512, io["cn_ones_1x512"], channel="const")
    out_dmas = emit_block1(ctx, io, stage=stage)
    if out_dmas:
        ctx.pb.waits_only("sync", out_dmas[-1:])
    else:
        ctx.pb.barrier()
    ctx.pb.realize()
    return nc


def host_rotary(own_start):
    t = (np.arange(-2048, 2048) + own_start) % N
    tt = t.astype(np.float64)
    half_idx = np.arange(0, 64, 2, dtype=np.float64)
    inv_freq = 1.0 / 10000.0 ** (half_idx / 64)
    base = (half_idx + 0.4 * 64) / (1.4 * 64)
    power = (tt - N // 2) / float(WINDOW // 2)
    fr = tt[:, None] * inv_freq[None, :]
    sc = base[None, :] ** power[:, None]
    cos = np.cos(np.concatenate([fr, fr], 1))
    sin = np.sin(np.concatenate([fr, fr], 1))
    scale = np.concatenate([sc, sc], 1)
    sgn = np.concatenate([-np.ones((1, 32)), np.ones((1, 32))], 1)
    return np.stack([cos * scale, sin * scale * sgn,
                     cos / scale, sin / scale * sgn]).astype(np.float32)


def host_l1_masks(is_second_half):
    sarr = np.arange(WINDOW)[:, None]
    marr = np.arange(WINDOW)[None, :]
    prev = np.where(sarr >= marr, 0.0, NEG).astype(np.float32)
    curr = np.where(sarr <= marr, 0.0, NEG).astype(np.float32)
    normal = np.concatenate([prev, curr], axis=0)
    masked0 = np.concatenate([np.full((WINDOW, WINDOW), NEG, np.float32), curr],
                             axis=0)
    out = np.stack([normal] * 4, axis=0)
    if not is_second_half:
        out[0] = masked0
    return out.astype(BF)


def host_l1_masksT(is_second_half):
    m = np.asarray(host_l1_masks(is_second_half), np.float32)
    return np.ascontiguousarray(m.transpose(0, 2, 1)).astype(BF)


def host_geglu_permute(w1, b1):
    L = w1.shape[0]
    w1p = np.zeros((L, 1024, 5504), w1.dtype)
    b1p = np.zeros((L, 5504), b1.dtype)
    c, fc = 0, 0
    while c < HID:
        wdt = min(64, HID - c)
        w1p[:, :, fc * 128:fc * 128 + wdt] = w1[:, :, c:c + wdt]
        w1p[:, :, fc * 128 + 64:fc * 128 + 64 + wdt] = w1[:, :, HID + c:HID + c + wdt]
        b1p[:, fc * 128:fc * 128 + wdt] = b1[:, c:c + wdt]
        b1p[:, fc * 128 + 64:fc * 128 + 64 + wdt] = b1[:, HID + c:HID + c + wdt]
        c += 64
        fc += 1
    return w1p, b1p


def host_w2_pad(w2):
    L = w2.shape[0]
    w2p = np.zeros((L, 2752, 1024), w2.dtype)
    c, fc = 0, 0
    while c < HID:
        wdt = min(64, HID - c)
        w2p[:, fc * 64:fc * 64 + wdt] = w2[:, c:c + wdt]
        c += 64
        fc += 1
    return w2p


def stage_block1_inputs(inputs, core, x1):
    b, h = core // 2, core % 2
    own = h * HALF
    cm = {k: np.ascontiguousarray(v) for k, v in host_consts().items()}
    w1p, b1p = host_geglu_permute(np.asarray(inputs["L1_ffn_w1"]),
                                  np.asarray(inputs["L1_ffn_b1"]))
    qs = np.asarray(inputs["L1_ffn_q_scale"])
    ks = np.asarray(inputs["L1_ffn_k_scale"])
    m = dict(
        w_qkv1=np.asarray(inputs["L1_attn_Wqkv"]).astype(BF),
        wg=np.asarray(inputs["L1_attn_Wg"]).astype(BF),
        bg=np.asarray(inputs["L1_attn_bg"]),
        wp=np.asarray(inputs["L1_attn_Wp"]).astype(BF),
        bp=np.asarray(inputs["L1_attn_bp"]),
        bg_pc=np.ascontiguousarray(np.asarray(inputs["L1_attn_bg"]).reshape(8, 128).T),
        bp_pc=np.ascontiguousarray(np.asarray(inputs["L1_attn_bp"]).reshape(8, 128).T),
        szero=np.full((128, 1), 0.0 if h == 0 else 1.0, np.float32),
        lng_pc=np.ascontiguousarray(np.asarray(inputs["L1_ffn_ln_g"])
                                    .reshape(2, 8, 128).transpose(0, 2, 1)),
        lnb_pc=np.ascontiguousarray(np.asarray(inputs["L1_ffn_ln_b"])
                                    .reshape(2, 8, 128).transpose(0, 2, 1)),
        wqkv_f=np.asarray(inputs["L1_ffn_wqkv"]).astype(BF),
        wo_f=np.asarray(inputs["L1_ffn_wo"]).astype(BF),
        w1p=w1p.astype(BF),
        b1p=b1p,
        w2p=host_w2_pad(np.asarray(inputs["L1_ffn_w2"])).astype(BF),
        b2f=np.asarray(inputs["L1_ffn_b2"]),
        qs_row=np.ascontiguousarray(
            np.tile(qs, (1, 8)).reshape(2, 1, 512) * QK_SCALE),
        ks_row=np.ascontiguousarray(np.tile(ks, (1, 8)).reshape(2, 1, 512)),
        rot=host_rotary(own),
        masks=host_l1_masks(h == 1),
        masksT=host_l1_masksT(h == 1),
        **cm,
    )
    if x1 is not None:
        m["x1self"] = np.ascontiguousarray(x1[b, own:own + HALF, :].T)
        m["x1part"] = np.ascontiguousarray(
            x1[b, (own + HALF) % N:(own + HALF) % N + HALF, :].T)
    return m


def emit_l1ffn_layer(ctx, io, l, env):
    nc, pb = ctx.nc, ctx.pb
    sb = ctx.sb
    QTH, KTH, VTOK = sb["QTH"], sb["KTH"], sb["VTOK"]
    WQS, WKS, WVS = sb["WQS"], sb["WKS"], sb["WVS"]
    YT, EB, MSK2, OT, OTMP = sb["YT"], sb["EB"], sb["MSK2"], sb["OT"], sb["OTMP"]
    MSKT = sb["MSKT"]
    HP, RAW2F, S1T = sb["HP"], sb["RAW2F"], sb["S1T"]
    GT2, QTMP, ETMP, RNR, RNB = sb["GT2"], sb["QTMP"], sb["ETMP"], sb["RNR"], sb["RNB"]
    RESF = sb["RESF"]
    KPB, VPB = sb["KPB"], sb["VPB"]
    S2R, BQS, ROT = sb["S2R"], sb["BQS"], sb["ROT"]
    LNG, LNB, B2R, B1R, SZC = sb["LNG"], sb["LNB"], sb["B2R"], sb["B1R"], sb["SZC"]
    MXRrow = sb["RNB"]
    ones_row = sb["ONES1x128"]
    ones_col = sb["ONES128x1"]
    ones512 = sb["ONES1x512"]
    ident = sb["IDENT"]
    TMP128 = sb["TR1"]
    NMC = sb["NMC"]
    NMCB = sb["NMCB"]
    MXR = sb["MXR"]

    ysrc = env["ysrc_b"]
    yoff = env["yoff"]
    ysrc_f = env["ysrc_f"]

    t_qsr = pb.dma("gpsimd", sb["QSR"][0:1, 0:512], io["qs_row"][l], channel="c2")
    t_ksr = pb.dma("gpsimd", sb["QSR"][0:1, 512:1024], io["ks_row"][l],
                   channel="c2")
    t_b2 = pb.dma("gpsimd", B2R, io["b2f"][l].unsqueeze(0), channel="bv")
    bq1 = mmg(ctx, 6, [(ctx.bank(6), ones_row, sb["QSR"][0:1, 0:512], None)],
              deps=[t_ksr])
    cb1 = pb.comp("vector", lambda e: e.tensor_copy(BQS[:, 0:512], ctx.bank(6)),
                  deps=[bq1])
    ctx.bank_tok[6] = cb1
    bq2 = mmg(ctx, 6, [(ctx.bank(6), ones_row, sb["QSR"][0:1, 512:1024], None)],
              deps=[t_ksr])
    cb2 = pb.comp("vector", lambda e: e.tensor_copy(BQS[:, 512:1024], ctx.bank(6)),
                  deps=[bq2])
    ctx.bank_tok[6] = cb2
    pb.barrier()

    wo_last = None
    for (t0, t1, wlist) in env["groups"]:
        for eng in ("vector", "scalar", "tensor"):
            pb.waits_only(eng, [env.get("xao_last"), env.get("res_last")])
        ntile = (t1 - t0) // 512
        wdeps = [env.get("wq_free"), env.get("msk_free")]
        env["wq_free"] = None
        t_wq = pb.dma("gpsimd", WQS, rearr_kpm(io["wqkv_f"][l])[:, :, 0:512],
                      deps=wdeps, channel="sla")
        t_wk = pb.dma("gpsimd", WKS, rearr_kpm(io["wqkv_f"][l])[:, :, 512:1024],
                      deps=wdeps, channel="slb")
        t_wv = pb.dma("gpsimd", WVS, rearr_kpm(io["wqkv_f"][l])[:, :, 1024:1536],
                      deps=wdeps, channel="u1")
        pb.phase = f"L{l}.qkv"
        yt_tok = env.get("msk_free")
        last_split = None
        prev_tt = None
        for ti in range(ntile):
            mt0 = t0 + ti * 512
            c0 = mt0 + yoff
            yt_tok = pb.dma("sync",
                            YT[:, 0:4160].rearrange("p (c m) -> p c m", m=520)
                            [:, :, 0:513],
                            rearr_kpm(ysrc)[:, :, c0 - 1:c0 + 512],
                            deps=[yt_tok, env["ysrc_ready"]], channel="rawf")
            sz_tok = None
            if mt0 <= 0 < mt0 + 512:
                p0 = -mt0

                def szwrap(e, sem, p0=p0):
                    for kc in range(4, 8):
                        e.tensor_scalar_mul(YT[:, kc * 520 + p0:kc * 520 + p0 + 1],
                                            YT[:, kc * 520 + p0:kc * 520 + p0 + 1],
                                            SZC).then_inc(sem, 1)
                sz_tok = pb.comp_n("vector", szwrap, 4, deps=[yt_tok])
            for mc in range(4):
                def lhs(kc, mc=mc):
                    off = 1 if kc < 4 else 0
                    return YT[:, kc * 520 + off + mc * 128:kc * 520 + off + mc * 128 + 128]
                tq = mmg(ctx, 0, [(ctx.bank(0), lhs(kc), WQS[:, kc * 512:(kc + 1) * 512],
                                   None) for kc in range(NCH)],
                         deps=[t_wq, yt_tok, sz_tok])
                tk = mmg(ctx, 1, [(ctx.bank(1), lhs(kc), WKS[:, kc * 512:(kc + 1) * 512],
                                   None) for kc in range(NCH)],
                         deps=[t_wk, yt_tok, sz_tok])
                tv = mmg(ctx, 2, [(ctx.bank(2), lhs(kc), WVS[:, kc * 512:(kc + 1) * 512],
                                   None) for kc in range(NCH)],
                         deps=[t_wv, yt_tok, sz_tok])
                trow = mt0 + mc * 128 + 2048
                t_rot = pb.dma("sync", ROT.rearrange("p (k c) -> p k c", k=4),
                               io["rot"][:, trow:trow + 128, :].transpose([1, 0, 2]),
                               deps=[env.get("rot_free"), last_split], channel="rot")
                env["rot_free"] = None
                cv = (mt0 + mc * 128 - t0) // 128
                ev = pb.comp("vector", lambda e, cv=cv: e.tensor_copy(
                    VTOK[:, cv * 512:(cv + 1) * 512], ctx.bank(2)), deps=[tv])
                ctx.bank_tok[2] = ev
                prev_part = None
                for part, bank, tok, csoff in (("q", 0, tq, 0), ("k", 1, tk, 128)):
                    t_sq = pb.comp("scalar", lambda e, bank=bank: e.activation(
                        QTMP, ctx.bank(bank), AF.Square), deps=[tok, prev_part])
                    t1s = pb.comp("vector", lambda e: e.tensor_reduce(
                        S2R[:, 0:8], QTMP.rearrange("p (h x) -> p h x", h=8),
                        axis=AX.X, op=ALU.add), deps=[t_sq])
                    t2s = pb.comp("scalar", lambda e: e.activation(
                        S2R[:, 8:16], S2R[:, 0:8], AF.Sqrt), deps=[t1s])
                    t3s = pb.comp("vector", lambda e: e.tensor_scalar_max(
                        S2R[:, 8:16], S2R[:, 8:16], 1e-12), deps=[t2s])
                    t4s = pb.comp("vector", lambda e: e.reciprocal(
                        S2R[:, 0:8], S2R[:, 8:16]), deps=[t3s])
                    t5s = pb.comp("vector", lambda e, bank=bank: e.tensor_tensor(
                        GT2.rearrange("p (h x) -> p h x", h=8),
                        ctx.bank(bank).rearrange("p (h x) -> p h x", h=8),
                        S2R[:, 0:8].unsqueeze(2).broadcast_to([128, 8, 64]),
                        op=ALU.mult), deps=[t4s, tok])
                    ctx.bank_tok[bank] = t5s
                    qof = 0 if part == "q" else 512
                    t6s = pb.comp("vector", lambda e, qof=qof: e.tensor_tensor(
                        GT2, GT2, BQS[:, qof:qof + 512], op=ALU.mult),
                        deps=[t5s, cb1, cb2])
                    cs0 = 0 if csoff == 0 else 128
                    t7 = pb.comp("vector", lambda e, cs0=cs0: e.tensor_tensor(
                        QTMP.rearrange("p (h x) -> p h x", h=8),
                        GT2.rearrange("p (h x) -> p h x", h=8),
                        ROT[:, cs0:cs0 + 64].unsqueeze(1).broadcast_to([128, 8, 64]),
                        op=ALU.mult), deps=[t6s, t_rot])
                    t8 = pb.comp("vector", lambda e, cs0=cs0: e.tensor_tensor(
                        ETMP.rearrange("p (h x) -> p h x", h=8)[:, :, 0:32],
                        GT2.rearrange("p (h x) -> p h x", h=8)[:, :, 32:64],
                        ROT[:, cs0 + 64:cs0 + 96].unsqueeze(1)
                        .broadcast_to([128, 8, 32]), op=ALU.mult),
                        deps=[t6s, t_rot])
                    t9 = pb.comp("vector", lambda e, cs0=cs0: e.tensor_tensor(
                        ETMP.rearrange("p (h x) -> p h x", h=8)[:, :, 32:64],
                        GT2.rearrange("p (h x) -> p h x", h=8)[:, :, 0:32],
                        ROT[:, cs0 + 96:cs0 + 128].unsqueeze(1)
                        .broadcast_to([128, 8, 32]), op=ALU.mult),
                        deps=[t6s, t_rot])
                    t10 = pb.comp("vector", lambda e: e.tensor_add(
                        KPB, QTMP, ETMP),
                        deps=[t7, t8, t9, prev_tt, env.get("ot_dma"),
                              env.get("xao_last")])
                    prev_part = t10
                    dst = QTH if part == "q" else KTH
                    for hc in range(4):
                        tt = pb.dma("scalar", TMP128,
                                    KPB[:, hc * 128:(hc + 1) * 128],
                                    deps=[t10, last_split], transpose=True,
                                    channel="tp")
                        colA = (2 * hc) * GSPAN + (mt0 + mc * 128 - t0)
                        colB = (2 * hc + 1) * GSPAN + (mt0 + mc * 128 - t0)
                        sp1 = pb.dma("sync", dst[0:64, colA:colA + 128],
                                     TMP128[0:64, :], deps=[tt], channel="sp")
                        sp2 = pb.dma("sync", dst[0:64, colB:colB + 128],
                                     TMP128[64:128, :], deps=[tt], channel="sp")
                        last_split = sp2
                        prev_tt = tt
        qkv_done = last_split
        if env.get("qkv_only"):
            return []

        has_w0 = any(w == 0 for w in wlist)
        mwar = [tq, tk, tv]
        tmsk_n2 = pb.dma("sync", WQS.rearrange("p (st m) -> p st m", m=512),
                         io["masks"][1].rearrange("(st p) m -> p st m", p=128),
                         deps=mwar, channel="mskA")
        tmsk_n1 = pb.dma("sync", WVS.rearrange("p (ms k) -> p ms k", k=1024),
                         io["masksT"][1].rearrange("(ms p) k -> p ms k", p=128),
                         deps=mwar, channel="mskB")
        tmsk_02 = tmsk_01 = None
        if has_w0:
            tmsk_02 = pb.dma("sync", WKS.rearrange("p (st m) -> p st m", m=512),
                             io["masks"][0].rearrange("(st p) m -> p st m", p=128),
                             deps=mwar, channel="mskC")
            tmsk_01 = pb.dma("sync",
                             YT[:, 0:4096].rearrange("p (ms k) -> p ms k", k=1024),
                             io["masksT"][0].rearrange("(ms p) k -> p ms k", p=128),
                             deps=mwar, channel="mskD")

        pb.phase = f"L{l}.attn"
        tadds_all = []
        for w in wlist:
            widx = 0 if w == 0 else 1
            mS = WQS if widx else WKS
            mS_tok = tmsk_n2 if widx else tmsk_02
            mT = WVS if widx else YT
            mT_tok = tmsk_n1 if widx else tmsk_01
            for h in range(8):
                qcol = lambda m: h * GSPAN + (w * 512 - t0) + m
                kcol = lambda sx: h * GSPAN + ((w - 1) * 512 - t0) + sx
                pb.phase = f"L{l}.attn.p1"
                nm_toks = []
                for msub in range(4):
                    tS1 = mmg(ctx, 4, [(ctx.bank(4),
                                        QTH[0:64, qcol(msub * 128):qcol(msub * 128) + 128],
                                        KTH[0:64, kcol(0):kcol(0) + 512], None)],
                              deps=[qkv_done])
                    tS2 = mmg(ctx, 5, [(ctx.bank(5),
                                        QTH[0:64, qcol(msub * 128):qcol(msub * 128) + 128],
                                        KTH[0:64, kcol(512):kcol(512) + 512], None)],
                              deps=[qkv_done])
                    tta = pb.comp("vector", lambda e, msub=msub, mT=mT:
                                  e.tensor_tensor(
                                      ctx.PS[:, 2048:3072], ctx.PS[:, 2048:3072],
                                      mT[:, msub * 1024:(msub + 1) * 1024],
                                      op=ALU.add),
                                  deps=[tS1, tS2, mT_tok])
                    tnm = pb.comp("vector", lambda e, msub=msub: e.tensor_reduce(
                        NMC[:, msub:msub + 1], ctx.PS[:, 2048:3072],
                        axis=AX.X, op=ALU.max, negate=True), deps=[tta])
                    ctx.bank_tok[4] = tnm
                    ctx.bank_tok[5] = tnm
                    nm_toks.append(tnm)
                tcb = pb.comp("vector", lambda e: e.tensor_copy(NMCB[:, 0:4],
                                                                NMC[:, 0:4]),
                              deps=nm_toks)
                pb.phase = f"L{l}.attn.tr"
                ttr = []
                for msub in range(4):
                    tt = pb.comp("tensor", lambda e, msub=msub: e.transpose(
                        ctx.bank_bf(6)[0:1, msub * 128:(msub + 1) * 128],
                        NMCB[:, msub:msub + 1], ident),
                        deps=[tcb, ctx.bank_tok[6]])
                    ttr.append(tt)
                tmxr = pb.comp("scalar", lambda e: e.copy(MXR[0:1, 0:512],
                                                          ctx.bank_bf(6)[0:1, 0:512]),
                               deps=ttr)
                ctx.bank_tok[6] = tmxr
                pb.phase = f"L{l}.attn.p2"
                lsum_done = None
                o_toks = []
                e_toks = []
                for st in range(8):
                    bkS = st % 2
                    tS = mmg(ctx, bkS, [
                        (ctx.bank(bkS),
                         KTH[0:64, kcol(st * 128):kcol(st * 128) + 128],
                         QTH[0:64, qcol(0):qcol(0) + 512], None),
                        (ctx.bank(bkS), ones_row, MXR[0:1, 0:512], None)],
                        deps=[tmxr, qkv_done], independent=True)
                    tadd = pb.comp("vector", lambda e, bkS=bkS, st=st, mS=mS:
                                   e.tensor_tensor(
                                       ETMP, ctx.bank(bkS),
                                       mS[:, st * 512:(st + 1) * 512],
                                       op=ALU.add),
                                   deps=[tS, mS_tok,
                                         e_toks[-1] if e_toks else None])
                    ctx.bank_tok[bkS] = tadd
                    tex = pb.comp("scalar", lambda e, st=st: e.activation(
                        EB[:, st * 512:(st + 1) * 512], ETMP, AF.Exp), deps=[tadd])
                    e_toks.append(tex)
                    tadds_all.append(tadd)
                pb.phase = f"L{l}.attn.ls"
                tls = mmg(ctx, 2, [(ctx.bank(2)[0:1, 0:512], ones_col,
                                    EB[:, st * 512:(st + 1) * 512], None)
                                   for st in range(8)],
                          deps=e_toks, cont=False)
                tfl = pb.comp("vector", lambda e: e.tensor_scalar_max(
                    RNR[0:1, 0:512], ctx.bank(2)[0:1, 0:512], 1e-35), deps=[tls])
                ctx.bank_tok[2] = tfl
                trc = pb.comp("vector", lambda e: e.reciprocal(
                    RNR[0:1, 0:512], RNR[0:1, 0:512]), deps=[tfl])
                trb = pb.comp("vector", lambda e: e.tensor_copy(
                    RNB[0:1, 0:512], RNR[0:1, 0:512]), deps=[trc])
                tbr = mmg(ctx, 3, [(ctx.bank(3), ones_row, RNB[0:1, 0:512], None)],
                          deps=[trb])
                pb.phase = f"L{l}.attn.o"
                oterms = []
                for st in range(8):
                    cv = ((w - 1) * 512 + st * 128 - t0) // 128
                    oterms.append((ctx.bank(6)[0:64, 0:512],
                                   VTOK[:, cv * 512 + h * 64:cv * 512 + h * 64 + 64],
                                   EB[:, st * 512:(st + 1) * 512], None))
                t_o = mmg(ctx, 6, oterms, deps=e_toks + [ctx.bank_tok[6]])
                toc = pb.comp("scalar", lambda e: e.copy(OTMP, ctx.bank(6)[0:64, 0:512]),
                              deps=[t_o])
                ctx.bank_tok[6] = toc
                hc = h // 2
                if h % 2 == 0:
                    tno = pb.comp("vector", lambda e, hc=hc: e.tensor_tensor(
                        OT[0:64, hc * 512:(hc + 1) * 512], OTMP,
                        ctx.bank(3)[0:64, 0:512], op=ALU.mult),
                        deps=[toc, tbr, env.get("ot_free")])
                    ctx.bank_tok[3] = tno
                    o_done = tno
                else:
                    tno = pb.comp("vector", lambda e: e.tensor_tensor(
                        KPB[0:64, 0:512], OTMP, ctx.bank(3)[0:64, 0:512],
                        op=ALU.mult),
                        deps=[toc, tbr, env.get("ot_dma"), env.get("xao_last")])
                    ctx.bank_tok[3] = tno
                    o_done = pb.dma("sync", OT[64:128, hc * 512:(hc + 1) * 512],
                                    KPB[0:64, 0:512], deps=[tno, env.get("ot_free")],
                                    channel="ot")
                    env["ot_dma"] = o_done
                env["o_done_" + str(h)] = o_done
            pb.phase = f"L{l}.wo"
            o_all = [env["o_done_" + str(h)] for h in range(8)]
            ycol = w * 512 + yoff
            wo_oc_last = env.get("xao_last")
            res_consumer = env.get("res_last")
            for oc in range(NCH):
                if ysrc_f is None:
                    yw_tok = pb.dma("gpsimd", RESF,
                                    rearr_kpm(ysrc)[:, oc, ycol:ycol + 512],
                                    deps=[res_consumer, o_all[-1]],
                                    channel="ycast")
                else:
                    yw_tok = pb.dma("sync", RESF,
                                    rearr_kpm(ysrc_f)[:, oc, ycol:ycol + 512],
                                    deps=[res_consumer, o_all[-1]],
                                    channel="resf")
                sl = (sb["SLA"], sb["SLB"])[oc % 2]
                ch = ("sla", "slb")[oc % 2]
                tld = pb.dma("gpsimd", sl[:, 0:512],
                             io["wo_f"][l].rearrange("(ic kp) o -> kp ic o", kp=128)
                             [:, :, oc * 128:(oc + 1) * 128],
                             deps=[ctx.slab_last[ch]], channel=ch)
                bk = 7
                terms = [(ctx.bank(bk), sl[:, ic * 128:(ic + 1) * 128],
                          OT[:, ic * 512:(ic + 1) * 512], None) for ic in range(4)]
                tm = mmg(ctx, bk, terms, deps=[tld] + o_all)
                ctx.slab_last[ch] = tm
                tadd = pb.comp("vector", lambda e, bk=bk, oc=oc: e.tensor_tensor(
                    GT2, ctx.bank(bk), RESF, op=ALU.add),
                    deps=[tm, yw_tok, wo_oc_last])
                res_consumer = tadd
                ctx.bank_tok[bk] = tadd
                tcb16 = pb.comp("vector", lambda e: e.tensor_copy(KPB, GT2),
                                deps=[tadd])
                col = w * 512 + env["wo_off"]
                d1 = pb.dma("sync", rearr_kpm(env["wo_f32"])[:, oc, col:col + 512],
                            GT2, deps=[tadd], channel="xao")
                d2 = pb.dma("sync", rearr_kpm(env["wo_b16"])[:, oc, col:col + 512],
                            KPB, deps=[tcb16], channel="xao")
                wo_oc_last = d2
            env["ot_free"] = tm
            env["xao_last"] = wo_oc_last
            env["res_last"] = res_consumer
            wo_last = wo_oc_last
        env["msk_free"] = (pb.psem["vector"], pb.cnt["vector"])

    pb.phase = f"L{l}.ffn"
    pb.barrier()
    final = env["final"]
    out_dmas = []

    def do_ffn_tile(a, b, ffn_prev):
        wdt = b - a
        ca = a + env["wo_off"]
        rf_tok = pb.dma("sync",
                        RAW2F.rearrange("p (c m) -> p c m", m=513)[:, :, 0:wdt + 1],
                        rearr_kpm(env["wo_f32"])[:, :, ca - 1:ca + wdt],
                        deps=[env.get("rawf_free"), ffn_prev], channel="raw2f")
        env["rawf_free"] = None
        sz_tok = None
        if a <= 0 < b:
            p0 = -a

            def szw2(e, sem, p0=p0):
                for kc in range(4, 8):
                    e.tensor_scalar_mul(RAW2F[:, kc * 513 + p0:kc * 513 + p0 + 1],
                                        RAW2F[:, kc * 513 + p0:kc * 513 + p0 + 1],
                                        SZC).then_inc(sem, 1)
            sz_tok = pb.comp_n("vector", szw2, 4, deps=[rf_tok])
        s1_tok = None
        if final:
            s1_tok = pb.dma("sync", S1T.rearrange("p (c m) -> p c m", c=8)
                            [:, :, 0:wdt],
                            rearr_kpm(io["sc_s1"])[:, :, a:a + wdt],
                            deps=[env.get("s1_free"), ffn_prev], channel="s1i",
                            allow_slow_non_contiguous=(wdt < 8))

        def rawsh(c):
            off = 1 if c < 4 else 0
            return RAW2F[:, c * 513 + off:c * 513 + off + wdt]
        t_xn, lnfree = emit_ln(ctx, rawsh, (sz_tok or rf_tok),
                               lambda c: YT[:, c * 512:c * 512 + wdt], mt=wdt)
        aff = t_xn
        for c in range(NCH):
            a1t = pb.comp("vector", lambda e, c=c: e.tensor_scalar_mul(
                YT[:, c * 512:c * 512 + wdt], YT[:, c * 512:c * 512 + wdt],
                LNG[:, l * 8 + c:l * 8 + c + 1]), deps=[aff])
            aff = pb.comp("vector", lambda e, c=c: e.tensor_scalar_add(
                YT[:, c * 512:c * 512 + wdt], YT[:, c * 512:c * 512 + wdt],
                LNB[:, l * 8 + c:l * 8 + c + 1]), deps=[a1t])
        h2_tok = aff
        rh_last = None
        b1_tok = None
        for fc in range(43):
            sl = (sb["SLA"], sb["SLB"])[fc % 2]
            ch = ("sla", "slb")[fc % 2]
            rbuf = (sb["B1R"], sb["B1RB"])[fc % 2]
            trb = pb.dma("gpsimd", rbuf,
                         io["b1p"][l][fc * 128:(fc + 1) * 128].unsqueeze(0),
                         deps=[env.get("rb_war_" + ch)], channel="rb1")
            tld = pb.dma("gpsimd", sl,
                         rearr_kpm(io["w1p"][l])[:, :, fc * 128:(fc + 1) * 128],
                         deps=[ctx.slab_last[ch]], channel=ch)
            bk = fc % 2
            terms = [(ctx.bank(bk)[:, 0:wdt], sl[:, kc * 128:(kc + 1) * 128],
                      YT[:, kc * 512:kc * 512 + wdt], None) for kc in range(NCH)]
            terms.append((ctx.bank(bk)[:, 0:wdt], rbuf, ones512[0:1, 0:wdt], None))
            tm1 = mmg(ctx, bk, terms, deps=[tld, trb, h2_tok])
            b1_tok = tm1
            ctx.slab_last[ch] = tm1
            env["rb_war_" + ch] = tm1
            tgl = pb.comp("scalar", lambda e, bk=bk, wdt=wdt: e.activation(
                ETMP[0:64, 0:wdt], ctx.bank(bk)[64:128, 0:wdt], AF.Gelu),
                deps=[tm1, rh_last])
            p = fc // 2
            r0 = (fc % 2) * 64
            tpr = pb.comp("vector", lambda e, bk=bk, p=p, r0=r0, wdt=wdt:
                          e.tensor_tensor(
                              HP[r0:r0 + 64, p * 512:p * 512 + wdt],
                              ctx.bank(bk)[0:64, 0:wdt],
                              ETMP[0:64, 0:wdt], op=ALU.mult), deps=[tgl])
            ctx.bank_tok[bk] = tpr
            rh_last = tpr
        for oc in range(NCH):
            wbuf = (WQS, WKS)[oc % 2]
            wch = ("u1", "u2")[oc % 2]
            war = env.get("w2_war_" + wch)
            w2a = pb.dma("gpsimd", wbuf[:, 0:21 * 128],
                         io["w2p"][l][0:2688].rearrange("(pk kp) o -> kp pk o", kp=128)
                         [:, :, oc * 128:(oc + 1) * 128],
                         deps=[war, env.get("w2_free")], channel=wch)
            w2b = pb.dma("gpsimd", wbuf[0:64, 21 * 128:22 * 128],
                         io["w2p"][l][2688:2752, oc * 128:(oc + 1) * 128],
                         deps=[war, env.get("w2_free")], channel=wch)
            env["w2_free"] = None
            bk = 2 + oc % 2
            terms = [(ctx.bank(bk)[:, 0:wdt], wbuf[:, pk * 128:(pk + 1) * 128],
                      HP[:, pk * 512:pk * 512 + wdt], None) for pk in range(21)]
            terms.append((ctx.bank(bk)[:, 0:wdt], wbuf[0:64, 21 * 128:22 * 128],
                          HP[0:64, 21 * 512:21 * 512 + wdt], None))
            terms.append((ctx.bank(bk)[:, 0:wdt],
                          B2R[:, oc * 128:(oc + 1) * 128], ones512[0:1, 0:wdt],
                          None))
            tm2 = mmg(ctx, bk, terms, deps=[w2b, rh_last, t_b2])
            env["w2_war_" + wch] = tm2
            tr1 = pb.comp("vector", lambda e, bk=bk, oc=oc, wdt=wdt: e.tensor_tensor(
                GT2[:, 0:wdt], ctx.bank(bk)[:, 0:wdt],
                RAW2F[:, oc * 513 + 1:oc * 513 + 1 + wdt], op=ALU.add),
                deps=[tm2, out_dmas[-1] if out_dmas else None])
            if final:
                tr2 = pb.comp("vector", lambda e, oc=oc, wdt=wdt: e.tensor_tensor(
                    GT2[:, 0:wdt], GT2[:, 0:wdt],
                    S1T[:, oc * 512:oc * 512 + wdt], op=ALU.add),
                    deps=[tr1, s1_tok])
                ctx.bank_tok[bk] = tr2
                dd = pb.dma("sync", rearr_kpm(io["out"])[:, oc, a:a + wdt],
                            GT2[:, 0:wdt], deps=[tr2], channel="outo",
                            allow_slow_non_contiguous=(wdt < 8))
                out_dmas.append(dd)
            else:
                ctx.bank_tok[bk] = tr1
                cb16 = pb.comp("vector", lambda e, wdt=wdt: e.tensor_copy(
                    KPB[:, 0:wdt], GT2[:, 0:wdt]), deps=[tr1])
                fcol = a + env["fo_off"]
                dd = pb.dma("sync",
                            rearr_kpm(env["ffn_f32"])[:, oc, fcol:fcol + wdt],
                            GT2[:, 0:wdt], deps=[tr1], channel="outo",
                            allow_slow_non_contiguous=(wdt < 8))
                dd = pb.dma("sync",
                            rearr_kpm(env["ffn_b16"])[:, oc, fcol:fcol + wdt],
                            KPB[:, 0:wdt], deps=[cb16], channel="outo",
                            allow_slow_non_contiguous=(wdt < 8))
                out_dmas.append(dd)
        env["s1_free"] = None if not final else tm2
        return out_dmas[-1]

    ffn_prev = None
    for (a, b) in env["ffn_tiles"]:
        ffn_prev = do_ffn_tile(a, b, ffn_prev)
    return out_dmas
